# version 21
# baseline (speedup 1.0000x reference)
import numpy as np
import ml_dtypes
from collections import defaultdict
from contextlib import ExitStack

import concourse.bass as bass
import concourse.mybir as mybir
from concourse.bass_utils import run_bass_kernel_spmd

F32 = mybir.dt.float32
BF16 = mybir.dt.bfloat16
I32 = mybir.dt.int32
AF = mybir.ActivationFunctionType
ALU = mybir.AluOpType
AX = mybir.AxisListType.X

NCORES = 8
T = 1024
D = 2048
H = 16
DH = 128
WIN = 2
CSC = float(DH) ** -0.5
NEG = -3.0e38
NTOK = 8192

DEBUG = False


class Prog:
    def __init__(self):
        self.steps = []
        self.cnt = defaultdict(int)
        self.tags = {}

    def step(self, eng, fn, waits=(), tag=None, inc=None, pad=False, dup=False):
        for t in waits:
            assert t in self.tags, f"wait on undefined tag {t}"
        if inc is not None:
            self.cnt[inc[0]] += inc[1]
            if tag is not None:
                self.tags[tag] = (inc[0], self.cnt[inc[0]])
        else:
            assert tag is None
        self.steps.append((eng, fn, tuple(self.tags[t] for t in waits), inc, pad, dup))

    def emit(self, eng_name, eng, sems, padfn=None):
        for (e, fn, waits, inc, pad, dup) in self.steps:
            if e != eng_name:
                continue
            byname = {}
            for (sname, val) in waits:
                byname[sname] = max(byname.get(sname, 0), val)
            for sname, val in byname.items():
                eng.wait_ge(sems[sname], val)
            if fn is None:
                continue
            inst = fn(eng)
            if dup:
                inst = fn(eng)
            if pad:
                assert padfn is not None
                inst = padfn(eng)
            if inc is not None:
                assert inst is not None
                inst.then_inc(sems[inc[0]], inc[1])


def ecopy(en, out, in_):
    if hasattr(en, "tensor_copy"):
        return en.tensor_copy(out=out, in_=in_)
    return en.copy(out=out, in_=in_)


def build_nc(debug=False, debug2=False):
    nc = bass.Bass("TRN2", target_bir_lowering=False)

    x_t = nc.dram_tensor("x_t", [D, T], F32, kind="ExternalInput").ap()
    x_n = nc.dram_tensor("x_n", [T, D], F32, kind="ExternalInput").ap()
    wqkvT = nc.dram_tensor("wqkvT", [D, 3 * D], F32, kind="ExternalInput").ap()
    woutT = nc.dram_tensor("woutT", [D, D], F32, kind="ExternalInput").ap()
    wffnT = nc.dram_tensor("wffnT", [D, D], F32, kind="ExternalInput").ap()
    nw_f = nc.dram_tensor("nw_f", [128, 16], F32, kind="ExternalInput").ap()
    gf_f = nc.dram_tensor("gf_f", [128, 16], F32, kind="ExternalInput").ap()
    pmk_h = nc.dram_tensor("pmk_h", [128, 256], BF16, kind="ExternalInput").ap()
    pmv_h = nc.dram_tensor("pmv_h", [16, 2048], BF16, kind="ExternalInput").ap()
    cos_h = nc.dram_tensor("cos_h", [128, T], BF16, kind="ExternalInput").ap()
    sin_h = nc.dram_tensor("sin_h", [128, T], BF16, kind="ExternalInput").ap()
    ptm_h = nc.dram_tensor("ptm_h", [128, 128], BF16, kind="ExternalInput").ap()
    idb_h = nc.dram_tensor("idb_h", [128, 128], BF16, kind="ExternalInput").ap()
    idf_h = nc.dram_tensor("idf_h", [128, 128], F32, kind="ExternalInput").ap()
    onb_h = nc.dram_tensor("onb_h", [128, 1], BF16, kind="ExternalInput").ap()
    onf_h = nc.dram_tensor("onf_h", [1, 128], F32, kind="ExternalInput").ap()
    onc_h = nc.dram_tensor("onc_h", [128, 1], F32, kind="ExternalInput").ap()
    mfirst_h = nc.dram_tensor("mfirst_h", [128, 128], BF16, kind="ExternalInput").ap()
    mdiag_h = nc.dram_tensor("mdiag_h", [128, 128], BF16, kind="ExternalInput").ap()
    out_d = nc.dram_tensor("out", [T, D], F32, kind="ExternalOutput").ap()
    cc_in = nc.dram_tensor("cc_in", [128, 128], F32, kind="Internal").ap()
    cc_out = nc.dram_tensor("cc_out", [128, 128], F32, kind="Internal").ap()

    dbg = {}
    if debug2:
        dbg["ctx"] = nc.dram_tensor("dbg_ctx", [128, H * T], BF16, kind="ExternalOutput").ap()
        dbg["xqb"] = nc.dram_tensor("dbg_xqb", [128, 16 * T], BF16, kind="ExternalOutput").ap()
        dbg["sc"] = nc.dram_tensor("dbg_sc", [128, 16], F32, kind="ExternalOutput").ap()
        dbg["pn"] = nc.dram_tensor("dbg_pn", [2, 128, 544], BF16, kind="ExternalOutput").ap()
        dbg["z"] = nc.dram_tensor("dbg_z", [128, 128], F32, kind="ExternalOutput").ap()
        dbg["v"] = nc.dram_tensor("dbg_v", [128, 8 * D], BF16, kind="ExternalOutput").ap()
        dbg["st"] = nc.dram_tensor("dbg_st", [128, 2048], F32, kind="ExternalOutput").ap()
        dbg["bc"] = nc.dram_tensor("dbg_bc", [128, 2048], F32, kind="ExternalOutput").ap()
        dbg["st0"] = nc.dram_tensor("dbg_st0", [1, 2048], F32, kind="ExternalOutput").ap()
        dbg["gfa"] = nc.dram_tensor("dbg_gfa", [128, 16], F32, kind="ExternalOutput").ap()
        dbg["mb"] = nc.dram_tensor("dbg_mb", [128, 1024], F32, kind="ExternalOutput").ap()
        dbg["rb"] = nc.dram_tensor("dbg_rb", [128, 1024], F32, kind="ExternalOutput").ap()
    if debug:
        dbg["xab"] = nc.dram_tensor("dbg_xab", [128, 16 * T], BF16, kind="ExternalOutput").ap()
        dbg["qh"] = nc.dram_tensor("dbg_qh", [H, 128, T], BF16, kind="ExternalOutput").ap()
        dbg["kh"] = nc.dram_tensor("dbg_kh", [H, 128, T], BF16, kind="ExternalOutput").ap()
        dbg["v"] = nc.dram_tensor("dbg_v", [128, 8 * D], BF16, kind="ExternalOutput").ap()
        dbg["st"] = nc.dram_tensor("dbg_st", [128, 2048], F32, kind="ExternalOutput").ap()
        dbg["bc"] = nc.dram_tensor("dbg_bc", [128, 2048], F32, kind="ExternalOutput").ap()
        dbg["st0"] = nc.dram_tensor("dbg_st0", [1, 2048], F32, kind="ExternalOutput").ap()
        dbg["gfa"] = nc.dram_tensor("dbg_gfa", [128, 16], F32, kind="ExternalOutput").ap()
        dbg["mb"] = nc.dram_tensor("dbg_mb", [128, 1024], F32, kind="ExternalOutput").ap()
        dbg["rb"] = nc.dram_tensor("dbg_rb", [128, 1024], F32, kind="ExternalOutput").ap()
        dbg["p"] = nc.dram_tensor("dbg_p", [H, WIN, 4, 128, 544], BF16, kind="ExternalOutput").ap()
        dbg["ctx"] = nc.dram_tensor("dbg_ctx", [128, H * T], BF16, kind="ExternalOutput").ap()
        dbg["xqb"] = nc.dram_tensor("dbg_xqb", [128, 16 * T], BF16, kind="ExternalOutput").ap()
        dbg["sc"] = nc.dram_tensor("dbg_sc", [128, 16], F32, kind="ExternalOutput").ap()

    with ExitStack() as ctx:
        e = ctx.enter_context

        xab_t = e(nc.sbuf_tensor("xab", [128, 16 * T], BF16))
        vsb_t = e(nc.sbuf_tensor("vsb", [128, 8 * D], BF16))
        ctx_t = e(nc.sbuf_tensor("ctxb", [128, H * T], BF16))
        wst_t = e(nc.sbuf_tensor("wstage", [128, 8192], F32))
        wbf0_t = e(nc.sbuf_tensor("wbf0", [128, 4096], BF16))
        wbf1_t = e(nc.sbuf_tensor("wbf1", [128, 4096], BF16))
        mega_t = e(nc.sbuf_tensor("mega", [128, 10240], BF16))
        stat_t = e(nc.sbuf_tensor("statsl", [128, 2048], F32))
        smal_t = e(nc.sbuf_tensor("smal", [128, 128], F32))
        outs_t = e(nc.sbuf_tensor("outsb", [128, 1024], F32))
        xnst_t = e(nc.sbuf_tensor("xnst", [128, 512], F32))
        o1b_t = e(nc.sbuf_tensor("o1b", [128, 512], F32))
        mfir_t = e(nc.sbuf_tensor("mfirst", [128, 128], BF16))
        mdia_t = e(nc.sbuf_tensor("mdiag", [128, 128], BF16))
        nw_t = e(nc.sbuf_tensor("nwsb", [128, 16], F32))
        gf_t = e(nc.sbuf_tensor("gfsb", [128, 16], F32))
        gfa_t = e(nc.sbuf_tensor("gfab", [128, 16], F32))
        pmk_t = e(nc.sbuf_tensor("pmk", [128, 256], BF16))
        pmv_t = e(nc.sbuf_tensor("pmv", [16, 2048], BF16))
        ptm_t = e(nc.sbuf_tensor("ptm", [128, 128], BF16))
        idb_t = e(nc.sbuf_tensor("idb", [128, 128], BF16))
        idf_t = e(nc.sbuf_tensor("idf", [128, 128], F32))
        onb_t = e(nc.sbuf_tensor("onb", [128, 1], BF16))
        onf_t = e(nc.sbuf_tensor("onf", [1, 128], F32))
        onc_t = e(nc.sbuf_tensor("onc", [128, 1], F32))
        ccst_t = e(nc.sbuf_tensor("ccst", [128, 128], F32))
        pmpt_t = [e(nc.sbuf_tensor(f"pmpt{w}", [16, 512], BF16)) for w in range(2)]
        ptj_t = [[e(nc.sbuf_tensor(f"ptj{w}_{j}", [128, 512 - 128 * j], BF16)) for j in range(4)]
                 for w in range(2)]
        qtmp_t = e(nc.sbuf_tensor("qtmp", [128, 1024], BF16))
        qtmk_t = e(nc.sbuf_tensor("qtmk", [128, 1024], BF16))
        rots_t = e(nc.sbuf_tensor("rotsb", [128, 1024], BF16))
        rotk_t = e(nc.sbuf_tensor("rotkb", [128, 1024], BF16))
        t1b_t = e(nc.sbuf_tensor("t1b", [128, 1024], BF16))

        xab = xab_t[:, :]
        vsb = vsb_t[:, :]
        ctxb = ctx_t[:, :]
        wst = wst_t[:, :]
        mega = mega_t[:, :]
        outsb = outs_t[:, :]
        xnst = xnst_t[:, :]
        o1b = o1b_t[:, :]
        qtmp_b = [qtmp_t[:, :], qtmk_t[:, :]]
        rots_b = [rots_t[:, :], rotk_t[:, :]]
        t1b = t1b_t[:, :]
        wbf = [wbf0_t[:, :], wbf1_t[:, :]]
        ptj = [[p[:, :] for p in row] for row in ptj_t]
        pmpt = [p[:, :] for p in pmpt_t]

        qh = [mega[:, 0:1024], mega[:, 1024:2048]]
        kh = [mega[:, 2048:3072], mega[:, 3072:4096]]
        cosb = mega[:, 4096:5120]
        sinb = mega[:, 5120:6144]
        p_sb = [mega[:, 6144:6688], mega[:, 6688:7232]]
        p_n = [mega[:, 7232:7776], mega[:, 7776:8320]]
        wfb = [mega[:, 0:4096], mega[:, 4096:8192]]
        wob = [wbf[0], wbf[1]]

        xsc = [ctxb[:, 2048 * s:2048 * (s + 1)].bitcast(F32) for s in range(4)]
        tsc = ctxb[:, 8192:10240].bitcast(F32)
        xhsc = ctxb[:, 10240:12288].bitcast(F32)
        xbsc = ctxb[:, 12288:13312]
        sqsc = ctxb[:, 13312:14336]
        mubc = wbf[0][:, 0:2048].bitcast(F32)
        rhobc = wbf[1][:, 0:2048].bitcast(F32)
        xsc3 = [xab[:, 2048 * s:2048 * (s + 1)].bitcast(F32) for s in range(2)]
        tsc3 = xab[:, 4096:6144].bitcast(F32)
        xhsc3 = xab[:, 6144:8192].bitcast(F32)
        mubc2 = xab[:, 8192:10240].bitcast(F32)
        rhobc2 = xab[:, 10240:12288].bitcast(F32)
        xgsc = xab[:, 12288:14336].bitcast(F32)
        xisc = xab[:, 14336:16384].bitcast(I32)

        def sl(row, a, b):
            return stat_t[row:row + 1, a:b]
        mu_r = lambda tt: sl(0, 512 * tt, 512 * (tt + 1))
        rho_r = lambda tt: sl(0, 1024 + 512 * tt, 1024 + 512 * (tt + 1))
        aux1 = lambda tt: outs_t[0:1, 512 * tt:512 * (tt + 1)]
        aux2 = lambda tt: (xnst_t if tt == 0 else o1b_t)[0:1, 0:512]

        S_ = lambda i: smal_t[:, i:i + 1]
        gmax = S_(0)
        am_c, am2_c = S_(1), S_(2)
        m_c = [S_(3), S_(4)]
        m2_c = [S_(5), S_(6)]
        negm_c = [S_(7), S_(8)]
        za = [S_(9), S_(10)]
        zb = [S_(11), S_(12)]
        zs = [S_(13), S_(38)]
        rz = [S_(14), S_(39)]
        swacc = [S_(15 + i) for i in range(8)]
        swacc_b = S_(23)
        swsum, sw_s, rw_bc, sc_bc, s127, gamc, tmpa, tmpb, junk = (
            S_(24), S_(25), S_(26), S_(27), S_(28), S_(29), S_(30), S_(31), S_(32))
        eps5_c, eps6_c = S_(33), S_(34)
        junkv, junks, junkg = S_(35), S_(36), S_(37)
        amall = smal_t[:, 40:56]

        pb = [e(nc.psum_tensor(f"pb{i}", [128, 512], F32)) for i in range(6)]
        ptp = [e(nc.psum_tensor(f"ptp{i}", [128, 128], BF16)) for i in range(2)]
        sa = [pb[2], pb[3]]
        ctxps = [pb[4], pb[4]]
        rotps = pb[5]

        sems = {}
        for sname in ["Y", "G", "Tm", "V", "Sc"]:
            sems[sname] = e(nc.semaphore(f"sem{sname}"))
        block = e(nc.Block())

        P = Prog()

        def joiner(name, waits):
            P.step("V", lambda v: v.memset(junk, 0.0), waits=waits, tag=name, inc=("V", 1))

        def cdma(dst, src):
            return lambda g: g.dma_start(out=dst, in_=src)
        consts = [
            (nw_t[:, :], nw_f), (gf_t[:, :], gf_f), (pmk_t[:, :], pmk_h), (pmv_t[:, :], pmv_h),
            (cosb, cos_h), (sinb, sin_h), (ptm_t[:, :], ptm_h), (idb_t[:, :], idb_h),
            (idf_t[:, :], idf_h), (onb_t[:, :], onb_h), (onf_t[:, :], onf_h), (onc_t[:, :], onc_h),
            (mfir_t[:, :], mfirst_h), (mdia_t[:, :], mdiag_h),
        ]
        for i, (dst, src) in enumerate(consts):
            P.step("G", cdma(dst, src), tag=f"c{i}", inc=("G", 16))
        P.step("V", lambda v: v.memset(smal_t[:, :], 0.0), waits=[], tag="sm0", inc=("V", 1), pad=True)
        P.step("Sc", lambda s: s.activation(out=gfa_t[:, :], in_=gf_t[:, :], func=AF.Abs),
               waits=["c1"], tag="gfab", inc=("Sc", 1), pad=True)
        def epsf(v):
            v.memset(eps5_c, 1e-5)
            return v.memset(eps6_c, 1e-6)
        P.step("V", epsf, waits=[], tag="eps", inc=("V", 1))

        for cc in range(16):
            w = [f"amm{cc-4}"] if cc >= 4 else []
            P.step("Y", cdma(xsc[cc % 4][:, :], x_t[128 * cc:128 * (cc + 1), :]),
                   waits=w, tag=f"axl{cc}", inc=("Y", 16))
            P.step("V", (lambda cc: lambda v: v.tensor_copy(out=xbsc, in_=xsc[cc % 4][:, :]))(cc),
                   waits=[f"axl{cc}"] + ([f"amm{cc-1}"] if cc > 0 else []),
                   tag=f"axb{cc}", inc=("V", 1))
            P.step("Sc", (lambda cc: lambda s: s.activation(out=sqsc, in_=xsc[cc % 4][:, :], func=AF.Square))(cc),
                   waits=[f"axl{cc}"] + ([f"amm{cc-1}"] if cc > 0 else []),
                   tag=f"asq{cc}", inc=("Sc", 1))

            def statmm(cc):
                def f(t):
                    first, last = (cc == 0), (cc == 15)
                    t.matmul(out=pb[0][0:1, :], lhsT=onb_t[:, :], rhs=xbsc[:, 0:512], start=first, stop=last)
                    t.matmul(out=pb[1][0:1, :], lhsT=onb_t[:, :], rhs=xbsc[:, 512:1024], start=first, stop=last)
                    t.matmul(out=pb[2][0:1, :], lhsT=onb_t[:, :], rhs=sqsc[:, 0:512], start=first, stop=last)
                    return t.matmul(out=pb[3][0:1, :], lhsT=onb_t[:, :], rhs=sqsc[:, 512:1024], start=first, stop=last)
                return f
            P.step("Tm", statmm(cc), waits=[f"axb{cc}", f"asq{cc}", "c9"],
                   tag=f"amm{cc}", inc=("Tm", 1), pad=(cc == 15))

        for tt in range(2):
            P.step("V", (lambda tt: lambda v: v.tensor_scalar(
                out=mu_r(tt), in0=pb[tt][0:1, :], scalar1=1.0 / 2048.0, scalar2=None, op0=ALU.mult))(tt),
                waits=["amm15"], tag=f"amu{tt}", inc=("V", 1), pad=True)
            P.step("V", (lambda tt: lambda v: v.tensor_scalar(
                out=aux1(tt), in0=pb[2 + tt][0:1, :], scalar1=1.0 / 2048.0, scalar2=None, op0=ALU.mult))(tt),
                waits=["amm15"], tag=f"am2{tt}", inc=("V", 1), pad=True)
            P.step("Sc", (lambda tt: lambda s: s.activation(out=aux2(tt), in_=mu_r(tt), func=AF.Square))(tt),
                   waits=[f"amu{tt}"], tag=f"amsq{tt}", inc=("Sc", 1), pad=True)
            P.step("V", (lambda tt: lambda v: v.tensor_tensor(
                out=aux1(tt), in0=aux1(tt), in1=aux2(tt), op=ALU.subtract))(tt),
                waits=[f"am2{tt}", f"amsq{tt}"], tag=f"avar{tt}", inc=("V", 1), pad=True)
            P.step("Sc", (lambda tt: lambda s: s.activation(out=aux2(tt), in_=aux1(tt), func=AF.Ln, bias=eps5_c[0:1, :]))(tt),
                   waits=[f"avar{tt}", "eps"], tag=f"alnv{tt}", inc=("Sc", 1), pad=True)
            P.step("Sc", (lambda tt: lambda s: s.activation(out=rho_r(tt), in_=aux1(tt), func=AF.Ln))(tt),
                   waits=[f"avar{tt}"], tag=f"alnt{tt}", inc=("Sc", 1), pad=True)
            P.step("Sc", (lambda tt: lambda s: s.activation(out=aux1(tt), in_=aux2(tt), func=AF.Exp, scale=-0.5))(tt),
                   waits=[f"alnv{tt}", f"alnt{tt}"], tag=f"ar1{tt}", inc=("Sc", 1), pad=True)
            P.step("V", (lambda tt: lambda v: v.tensor_tensor(
                out=rho_r(tt), in0=rho_r(tt), in1=aux2(tt), op=ALU.subtract))(tt),
                waits=[f"ar1{tt}"], tag=f"alnd{tt}", inc=("V", 1), pad=True)
            P.step("Sc", (lambda tt: lambda s: s.activation(out=rho_r(tt), in_=rho_r(tt), func=AF.Exp))(tt),
                   waits=[f"alnd{tt}"], tag=f"at2{tt}", inc=("Sc", 1), pad=True)
            P.step("Sc", (lambda tt: lambda s: s.activation(out=rho_r(tt), in_=rho_r(tt), func=AF.Ln, bias=eps6_c[0:1, :]))(tt),
                   waits=[f"at2{tt}", "eps"], tag=f"alt2{tt}", inc=("Sc", 1), pad=True)
            P.step("Sc", (lambda tt: lambda s: s.activation(out=rho_r(tt), in_=rho_r(tt), func=AF.Exp, scale=-0.5))(tt),
                   waits=[f"alt2{tt}"], tag=f"ar2{tt}", inc=("Sc", 1), pad=True)
            P.step("V", (lambda tt: lambda v: v.tensor_tensor(
                out=rho_r(tt), in0=rho_r(tt), in1=aux1(tt), op=ALU.mult))(tt),
                waits=[f"ar2{tt}"], tag=f"arho{tt}", inc=("V", 1), pad=True)

        def bcmm(dst_bank, src, tt):
            def f(t):
                return t.matmul(out=pb[dst_bank][:, :], lhsT=onf_t[:, :], rhs=src(tt), start=True, stop=True)
            return f
        for tt in range(2):
            P.step("Tm", bcmm(tt, mu_r, tt), waits=[f"arho{tt}", f"amu{tt}", "c10"],
                   tag=f"bmu{tt}", inc=("Tm", 1), pad=True)
            P.step("Tm", bcmm(2 + tt, rho_r, tt), waits=[f"arho{tt}"], tag=f"brho{tt}", inc=("Tm", 1), pad=True)
        for tt in range(2):
            P.step("V", (lambda tt: lambda v: v.tensor_copy(out=mubc[:, 512 * tt:512 * (tt + 1)], in_=pb[tt][:, :]))(tt),
                   waits=[f"bmu{tt}"], tag=f"emu{tt}", inc=("V", 1))
            P.step("V", (lambda tt: lambda v: v.tensor_copy(out=rhobc[:, 512 * tt:512 * (tt + 1)], in_=pb[2 + tt][:, :]))(tt),
                   waits=[f"brho{tt}"], tag=f"erho{tt}", inc=("V", 1))

        if debug2:
            P.step("G", lambda g: g.dma_start(out=dbg["st0"], in_=stat_t[0:1, :]),
                   waits=["arho0", "arho1", "amu0", "amu1"], tag="dst0", inc=("G", 16))
        if debug2:
            P.step("G", lambda g: g.dma_start(out=dbg["gfa"], in_=gfa_t[:, :]),
                   waits=["gfab"], tag="dgfa", inc=("G", 16))
            P.step("G", lambda g: g.dma_start(out=dbg["mb"], in_=mubc),
                   waits=["emu0", "emu1"], tag="dmb", inc=("G", 16))
            P.step("G", lambda g: g.dma_start(out=dbg["rb"], in_=rhobc),
                   waits=["erho0", "erho1"], tag="drb", inc=("G", 16))
        for cc in range(16):
            w = [f"asub{cc-4}"] if cc >= 4 else ["amm15"]
            P.step("Y", cdma(xsc[cc % 4][:, :], x_t[128 * cc:128 * (cc + 1), :]),
                   waits=w, tag=f"axr{cc}", inc=("Y", 16))
            P.step("V", (lambda cc: lambda v: v.tensor_tensor(
                out=tsc, in0=xsc[cc % 4][:, :], in1=mubc, op=ALU.subtract))(cc),
                waits=[f"axr{cc}", "emu0", "emu1"] + ([f"axh{cc-1}"] if cc > 0 else []),
                tag=f"asub{cc}", inc=("V", 1))
            P.step("V", (lambda cc: lambda v: v.tensor_tensor(
                out=xhsc, in0=tsc, in1=rhobc, op=ALU.mult))(cc),
                waits=[f"asub{cc}", "erho0", "erho1"], tag=f"axh{cc}", inc=("V", 1))
            P.step("V", (lambda cc: lambda v: v.tensor_scalar(
                out=xab[:, T * cc:T * (cc + 1)], in0=xhsc,
                scalar1=nw_t[:, cc:cc + 1], scalar2=None, op0=ALU.mult))(cc),
                waits=[f"axh{cc}", "c0"], tag=f"axab{cc}", inc=("V", 1))
            P.step("V", (lambda cc: lambda v: v.tensor_reduce(
                out=amall[:, cc:cc + 1], in_=xhsc, axis=AX, op=ALU.max, apply_absolute_value=True))(cc),
                waits=[f"axh{cc}"], tag=f"aam{cc}", inc=("V", 1), pad=True)
        P.step("V", lambda v: v.tensor_tensor(out=amall, in0=amall, in1=gfa_t[:, :], op=ALU.mult),
               waits=[f"aam{cc}" for cc in range(16)] + ["gfab", "axab15"], tag="agm_m", inc=("V", 1), pad=True)
        P.step("V", lambda v: v.tensor_reduce(out=gmax, in_=amall, axis=AX, op=ALU.max),
               waits=["agm_m"], tag="agm", inc=("V", 1), pad=True)
        APPLY_DONE = "agm"
        if debug:
            P.step("G", lambda g: g.dma_start(out=dbg["xab"], in_=xab),
                   waits=["axab15", "axab14"], tag="dbgxab", inc=("G", 16))

        P.step("V", lambda v: v.memset(ccst_t[:, :], NEG), waits=[], tag="ccm", inc=("V", 1))
        P.step("V", lambda v: v.tensor_copy(out=ccst_t[:, 0:1], in_=gmax),
               waits=["ccm", APPLY_DONE], tag="ccc", inc=("V", 1), pad=True)

        wsh = [wst[:, 0:4096], wst[:, 4096:8192]]
        nvg = 8
        for g in range(nvg):
            w = [f"vcv{g-2}"] if g >= 2 else []
            src = wqkvT[:, 2 * D + 256 * g:2 * D + 256 * (g + 1)].rearrange("(c p) m -> p c m", p=128)
            dst3 = wsh[g % 2].rearrange("p (c m) -> p c m", m=256)
            P.step("Y", (lambda dst3, src: lambda y: y.dma_start(out=dst3, in_=src))(dst3, src),
                   waits=w, tag=f"vld{g}", inc=("Y", 16))
            wguard = [APPLY_DONE] if g < 2 else [f"vmm{g-2}_7"]
            cvt = []
            for i in range(16):
                engx = ["G", "Sc", "G"][i % 3]
                P.step(engx, (lambda g, i: lambda en: ecopy(en,
                    wbf[g % 2][:, 256 * i:256 * (i + 1)], wsh[g % 2][:, 256 * i:256 * (i + 1)]))(g, i),
                    waits=[f"vld{g}"] + wguard, tag=f"vcv{g}_{i}", inc=(engx, 1))
                cvt.append(f"vcv{g}_{i}")
            joiner(f"vcv{g}", cvt)
            for tc in range(8):
                bank = tc % 4

                def vmm(g, tc, bank):
                    def f(t):
                        r = None
                        for cc in range(16):
                            r = t.matmul(out=pb[bank][:, 0:256],
                                         lhsT=xab[:, T * cc + 128 * tc:T * cc + 128 * (tc + 1)],
                                         rhs=wbf[g % 2][:, 256 * cc:256 * (cc + 1)],
                                         start=(cc == 0), stop=(cc == 15))
                        return r
                    return f
                w = [f"vcv{g}", APPLY_DONE]
                if g > 0 or tc >= 4:
                    pg, ptc = (g, tc - 4) if tc >= 4 else (g - 1, tc + 4)
                    w.append(f"vev{pg}_{ptc}")
                P.step("Tm", vmm(g, tc, bank), waits=w, tag=f"vmm{g}_{tc}", inc=("Tm", 1))
                P.step("V", (lambda g, tc, bank: lambda v: v.tensor_copy(
                    out=vsb[:, D * tc + 256 * g:D * tc + 256 * (g + 1)], in_=pb[bank][:, 0:256]))(g, tc, bank),
                    waits=[f"vmm{g}_{tc}"], tag=f"vev{g}_{tc}", inc=("V", 1))
        if debug:
            P.step("G", lambda g: g.dma_start(out=dbg["v"], in_=vsb),
                   waits=[f"vev{nvg-1}_{tc}" for tc in range(8)], tag="dbgv", inc=("G", 16))

        P.step("V", lambda v: v.memset(sa[0][:, :], -1.0e30),
               waits=["erho0", "erho1", f"vev{nvg-1}_6"], tag="sainit0", inc=("V", 1))
        P.step("V", lambda v: v.memset(sa[1][:, :], -1.0e30),
               waits=["erho0", "erho1", f"vev{nvg-1}_7"], tag="sainit1", inc=("V", 1))
        sa_user = {0: "sainit0", 1: "sainit1"}
        ptp_user = {0: None, 1: None}
        ctx_user = {0: None, 1: None}
        negm_user = {0: None, 1: None}
        qkbank_ev = {0: f"vev{nvg-1}_6", 1: f"vev{nvg-1}_7"}
        rot_user = {0: None}
        av_prev = [None]
        rope_reader = {}

        for h in range(16):
            hp = h % 2
            wq_src = wqkvT[:, 128 * h:128 * (h + 1)].rearrange("(c p) m -> p c m", p=128)
            wk_src = wqkvT[:, D + 128 * h:D + 128 * (h + 1)].rearrange("(c p) m -> p c m", p=128)
            dsth = wsh[hp].rearrange("p (c m) -> p c m", m=256)
            wwait = [f"vcv{6+hp}"] if h < 2 else [f"qcv{h-2}"]
            if 2 <= h <= 9:
                wwait.append(f"swa{h-2}b")
            P.step("Y", (lambda dsth, wq_src: lambda y: y.dma_start(out=dsth[:, :, 0:128], in_=wq_src))(dsth, wq_src),
                   waits=wwait, tag=f"qld{h}a", inc=("Y", 16))
            P.step("Y", (lambda dsth, wk_src: lambda y: y.dma_start(out=dsth[:, :, 128:256], in_=wk_src))(dsth, wk_src),
                   waits=[], tag=f"qld{h}", inc=("Y", 16))
            cvw = [f"vmm{6+hp}_7"] if h < 2 else [f"qmmk{h-2}_1"]
            cvt = []
            for i in range(16):
                engx = ["G", "Sc", "G"][i % 3]
                P.step(engx, (lambda hp, i: lambda en: ecopy(en,
                    wbf[hp][:, 256 * i:256 * (i + 1)], wsh[hp][:, 256 * i:256 * (i + 1)]))(hp, i),
                    waits=[f"qld{h}"] + cvw, tag=f"qcv{h}_{i}", inc=(engx, 1))
                cvt.append(f"qcv{h}_{i}")
            joiner(f"qcv{h}", cvt)

            if 1 <= h <= 8:
                k = h - 1
                half = k % 2
                srcw_ = wffnT[:, 256 * k:256 * (k + 1)].rearrange("(c p) m -> p c m", p=128)
                dstw_ = wsh[half].rearrange("p (c m) -> p c m", m=256)
                P.step("Y", (lambda dstw_, srcw_: lambda y: y.dma_start(out=dstw_, in_=srcw_))(dstw_, srcw_),
                       waits=[f"qcv{k}"] + ([f"swa{k-2}b"] if k >= 2 else []),
                       tag=f"swl{k}", inc=("Y", 16))
                P.step("Sc", (lambda half, k: lambda s: s.activation(
                    out=outsb.bitcast(BF16)[:, 0:2048], in_=wsh[half][:, 0:2048], func=AF.Abs,
                    accum_out=swacc[k]))(half, k),
                    waits=[f"swl{k}"] + ([f"swv{k-1}"] if k >= 1 else []), tag=f"swa{k}a", inc=("Sc", 1))
                P.step("Sc", (lambda half, k: lambda s: s.activation(
                    out=outsb.bitcast(BF16)[:, 0:2048], in_=wsh[half][:, 2048:4096], func=AF.Abs,
                    accum_out=swacc_b))(half, k),
                    waits=[f"swa{k}a"], tag=f"swa{k}b", inc=("Sc", 1), pad=True)
                P.step("V", (lambda k: lambda v: v.tensor_tensor(
                    out=swacc[k], in0=swacc[k], in1=swacc_b, op=ALU.add))(k),
                    waits=[f"swa{k}b"], tag=f"swv{k}", inc=("V", 1), pad=True)

            for part, name in ((0, "q"), (1, "k")):
                dstbuf = qh[hp] if part == 0 else kh[hp]
                qt_b = qtmp_b[part]
                ro_b = rots_b[part]
                for tt in range(2):
                    bank = tt

                    def qmm(hp, part, tt, bank):
                        def f(t):
                            r = None
                            wb = wbf[hp]
                            for cc in range(16):
                                r = t.matmul(out=pb[bank][:, :],
                                             lhsT=wb[:, 256 * cc + 128 * part:256 * cc + 128 * (part + 1)],
                                             rhs=xab[:, T * cc + 512 * tt:T * cc + 512 * (tt + 1)],
                                             start=(cc == 0), stop=(cc == 15))
                            return r
                        return f
                    P.step("Tm", qmm(hp, part, tt, bank),
                           waits=[f"qcv{h}", qkbank_ev[bank]],
                           tag=f"qmm{name}{h}_{tt}", inc=("Tm", 1))
                    evw = [f"qmm{name}{h}_{tt}"]
                    if h > 0:
                        evw.append(f"qrot{name}{h-1}_{tt}")
                    if part == 0:
                        P.step("V", (lambda tt, bank, qt_b: lambda v: v.tensor_scalar(
                            out=qt_b[:, 512 * tt:512 * (tt + 1)], in0=pb[bank][:, :],
                            scalar1=CSC, scalar2=None, op0=ALU.mult))(tt, bank, qt_b),
                            waits=evw, tag=f"qev{name}{h}_{tt}", inc=("V", 1))
                    else:
                        P.step("V", (lambda tt, bank, qt_b: lambda v: v.tensor_copy(
                            out=qt_b[:, 512 * tt:512 * (tt + 1)], in_=pb[bank][:, :]))(tt, bank, qt_b),
                            waits=evw, tag=f"qev{name}{h}_{tt}", inc=("V", 1))
                    qkbank_ev[bank] = f"qev{name}{h}_{tt}"
                    rw_ = [f"qev{name}{h}_{tt}", "c6"]
                    if rot_user[0]:
                        rw_.append(rot_user[0])
                    P.step("Tm", (lambda tt, qt_b: lambda t: t.matmul(
                        out=rotps[:, :], lhsT=ptm_t[:, :], rhs=qt_b[:, 512 * tt:512 * (tt + 1)],
                        start=True, stop=True))(tt, qt_b),
                        waits=rw_, tag=f"qrot{name}{h}_{tt}", inc=("Tm", 1))
                    sw_ = [f"qrot{name}{h}_{tt}"]
                    if h > 0:
                        sw_.append(f"rope{name}{h-1}")
                    P.step("Sc", (lambda tt, ro_b: lambda s: s.copy(
                        out=ro_b[:, 512 * tt:512 * (tt + 1)], in_=rotps[:, :]))(tt, ro_b),
                        waits=sw_, tag=f"qrev{name}{h}_{tt}", inc=("Sc", 1))
                    rot_user[0] = f"qrev{name}{h}_{tt}"
                P.step("V", (lambda qt_b: lambda v: v.tensor_tensor(
                    out=t1b, in0=qt_b, in1=cosb, op=ALU.mult))(qt_b),
                    waits=[f"qev{name}{h}_1", "c4"], tag=f"rt1{name}{h}", inc=("V", 1))
                P.step("V", (lambda ro_b: lambda v: v.tensor_tensor(
                    out=ro_b, in0=ro_b, in1=sinb, op=ALU.mult))(ro_b),
                    waits=[f"qrev{name}{h}_0", f"qrev{name}{h}_1", "c5"], tag=f"rt2{name}{h}", inc=("V", 1))
                ropew = [f"rt1{name}{h}", f"rt2{name}{h}"]
                if h >= 2 and rope_reader.get(hp):
                    ropew.append(rope_reader[hp])
                P.step("V", (lambda dstbuf, ro_b: lambda v: v.tensor_tensor(
                    out=dstbuf, in0=t1b, in1=ro_b, op=ALU.add))(dstbuf, ro_b),
                    waits=ropew, tag=f"rope{name}{h}", inc=("V", 1))
            if debug:
                P.step("G", (lambda h: lambda g: g.dma_start(out=dbg["qh"][h], in_=qh[h % 2]))(h),
                       waits=[f"ropeq{h}"], tag=f"dbgq{h}", inc=("G", 16))
                P.step("G", (lambda h: lambda g: g.dma_start(out=dbg["kh"][h], in_=kh[h % 2]))(h),
                       waits=[f"ropek{h}"], tag=f"dbgk{h}", inc=("G", 16))

            def attn_A(ah, pos):
                w_ = pos % 2
                qc = pos // 2
                par = w_
                ntok = 128 * (qc + 1)
                ahp = ah % 2
                qslice = qh[ahp][:, 512 * w_ + 128 * qc:512 * w_ + 128 * (qc + 1)]
                kslice = kh[ahp][:, 512 * w_:512 * w_ + ntok]
                wts = [f"ropeq{ah}", f"ropek{ah}", "c2", "c12", "c13", "c7", sa_user[par]]
                msk = mfir_t if qc == 0 else mdia_t

                def qkmm(qslice, kslice, par, ntok, ah, qc, msk):
                    def f(t):
                        t.matmul(out=sa[par][:, 0:ntok], lhsT=qslice, rhs=kslice,
                                 start=True, stop=True)
                        mm = t.matmul(out=sa[par][:, 128 * qc:ntok], lhsT=idb_t[:, :],
                                      rhs=msk[:, :], start=False, stop=True)
                        if qc < 3:
                            mm = t.matmul(out=sa[par][:, ntok:ntok + 16], lhsT=qslice,
                                          rhs=pmk_t[:, 16 * ah:16 * (ah + 1)], start=True, stop=True)
                        return mm
                    return f
                P.step("Tm", qkmm(qslice, kslice, par, ntok, ah, qc, msk),
                       waits=wts, tag=f"qk{ah}_{w_}_{qc}", inc=("Tm", 1))
                area = ntok + 16 if qc < 3 else 512
                P.step("V", (lambda par, area: lambda v: v.tensor_reduce(
                    out=negm_c[par], in_=sa[par][:, 0:area], axis=AX, op=ALU.max, negate=True))(par, area),
                    waits=[f"qk{ah}_{w_}_{qc}"] + ([negm_user[par]] if negm_user[par] else []),
                    tag=f"ngm{ah}_{w_}_{qc}", inc=("V", 1), pad=True)

            def attn_PM3(ah, pos):
                w_ = pos % 2
                par = w_
                pm_bank, pm_c0 = 1 - par, 480
                ahp = ah % 2
                qc = pos // 2
                qslice = qh[ahp][:, 512 * w_ + 128 * qc:512 * w_ + 128 * (qc + 1)]
                P.step("Tm", (lambda qslice, pm_bank, pm_c0, ah: lambda t: t.matmul(
                    out=sa[pm_bank][:, pm_c0:pm_c0 + 16], lhsT=qslice,
                    rhs=pmk_t[:, 16 * ah:16 * (ah + 1)], start=True, stop=True))(qslice, pm_bank, pm_c0, ah),
                    waits=[f"qk{ah}_{w_}_3", sa_user[1 - par]], tag=f"pm3{ah}_{w_}", inc=("Tm", 1))

            def attn_B(ah, pos):
                w_ = pos % 2
                qc = pos // 2
                par = w_
                ntok = 128 * (qc + 1)
                pm_bank, pm_c0 = (par, ntok) if qc < 3 else (1 - par, 480)
                expw = [f"ngm{ah}_{w_}_{qc}"]
                if qc == 3:
                    expw.append(f"pm3{ah}_{w_}")
                prev_pn = (f"pn{ah}_{w_}_{qc-1}" if qc >= 1 else
                           (f"pn{ah-1}_{w_}_3" if ah > 0 else None))
                if prev_pn:
                    expw.append(prev_pn)

                def expf(par, ntok, pm_bank, pm_c0):
                    def f(s):
                        s.activation(out=p_sb[par][:, 16:16 + ntok], in_=sa[par][:, 0:ntok],
                                     func=AF.Exp, bias=negm_c[par], scale=1.0, accum_out=za[par])
                        return s.activation(out=p_sb[par][:, 0:16], in_=sa[pm_bank][:, pm_c0:pm_c0 + 16],
                                            func=AF.Exp, bias=negm_c[par], scale=1.0, accum_out=zb[par])
                    return f
                P.step("Sc", expf(par, ntok, pm_bank, pm_c0),
                       waits=expw, tag=f"exp{ah}_{w_}_{qc}", inc=("Sc", 1), pad=True)
                sa_user[par] = f"exp{ah}_{w_}_{qc}"
                if qc == 3:
                    sa_user[1 - par] = f"exp{ah}_{w_}_{qc}"
                negm_user[par] = f"exp{ah}_{w_}_{qc}"

            def attn_C(ah, pos):
                nonlocal_av = None
                w_ = pos % 2
                qc = pos // 2
                par = w_
                ntok = 128 * (qc + 1)
                P.step("V", (lambda par: lambda v: v.tensor_tensor(
                    out=zs[par], in0=za[par], in1=zb[par], op=ALU.add))(par),
                    waits=[f"exp{ah}_{w_}_{qc}"], tag=f"zz{ah}_{w_}_{qc}", inc=("V", 1), pad=True, dup=True)
                P.step("V", (lambda par: lambda v: v.reciprocal(out=rz[par], in_=zs[par]))(par),
                       waits=[f"zz{ah}_{w_}_{qc}"], tag=f"rzt{ah}_{w_}_{qc}", inc=("V", 1), pad=True, dup=True)
                pnw = [f"rzt{ah}_{w_}_{qc}"]
                prev_tp = (f"tp{ah}_{w_}_{qc-1}" if qc >= 1 else
                           (f"tp{ah-1}_{w_}_3" if ah > 0 else None))
                if prev_tp:
                    pnw.append(prev_tp)
                P.step("V", (lambda par, ntok: lambda v: v.tensor_scalar(
                    out=p_n[par][:, 0:16 + ntok], in0=p_sb[par][:, 0:16 + ntok],
                    scalar1=rz[par], scalar2=None, op0=ALU.mult))(par, ntok),
                    waits=pnw, tag=f"pn{ah}_{w_}_{qc}", inc=("V", 1))
                if debug:
                    P.step("G", (lambda ah, w_, qc, par: lambda g: g.dma_start(
                        out=dbg["p"][ah, w_, qc], in_=p_n[par]))(ah, w_, qc, par),
                        waits=[f"pn{ah}_{w_}_{qc}"], tag=f"dbgp{ah}_{w_}_{qc}", inc=("G", 16))
                ntp = qc + 2
                tpe_tags = []
                for jj in range(ntp):
                    tpar = jj % 2
                    tw = [f"pn{ah}_{w_}_{qc}", "c7"]
                    if ptp_user[tpar]:
                        tw.append(ptp_user[tpar])
                    if jj == 0:
                        srcp = p_n[par][:, 0:16]
                        dstp = ptp[tpar][0:16, :]
                        dste = pmpt[w_][:, 128 * qc:128 * (qc + 1)]
                    else:
                        j = jj - 1
                        srcp = p_n[par][:, 16 + 128 * j:16 + 128 * (j + 1)]
                        dstp = ptp[tpar][:, :]
                        dste = ptj[w_][j][:, 128 * (qc - j):128 * (qc - j + 1)]
                    P.step("Tm", (lambda srcp, dstp: lambda t: t.transpose(
                        out=dstp, in_=srcp, identity=idb_t[:, :]))(srcp, dstp),
                        waits=tw, tag=f"tpp{ah}_{w_}_{qc}_{jj}", inc=("Tm", 1))
                    ew = [f"tpp{ah}_{w_}_{qc}_{jj}"]
                    if av_prev[0]:
                        ew.append(av_prev[0])
                    if jj % 2 == 0:
                        P.step("Sc", (lambda dstp, dste: lambda s: s.copy(out=dste, in_=dstp))(dstp, dste),
                               waits=ew, tag=f"tpe{ah}_{w_}_{qc}_{jj}", inc=("Sc", 1))
                    else:
                        P.step("V", (lambda dstp, dste: lambda v: v.tensor_copy(out=dste, in_=dstp))(dstp, dste),
                               waits=ew, tag=f"tpe{ah}_{w_}_{qc}_{jj}", inc=("V", 1))
                    ptp_user[tpar] = f"tpe{ah}_{w_}_{qc}_{jj}"
                    tpe_tags.append(f"tpe{ah}_{w_}_{qc}_{jj}")
                joiner(f"tp{ah}_{w_}_{qc}", tpe_tags)

            def attn_AV(ah):
                for w_ in range(2):
                    avw = [f"tp{ah}_{w_}_3", "c3"]
                    avw += [f"vev{ah // 2}_{4 * w_ + j}" for j in range(4)]
                    if ctx_user[0]:
                        avw.append(ctx_user[0])

                    def avf(ah, w_):
                        def f(t):
                            t.matmul(out=ctxps[0][:, :], lhsT=pmv_t[:, 128 * ah:128 * (ah + 1)],
                                     rhs=pmpt[w_][:, :], start=True, stop=False)
                            r = None
                            for j in range(4):
                                tcj = 4 * w_ + j
                                nq = 512 - 128 * j
                                r = t.matmul(out=ctxps[0][:, 128 * j:512],
                                             lhsT=vsb[:, D * tcj + 128 * ah:D * tcj + 128 * (ah + 1)],
                                             rhs=ptj[w_][j][:, 0:nq],
                                             start=False, stop=(j == 3))
                            return r
                        return f
                    P.step("Tm", avf(ah, w_), waits=avw, tag=f"av{ah}_{w_}", inc=("Tm", 1))
                    av_prev[0] = f"av{ah}_{w_}"
                    P.step("V", (lambda ah, w_: lambda v: v.tensor_copy(
                        out=ctxb[:, T * ah + 512 * w_:T * ah + 512 * (w_ + 1)], in_=ctxps[0][:, :]))(ah, w_),
                        waits=[f"av{ah}_{w_}"], tag=f"cev{ah}_{w_}", inc=("V", 1))
                    ctx_user[0] = f"cev{ah}_{w_}"
                rope_reader[ah % 2] = f"pm3{ah}_1"

            def attn_all(ah):
                for r in range(10):
                    if r == 7:
                        attn_PM3(ah, 6)
                    if r == 8:
                        attn_PM3(ah, 7)
                    if 1 <= r <= 8:
                        attn_B(ah, r - 1)
                    if r < 8:
                        attn_A(ah, r)
                    if 2 <= r:
                        attn_C(ah, r - 2)
                attn_AV(ah)

            if h >= 1:
                attn_all(h - 1)
        attn_all(15)

        if debug:
            P.step("G", lambda g: g.dma_start(out=dbg["ctx"], in_=ctxb),
                   waits=["cev15_1", "cev15_0"], tag="dbgctx", inc=("G", 16))

        P.step("G", lambda g: g.dma_start(out=cc_in, in_=ccst_t[:, :]), waits=["ccc"], tag="ccw", inc=("G", 16))
        P.step("G", lambda g: g.collective_compute(
            "AllReduce", ALU.max, replica_groups=[list(range(NCORES))],
            ins=[cc_in.opt()], outs=[cc_out.opt()]),
            waits=["ccw"], tag="ccr", inc=("G", 1))
        P.step("G", lambda g: g.dma_start(out=ccst_t[:, :], in_=cc_out),
               waits=["ccr"], tag="ccb", inc=("G", 16))

        SA0_FIN = sa_user[0]
        SA1_FIN = sa_user[1]
        QK0_FIN = qkbank_ev[0]
        QK1_FIN = qkbank_ev[1]

        P.step("V", lambda v: v.tensor_reduce(out=swsum, in_=smal_t[:, 15:23], axis=AX, op=ALU.add),
               waits=[f"swv{k}" for k in range(8)], tag="sws", inc=("V", 1), pad=True)
        P.step("Tm", lambda t: t.matmul(out=pb[0][0:1, 0:1], lhsT=swsum, rhs=onc_t[:, :],
                                        start=True, stop=True),
               waits=["sws", "c11", QK0_FIN], tag="swm", inc=("Tm", 1), pad=True)
        P.step("V", lambda v: v.tensor_scalar(out=sw_s[0:1, :], in0=pb[0][0:1, 0:1],
                                              scalar1=1.0 / (2048.0 * 2048.0), scalar2=None, op0=ALU.mult),
               waits=["swm"], tag="swv", inc=("V", 1), pad=True)
        P.step("Tm", lambda t: t.transpose(out=pb[1][0:1, 0:128], in_=ccst_t[:, 0:1], identity=idf_t[:, :]),
               waits=["ccb", "c8", QK1_FIN], tag="qst", inc=("Tm", 1), pad=True)
        P.step("V", lambda v: v.tensor_reduce(out=tmpa[0:1, :], in_=pb[1][0:1, 0:128], axis=AX, op=ALU.max),
               waits=["qst"], tag="qsr", inc=("V", 1), pad=True)
        P.step("V", lambda v: v.tensor_scalar(out=tmpa[0:1, :], in0=tmpa[0:1, :], scalar1=1e-5,
                                              scalar2=None, op0=ALU.max),
               waits=["qsr"], tag="qse", inc=("V", 1), pad=True)
        P.step("Tm", lambda t: t.matmul(out=pb[0][:, 0:1], lhsT=onf_t[:, :], rhs=tmpa[0:1, :],
                                        start=True, stop=True),
               waits=["qse", "swv", "c10"], tag="qsb", inc=("Tm", 1), pad=True)
        P.step("V", lambda v: v.tensor_copy(out=sc_bc, in_=pb[0][:, 0:1]),
               waits=["qsb"], tag="qsc", inc=("V", 1), pad=True)
        P.step("Tm", lambda t: t.matmul(out=pb[1][:, 0:1], lhsT=onf_t[:, :], rhs=sw_s[0:1, :], start=True, stop=True),
               waits=["qsc", "swv"], tag="qwb", inc=("Tm", 1), pad=True)
        P.step("V", lambda v: v.tensor_copy(out=tmpb, in_=pb[1][:, 0:1]),
               waits=["qwb"], tag="qwc", inc=("V", 1), pad=True)
        P.step("V", lambda v: v.reciprocal(out=rz[0], in_=sc_bc), waits=["qsc", "pn15_1_3"], tag="qrc", inc=("V", 1), pad=True)
        P.step("V", lambda v: v.tensor_scalar(out=s127, in0=rz[0], scalar1=127.0, scalar2=None, op0=ALU.mult),
               waits=["qrc"], tag="qs127", inc=("V", 1), pad=True)
        P.step("V", lambda v: v.reciprocal(out=rw_bc, in_=tmpb), waits=["qwc"], tag="qrw", inc=("V", 1), pad=True)
        P.step("V", lambda v: v.tensor_tensor(out=gamc, in0=sc_bc, in1=tmpb, op=ALU.mult),
               waits=["qwc"], tag="qg1", inc=("V", 1), pad=True)
        P.step("V", lambda v: v.tensor_scalar(out=gamc, in0=gamc, scalar1=1.0 / 127.0, scalar2=None, op0=ALU.mult),
               waits=["qg1"], tag="qgam", inc=("V", 1), pad=True)
        if debug:
            def dbgsc_f(v):
                v.tensor_copy(out=outsb[:, 0:1], in_=sc_bc)
                v.tensor_copy(out=outsb[:, 1:2], in_=tmpb)
                v.tensor_copy(out=outsb[:, 2:3], in_=gamc)
                v.tensor_copy(out=outsb[:, 3:4], in_=s127)
                for kk in range(8):
                    v.tensor_copy(out=outsb[:, 5 + kk:6 + kk], in_=swacc[kk])
                v.tensor_copy(out=outsb[:, 13:14], in_=swsum)
                return v.tensor_copy(out=outsb[:, 4:5], in_=gmax)
            P.step("V", dbgsc_f, waits=["qgam", "qs127", "swa7b"], tag="dbgscv", inc=("V", 1))
            P.step("G", lambda g: g.dma_start(out=dbg["sc"], in_=outsb[:, 0:16]),
                   waits=["dbgscv"], tag="dbgsc", inc=("G", 16))

        XAB_FREE = "qmmk15_1"
        for tt in range(2):
            P.step("Tm", bcmm(tt, mu_r, tt), waits=["qsb", "qsc", XAB_FREE], tag=f"b2mu{tt}", inc=("Tm", 1), pad=True)
            P.step("Tm", bcmm(2 + tt, rho_r, tt), waits=[SA0_FIN, SA1_FIN, "qwc"], tag=f"b2rho{tt}", inc=("Tm", 1), pad=True)
        for tt in range(2):
            P.step("V", (lambda tt: lambda v: v.tensor_copy(
                out=mubc2[:, 512 * tt:512 * (tt + 1)], in_=pb[tt][:, :]))(tt),
                waits=[f"b2mu{tt}"], tag=f"e2mu{tt}", inc=("V", 1))
            P.step("V", (lambda tt: lambda v: v.tensor_copy(
                out=rhobc2[:, 512 * tt:512 * (tt + 1)], in_=pb[2 + tt][:, :]))(tt),
                waits=[f"b2rho{tt}"], tag=f"e2rho{tt}", inc=("V", 1))

        xqb = vsb
        for cc in range(16):
            w = [f"exh{cc-2}"] if cc >= 2 else ["e2mu0", "e2mu1", "e2rho0", "e2rho1"]
            P.step("Y", (lambda cc: lambda y: y.dma_start(
                out=xsc3[cc % 2][:, :], in_=x_t[128 * cc:128 * (cc + 1), :]))(cc),
                waits=w, tag=f"exl{cc}", inc=("Y", 16))
            P.step("V", (lambda cc: lambda v: v.tensor_tensor(
                out=tsc3, in0=xsc3[cc % 2][:, :], in1=mubc2, op=ALU.subtract))(cc),
                waits=[f"exl{cc}"] + ([f"exq{cc-1}"] if cc > 0 else []), tag=f"esb{cc}", inc=("V", 1))
            P.step("V", (lambda cc: lambda v: v.tensor_tensor(
                out=xhsc3, in0=tsc3, in1=rhobc2, op=ALU.mult))(cc),
                waits=[f"esb{cc}"], tag=f"exh{cc}", inc=("V", 1))
            P.step("V", (lambda cc: lambda v: v.tensor_scalar(
                out=xgsc, in0=xhsc3, scalar1=gf_t[:, cc:cc + 1],
                scalar2=s127, op0=ALU.mult, op1=ALU.mult))(cc),
                waits=[f"exh{cc}", "qs127", "c1"], tag=f"exg{cc}", inc=("V", 1))
            iw = [f"exg{cc}"]
            if cc == 0:
                iw += ["av15_1", "cev15_1"]
            P.step("V", (lambda cc: lambda v: v.tensor_copy(out=xisc, in_=xgsc))(cc),
                   waits=iw, tag=f"exi{cc}", inc=("V", 1))
            P.step("V", (lambda cc: lambda v: v.tensor_copy(
                out=xqb[:, T * cc:T * (cc + 1)], in_=xisc))(cc),
                waits=[f"exi{cc}"], tag=f"exq{cc}", inc=("V", 1))
        if debug:
            P.step("G", lambda g: g.dma_start(out=dbg["xqb"], in_=xqb),
                   waits=["exq15"], tag="dbgxqb", inc=("G", 16))

        wo_user = {0: "swa6b", 1: "swa7b"}
        wob_user = {0: "qmmk14_1", 1: "qmmk15_1"}
        wfb_user = {0: "tp15_1_3", 1: "tp15_1_3"}
        pbD_user = {0: "e2mu0", 1: "e2mu1"}
        pbF_user = {0: "e2rho0", 1: "e2rho1"}
        for og in range(8):
            par = og % 2
            srco = woutT[:, 256 * og:256 * (og + 1)].rearrange("(c p) m -> p c m", p=128)
            srcf = wffnT[:, 256 * og:256 * (og + 1)].rearrange("(c p) m -> p c m", p=128)
            dsto = wsh[par].rearrange("p (c m) -> p c m", m=256)
            dstf = wsh[1 - par].rearrange("p (c m) -> p c m", m=256)
            P.step("Y", (lambda dsto, srco: lambda y: y.dma_start(out=dsto, in_=srco))(dsto, srco),
                   waits=[wo_user[par]], tag=f"dol{og}", inc=("Y", 16))
            P.step("Y", (lambda dstf, srcf: lambda y: y.dma_start(out=dstf, in_=srcf))(dstf, srcf),
                   waits=[wo_user[1 - par]], tag=f"dfl{og}", inc=("Y", 16))
            cvt = []
            for i in range(16):
                engx = ["G", "Sc"][i % 2]
                P.step(engx, (lambda par, i: lambda en: ecopy(en,
                    wob[par][:, 256 * i:256 * (i + 1)], wsh[par][:, 256 * i:256 * (i + 1)]))(par, i),
                    waits=[f"dol{og}", wob_user[par]], tag=f"dcv{og}_{i}", inc=(engx, 1))
                cvt.append(f"dcv{og}_{i}")
            joiner(f"dcv{og}", cvt)
            wo_user[par] = f"dcv{og}"
            fcvt = []
            for i in range(8):
                c0 = 512 * i
                srcw = wsh[1 - par][:, c0:c0 + 512]
                prevd = f"f2_{og}_{i-1}" if i > 0 else (f"f2_{og-1}_7" if og > 0 else None)
                prevc = f"f3_{og}_{i-1}" if i > 0 else (f"f3_{og-1}_7" if og > 0 else None)
                w1 = [f"dfl{og}", "qrw", wfb_user[par], "exq15"]
                if prevd:
                    w1.append(prevd)
                P.step("V", (lambda srcw: lambda v: v.tensor_scalar(
                    out=xgsc[:, 0:512], in0=srcw, scalar1=rw_bc, scalar2=1.0,
                    op0=ALU.mult, op1=ALU.min))(srcw),
                    waits=w1, tag=f"f1_{og}_{i}", inc=("V", 1))
                w2 = [f"f1_{og}_{i}"]
                if prevc:
                    w2.append(prevc)
                P.step("V", lambda v: v.tensor_scalar(
                    out=tsc3[:, 0:512], in0=xgsc[:, 0:512], scalar1=-1.0, scalar2=None, op0=ALU.max),
                    waits=w2, tag=f"f2_{og}_{i}", inc=("V", 1))
                P.step("V", lambda v: v.tensor_copy(out=xisc[:, 0:512], in_=tsc3[:, 0:512]),
                       waits=[f"f2_{og}_{i}"], tag=f"f3_{og}_{i}", inc=("V", 1))
                P.step("V", (lambda par, i: lambda v: v.tensor_copy(
                    out=wfb[par].rearrange("p (a b) -> p a b", a=8)[:, i, :],
                    in_=xisc[:, 0:512]))(par, i),
                    waits=[f"f3_{og}_{i}"], tag=f"fcv{og}_{i}", inc=("V", 1))
                fcvt.append(f"fcv{og}_{i}")
            joiner(f"fcv{og}", fcvt)
            wo_user[1 - par] = f"fcv{og}"

            for tc in range(8):
                xw = [f"res{og}_{tc-2}"] if tc >= 2 else ([f"res{og-1}_{tc+6}"] if og > 0 else [])
                P.step("G", (lambda og, tc: lambda g: g.dma_start(
                    out=xnst[:, 256 * (tc % 2):256 * (tc % 2) + 256],
                    in_=x_n[128 * tc:128 * (tc + 1), 256 * og:256 * (og + 1)]))(og, tc),
                    waits=xw, tag=f"xnl{og}_{tc}", inc=("G", 16))
                bD = tc % 2
                bF = 2 + tc % 2

                def domm(og, tc, bD, par):
                    def f(t):
                        r = None
                        for hh in range(16):
                            r = t.matmul(out=pb[bD][:, 0:256],
                                         lhsT=ctxb[:, T * hh + 128 * tc:T * hh + 128 * (tc + 1)],
                                         rhs=wob[par][:, 256 * hh:256 * (hh + 1)],
                                         start=(hh == 0), stop=(hh == 15))
                        return r
                    return f

                def femm(og, tc, bF, par):
                    def f(t):
                        r = None
                        for cc in range(16):
                            r = t.matmul(out=pb[bF][:, 0:256],
                                         lhsT=xqb[:, T * cc + 128 * tc:T * cc + 128 * (tc + 1)],
                                         rhs=wfb[par][:, 256 * cc:256 * (cc + 1)],
                                         start=(cc == 0), stop=(cc == 15))
                        return r
                    return f
                P.step("Tm", domm(og, tc, bD, par),
                       waits=[f"dcv{og}", pbD_user[tc % 2], "cev15_1"], tag=f"dmm{og}_{tc}", inc=("Tm", 1))
                P.step("Tm", femm(og, tc, bF, par),
                       waits=[f"fcv{og}", pbF_user[tc % 2], "exq15"], tag=f"fmm{og}_{tc}", inc=("Tm", 1))
                e1w = [f"dmm{og}_{tc}", f"xnl{og}_{tc}"]
                if tc >= 2:
                    e1w.append(f"res{og}_{tc-2}")
                elif og > 0:
                    e1w.append(f"res{og-1}_{tc+6}")
                P.step("V", (lambda og, tc, bD: lambda v: v.tensor_tensor(
                    out=o1b[:, 256 * (tc % 2):256 * (tc % 2) + 256], in0=pb[bD][:, 0:256],
                    in1=xnst[:, 256 * (tc % 2):256 * (tc % 2) + 256], op=ALU.add))(og, tc, bD),
                    waits=e1w, tag=f"ev1{og}_{tc}", inc=("V", 1))
                pbD_user[tc % 2] = f"ev1{og}_{tc}"
                rw2 = [f"ev1{og}_{tc}", f"fmm{og}_{tc}", "qgam"]
                if tc >= 2:
                    rw2.append(f"sto{og}_{tc-2}")
                elif og > 0:
                    rw2.append(f"sto{og-1}_{tc+6}")
                P.step("V", (lambda og, tc, bF: lambda v: v.scalar_tensor_tensor(
                    out=outsb[:, 512 * (tc % 2):512 * (tc % 2) + 256], in0=pb[bF][:, 0:256],
                    scalar=gamc, in1=o1b[:, 256 * (tc % 2):256 * (tc % 2) + 256],
                    op0=ALU.mult, op1=ALU.add))(og, tc, bF),
                    waits=rw2, tag=f"res{og}_{tc}", inc=("V", 1))
                pbF_user[tc % 2] = f"res{og}_{tc}"
                P.step("G", (lambda og, tc: lambda g: g.dma_start(
                    out=out_d[128 * tc:128 * (tc + 1), 256 * og:256 * (og + 1)],
                    in_=outsb[:, 512 * (tc % 2):512 * (tc % 2) + 256]))(og, tc),
                    waits=[f"res{og}_{tc}"], tag=f"sto{og}_{tc}", inc=("G", 16))
            wob_user[par] = f"dmm{og}_7"
            wfb_user[par] = f"fmm{og}_7"

        if debug2:
            ALLST = [f"sto{og}_{tc}" for og in range(8) for tc in range(8)]
            P.step("G", lambda g: g.dma_start(out=dbg["ctx"], in_=ctxb), waits=ALLST, tag="d2a", inc=("G", 16))
            P.step("G", lambda g: g.dma_start(out=dbg["xqb"], in_=xqb), waits=["d2a"], tag="d2b", inc=("G", 16))
            def d2scf(v):
                v.tensor_copy(out=outsb[:, 0:1], in_=sc_bc)
                v.tensor_copy(out=outsb[:, 1:2], in_=tmpb)
                v.tensor_copy(out=outsb[:, 2:3], in_=gamc)
                v.tensor_copy(out=outsb[:, 3:4], in_=s127)
                return v.tensor_copy(out=outsb[:, 4:5], in_=gmax)
            P.step("V", d2scf, waits=ALLST, tag="d2sc", inc=("V", 1))
            P.step("G", lambda g: g.dma_start(out=dbg["sc"], in_=outsb[:, 0:16]), waits=["d2sc"], tag="d2c", inc=("G", 16))
            P.step("G", lambda g: g.dma_start(out=dbg["pn"][0], in_=p_n[0]), waits=["d2c"], tag="d2d", inc=("G", 16))
            P.step("G", lambda g: g.dma_start(out=dbg["pn"][1], in_=p_n[1]), waits=["d2d"], tag="d2e", inc=("G", 16))
            P.step("G", lambda g: g.dma_start(out=dbg["z"], in_=smal_t[:, :]), waits=["d2e"], tag="d2f", inc=("G", 16))
            P.step("G", lambda g: g.dma_start(out=dbg["v"], in_=vsb), waits=["d2f"], tag="d2g", inc=("G", 16))
            P.step("G", lambda g: g.dma_start(out=dbg["st"], in_=stat_t[:, :]), waits=["d2g"], tag="d2h", inc=("G", 16))
            P.step("G", lambda g: g.dma_start(out=dbg["bc"][:, 0:1024], in_=mubc2),
                   waits=["d2h"], tag="d2i0", inc=("G", 16))
            P.step("G", lambda g: g.dma_start(out=dbg["bc"][:, 1024:2048], in_=rhobc2),
                   waits=["d2i0"], tag="d2i", inc=("G", 16))
        P.step("G", None, waits=[f"sto{og}_{tc}" for og in range(8) for tc in range(8)]
               + (["d2i"] if debug2 else []))

        for (e_, fn_, waits_, inc_, pad_, dup_) in P.steps:
            for (sn, val) in waits_:
                assert val <= P.cnt[sn], (sn, val, P.cnt[sn])

        @block.sync
        def _(y):
            P.emit("Y", y, sems)

        @block.vector
        def _(v):
            P.emit("V", v, sems, padfn=lambda en: en.memset(junkv, 0.0))

        @block.scalar
        def _(s):
            def scpad(en):
                en.memzero(junks)
                return en.memzero(junks)
            P.emit("Sc", s, sems, padfn=scpad)

        @block.tensor
        def _(t):
            P.emit("Tm", t, sems,
                   padfn=lambda en: en.transpose(out=ptp[0][0:1, 0:128], in_=onb_t[:, :],
                                                 identity=idb_t[:, :]))

        @block.gpsimd
        def _(g):
            P.emit("G", g, sems, padfn=lambda en: en.memset(junkg, 0.0))

    return nc


def _host_prep(inputs):
    x = np.ascontiguousarray(np.asarray(inputs["L0_attn_x"], dtype=np.float32).reshape(NTOK, D))
    wqkvT = np.ascontiguousarray(np.asarray(inputs["L0_attn_w_qkv"], dtype=np.float32).T)
    woutT = np.ascontiguousarray(np.asarray(inputs["L0_attn_w_out"], dtype=np.float32).T)
    wffnT = np.ascontiguousarray(np.asarray(inputs["L0_ffn_weight"], dtype=np.float32).T)
    nw = np.asarray(inputs["L0_attn_norm_w"], dtype=np.float32)
    gf = np.asarray(inputs["L0_ffn_norm_weight"], dtype=np.float32)
    pm = np.asarray(inputs["L0_attn_pm"], dtype=np.float32)

    nw_f = np.ascontiguousarray(nw.reshape(16, 128).T)
    gf_f = np.ascontiguousarray(gf.reshape(16, 128).T)
    pmk_h = np.ascontiguousarray(pm[0].transpose(2, 0, 1).reshape(128, 256)).astype(ml_dtypes.bfloat16)
    pmv_h = np.ascontiguousarray(pm[1].transpose(1, 0, 2).reshape(16, 2048)).astype(ml_dtypes.bfloat16)

    inv = 1.0 / (10000.0 ** (np.arange(0, DH, 2, dtype=np.float64) / DH))
    ptmat = np.zeros((128, 128), dtype=np.float32)
    for i in range(64):
        ptmat[2 * i, 2 * i + 1] = -1.0
        ptmat[2 * i + 1, 2 * i] = 1.0
    ptm_h = np.ascontiguousarray(ptmat.T).astype(ml_dtypes.bfloat16)
    idb_h = np.eye(128, dtype=np.float32).astype(ml_dtypes.bfloat16)
    idf_h = np.eye(128, dtype=np.float32)
    onb_h = np.ones((128, 1), dtype=np.float32).astype(ml_dtypes.bfloat16)
    onf_h = np.ones((1, 128), dtype=np.float32)
    onc_h = np.ones((128, 1), dtype=np.float32)
    r = np.arange(128)
    c = np.arange(128)
    mdiag = np.where(c[None, :] <= r[:, None], 0.0, -1.0e30).astype(np.float32).astype(ml_dtypes.bfloat16)
    mfirst = np.where((c[None, :] <= r[:, None]) | (c[None, :] < 16), 0.0, -1.0e30).astype(np.float32).astype(ml_dtypes.bfloat16)

    in_maps = []
    for core in range(NCORES):
        sh = x[T * core:T * (core + 1)]
        pos = np.arange(T * core, T * (core + 1), dtype=np.float64)
        fr = pos[:, None] * inv[None, :]
        cosx = np.repeat(np.cos(fr), 2, axis=1).T.astype(np.float32)
        sinx = np.repeat(np.sin(fr), 2, axis=1).T.astype(np.float32)
        in_maps.append(dict(
            x_t=np.ascontiguousarray(sh.T), x_n=np.ascontiguousarray(sh),
            wqkvT=wqkvT, woutT=woutT, wffnT=wffnT, nw_f=nw_f, gf_f=gf_f,
            pmk_h=pmk_h, pmv_h=pmv_h,
            cos_h=np.ascontiguousarray(cosx).astype(ml_dtypes.bfloat16),
            sin_h=np.ascontiguousarray(sinx).astype(ml_dtypes.bfloat16),
            ptm_h=ptm_h, idb_h=idb_h, idf_h=idf_h, onb_h=onb_h, onf_h=onf_h,
            onc_h=onc_h, mfirst_h=mfirst, mdiag_h=mdiag,
        ))
    return in_maps


def kernel(**inputs):
    in_maps = _host_prep(inputs)
    nc = build_nc(debug=DEBUG)
    res = run_bass_kernel_spmd(nc, in_maps, core_ids=list(range(NCORES)))
    out = np.concatenate([r["out"] for r in res.results], axis=0)
    return out.reshape(1, NTOK, D)


if __name__ == "__main__":
    import reference
    inp = reference.setup_inputs(0)
    got = kernel(**{k: np.asarray(v) for k, v in inp.items()})
    ref = np.asarray(reference.reference(**inp))
    err = np.mean((got - ref) ** 2) / np.mean(ref ** 2)
    print("resvar:", err, "maxabs:", np.max(np.abs(got - ref)))


# version 22
# speedup vs baseline: 1.0338x; 1.0338x over previous
import numpy as np
import ml_dtypes
from collections import defaultdict
from contextlib import ExitStack

import concourse.bass as bass
import concourse.mybir as mybir
from concourse.bass_utils import run_bass_kernel_spmd

F32 = mybir.dt.float32
BF16 = mybir.dt.bfloat16
I32 = mybir.dt.int32
AF = mybir.ActivationFunctionType
ALU = mybir.AluOpType
AX = mybir.AxisListType.X

NCORES = 8
T = 1024
D = 2048
H = 16
DH = 128
WIN = 2
CSC = float(DH) ** -0.5
NEG = -3.0e38
NTOK = 8192

DEBUG = False


class Prog:
    def __init__(self):
        self.steps = []
        self.cnt = defaultdict(int)
        self.tags = {}

    def step(self, eng, fn, waits=(), tag=None, inc=None, pad=False, dup=False):
        for t in waits:
            assert t in self.tags, f"wait on undefined tag {t}"
        if inc is not None:
            self.cnt[inc[0]] += inc[1]
            if tag is not None:
                self.tags[tag] = (inc[0], self.cnt[inc[0]])
        else:
            assert tag is None
        self.steps.append((eng, fn, tuple(self.tags[t] for t in waits), inc, pad, dup))

    def emit(self, eng_name, eng, sems, padfn=None):
        for (e, fn, waits, inc, pad, dup) in self.steps:
            if e != eng_name:
                continue
            byname = {}
            for (sname, val) in waits:
                byname[sname] = max(byname.get(sname, 0), val)
            for sname, val in byname.items():
                eng.wait_ge(sems[sname], val)
            if fn is None:
                continue
            inst = fn(eng)
            if dup:
                inst = fn(eng)
            if pad:
                assert padfn is not None
                inst = padfn(eng)
            if inc is not None:
                assert inst is not None
                inst.then_inc(sems[inc[0]], inc[1])


def ecopy(en, out, in_):
    if hasattr(en, "tensor_copy"):
        return en.tensor_copy(out=out, in_=in_)
    return en.copy(out=out, in_=in_)


def build_nc(debug=False, debug2=False):
    nc = bass.Bass("TRN2", target_bir_lowering=False)

    x_t = nc.dram_tensor("x_t", [D, T], F32, kind="ExternalInput").ap()
    x_n = nc.dram_tensor("x_n", [T, D], F32, kind="ExternalInput").ap()
    wqkvT = nc.dram_tensor("wqkvT", [D, 3 * D], F32, kind="ExternalInput").ap()
    woutT = nc.dram_tensor("woutT", [D, D], F32, kind="ExternalInput").ap()
    wffnT = nc.dram_tensor("wffnT", [D, D], F32, kind="ExternalInput").ap()
    nw_f = nc.dram_tensor("nw_f", [128, 16], F32, kind="ExternalInput").ap()
    gf_f = nc.dram_tensor("gf_f", [128, 16], F32, kind="ExternalInput").ap()
    pmk_h = nc.dram_tensor("pmk_h", [128, 256], BF16, kind="ExternalInput").ap()
    pmv_h = nc.dram_tensor("pmv_h", [16, 2048], BF16, kind="ExternalInput").ap()
    cos_h = nc.dram_tensor("cos_h", [128, T], BF16, kind="ExternalInput").ap()
    sin_h = nc.dram_tensor("sin_h", [128, T], BF16, kind="ExternalInput").ap()
    ptm_h = nc.dram_tensor("ptm_h", [128, 128], BF16, kind="ExternalInput").ap()
    idb_h = nc.dram_tensor("idb_h", [128, 128], BF16, kind="ExternalInput").ap()
    idf_h = nc.dram_tensor("idf_h", [128, 128], F32, kind="ExternalInput").ap()
    onb_h = nc.dram_tensor("onb_h", [128, 1], BF16, kind="ExternalInput").ap()
    onf_h = nc.dram_tensor("onf_h", [1, 128], F32, kind="ExternalInput").ap()
    onc_h = nc.dram_tensor("onc_h", [128, 1], F32, kind="ExternalInput").ap()
    mfirst_h = nc.dram_tensor("mfirst_h", [128, 128], BF16, kind="ExternalInput").ap()
    mdiag_h = nc.dram_tensor("mdiag_h", [128, 128], BF16, kind="ExternalInput").ap()
    out_d = nc.dram_tensor("out", [T, D], F32, kind="ExternalOutput").ap()
    cc_in = nc.dram_tensor("cc_in", [128, 128], F32, kind="Internal").ap()
    cc_out = nc.dram_tensor("cc_out", [128, 128], F32, kind="Internal").ap()

    dbg = {}
    if debug2:
        dbg["ctx"] = nc.dram_tensor("dbg_ctx", [128, H * T], BF16, kind="ExternalOutput").ap()
        dbg["xqb"] = nc.dram_tensor("dbg_xqb", [128, 16 * T], BF16, kind="ExternalOutput").ap()
        dbg["sc"] = nc.dram_tensor("dbg_sc", [128, 16], F32, kind="ExternalOutput").ap()
        dbg["pn"] = nc.dram_tensor("dbg_pn", [2, 128, 544], BF16, kind="ExternalOutput").ap()
        dbg["z"] = nc.dram_tensor("dbg_z", [128, 128], F32, kind="ExternalOutput").ap()
        dbg["v"] = nc.dram_tensor("dbg_v", [128, 8 * D], BF16, kind="ExternalOutput").ap()
        dbg["st"] = nc.dram_tensor("dbg_st", [128, 2048], F32, kind="ExternalOutput").ap()
        dbg["bc"] = nc.dram_tensor("dbg_bc", [128, 2048], F32, kind="ExternalOutput").ap()
        dbg["st0"] = nc.dram_tensor("dbg_st0", [1, 2048], F32, kind="ExternalOutput").ap()
        dbg["gfa"] = nc.dram_tensor("dbg_gfa", [128, 16], F32, kind="ExternalOutput").ap()
        dbg["mb"] = nc.dram_tensor("dbg_mb", [128, 1024], F32, kind="ExternalOutput").ap()
        dbg["rb"] = nc.dram_tensor("dbg_rb", [128, 1024], F32, kind="ExternalOutput").ap()
    if debug:
        dbg["xab"] = nc.dram_tensor("dbg_xab", [128, 16 * T], BF16, kind="ExternalOutput").ap()
        dbg["qh"] = nc.dram_tensor("dbg_qh", [H, 128, T], BF16, kind="ExternalOutput").ap()
        dbg["kh"] = nc.dram_tensor("dbg_kh", [H, 128, T], BF16, kind="ExternalOutput").ap()
        dbg["v"] = nc.dram_tensor("dbg_v", [128, 8 * D], BF16, kind="ExternalOutput").ap()
        dbg["st"] = nc.dram_tensor("dbg_st", [128, 2048], F32, kind="ExternalOutput").ap()
        dbg["bc"] = nc.dram_tensor("dbg_bc", [128, 2048], F32, kind="ExternalOutput").ap()
        dbg["st0"] = nc.dram_tensor("dbg_st0", [1, 2048], F32, kind="ExternalOutput").ap()
        dbg["gfa"] = nc.dram_tensor("dbg_gfa", [128, 16], F32, kind="ExternalOutput").ap()
        dbg["mb"] = nc.dram_tensor("dbg_mb", [128, 1024], F32, kind="ExternalOutput").ap()
        dbg["rb"] = nc.dram_tensor("dbg_rb", [128, 1024], F32, kind="ExternalOutput").ap()
        dbg["p"] = nc.dram_tensor("dbg_p", [H, WIN, 4, 128, 544], BF16, kind="ExternalOutput").ap()
        dbg["ctx"] = nc.dram_tensor("dbg_ctx", [128, H * T], BF16, kind="ExternalOutput").ap()
        dbg["xqb"] = nc.dram_tensor("dbg_xqb", [128, 16 * T], BF16, kind="ExternalOutput").ap()
        dbg["sc"] = nc.dram_tensor("dbg_sc", [128, 16], F32, kind="ExternalOutput").ap()

    with ExitStack() as ctx:
        e = ctx.enter_context

        xab_t = e(nc.sbuf_tensor("xab", [128, 16 * T], BF16))
        vsb_t = e(nc.sbuf_tensor("vsb", [128, 8 * D], BF16))
        ctx_t = e(nc.sbuf_tensor("ctxb", [128, H * T], BF16))
        wst_t = e(nc.sbuf_tensor("wstage", [128, 8192], F32))
        wbf0_t = e(nc.sbuf_tensor("wbf0", [128, 4096], BF16))
        wbf1_t = e(nc.sbuf_tensor("wbf1", [128, 4096], BF16))
        mega_t = e(nc.sbuf_tensor("mega", [128, 10240], BF16))
        stat_t = e(nc.sbuf_tensor("statsl", [128, 2048], F32))
        smal_t = e(nc.sbuf_tensor("smal", [128, 128], F32))
        outs_t = e(nc.sbuf_tensor("outsb", [128, 1024], F32))
        xnst_t = e(nc.sbuf_tensor("xnst", [128, 512], F32))
        o1b_t = e(nc.sbuf_tensor("o1b", [128, 512], F32))
        mfir_t = e(nc.sbuf_tensor("mfirst", [128, 128], BF16))
        mdia_t = e(nc.sbuf_tensor("mdiag", [128, 128], BF16))
        nw_t = e(nc.sbuf_tensor("nwsb", [128, 16], F32))
        gf_t = e(nc.sbuf_tensor("gfsb", [128, 16], F32))
        gfa_t = e(nc.sbuf_tensor("gfab", [128, 16], F32))
        pmk_t = e(nc.sbuf_tensor("pmk", [128, 256], BF16))
        pmv_t = e(nc.sbuf_tensor("pmv", [16, 2048], BF16))
        ptm_t = e(nc.sbuf_tensor("ptm", [128, 128], BF16))
        idb_t = e(nc.sbuf_tensor("idb", [128, 128], BF16))
        idf_t = e(nc.sbuf_tensor("idf", [128, 128], F32))
        onb_t = e(nc.sbuf_tensor("onb", [128, 1], BF16))
        onf_t = e(nc.sbuf_tensor("onf", [1, 128], F32))
        onc_t = e(nc.sbuf_tensor("onc", [128, 1], F32))
        ccst_t = e(nc.sbuf_tensor("ccst", [128, 128], F32))
        pmpt_t = [e(nc.sbuf_tensor(f"pmpt{w}", [16, 512], BF16)) for w in range(2)]
        ptj_t = [[e(nc.sbuf_tensor(f"ptj{w}_{j}", [128, 512 - 128 * j], BF16)) for j in range(4)]
                 for w in range(2)]
        qtmp_t = e(nc.sbuf_tensor("qtmp", [128, 1024], BF16))
        qtmk_t = e(nc.sbuf_tensor("qtmk", [128, 1024], BF16))
        rots_t = e(nc.sbuf_tensor("rotsb", [128, 1024], BF16))
        rotk_t = e(nc.sbuf_tensor("rotkb", [128, 1024], BF16))
        t1b_t = e(nc.sbuf_tensor("t1b", [128, 1024], BF16))

        xab = xab_t[:, :]
        vsb = vsb_t[:, :]
        ctxb = ctx_t[:, :]
        wst = wst_t[:, :]
        mega = mega_t[:, :]
        outsb = outs_t[:, :]
        xnst = xnst_t[:, :]
        o1b = o1b_t[:, :]
        qtmp_b = [qtmp_t[:, :], qtmk_t[:, :]]
        rots_b = [rots_t[:, :], rotk_t[:, :]]
        t1b = t1b_t[:, :]
        wbf = [wbf0_t[:, :], wbf1_t[:, :]]
        ptj = [[p[:, :] for p in row] for row in ptj_t]
        pmpt = [p[:, :] for p in pmpt_t]

        qh = [mega[:, 0:1024], mega[:, 1024:2048]]
        kh = [mega[:, 2048:3072], mega[:, 3072:4096]]
        cosb = mega[:, 4096:5120]
        sinb = mega[:, 5120:6144]
        p_sb = [mega[:, 6144:6688], mega[:, 6688:7232]]
        p_n = [mega[:, 7232:7776], mega[:, 7776:8320]]
        wfb = [mega[:, 0:4096], mega[:, 4096:8192]]
        wob = [wbf[0], wbf[1]]

        xsc = [ctxb[:, 2048 * s:2048 * (s + 1)].bitcast(F32) for s in range(4)]
        tsc = ctxb[:, 8192:10240].bitcast(F32)
        xhsc = ctxb[:, 10240:12288].bitcast(F32)
        xbsc = ctxb[:, 12288:13312]
        sqsc = ctxb[:, 13312:14336]
        mubc = wbf[0][:, 0:2048].bitcast(F32)
        rhobc = wbf[1][:, 0:2048].bitcast(F32)
        xsc3 = [xab[:, 2048 * s:2048 * (s + 1)].bitcast(F32) for s in range(2)]
        tsc3 = xab[:, 4096:6144].bitcast(F32)
        xhsc3 = xab[:, 6144:8192].bitcast(F32)
        mubc2 = xab[:, 8192:10240].bitcast(F32)
        rhobc2 = xab[:, 10240:12288].bitcast(F32)
        xgsc = xab[:, 12288:14336].bitcast(F32)
        xisc = xab[:, 14336:16384].bitcast(I32)

        def sl(row, a, b):
            return stat_t[row:row + 1, a:b]
        mu_r = lambda tt: sl(0, 512 * tt, 512 * (tt + 1))
        rho_r = lambda tt: sl(0, 1024 + 512 * tt, 1024 + 512 * (tt + 1))
        aux1 = lambda tt: outs_t[0:1, 512 * tt:512 * (tt + 1)]
        aux2 = lambda tt: (xnst_t if tt == 0 else o1b_t)[0:1, 0:512]

        S_ = lambda i: smal_t[:, i:i + 1]
        gmax = S_(0)
        am_c, am2_c = S_(1), S_(2)
        m_c = [S_(3), S_(4)]
        m2_c = [S_(5), S_(6)]
        negm_c = [S_(7), S_(8)]
        za = [S_(9), S_(10)]
        zb = [S_(11), S_(12)]
        zs = [S_(13), S_(38)]
        rz = [S_(14), S_(39)]
        swacc = [S_(15 + i) for i in range(8)]
        swacc_b = S_(23)
        swsum, sw_s, rw_bc, sc_bc, s127, gamc, tmpa, tmpb, junk = (
            S_(24), S_(25), S_(26), S_(27), S_(28), S_(29), S_(30), S_(31), S_(32))
        eps5_c, eps6_c = S_(33), S_(34)
        junkv, junks, junkg = S_(35), S_(36), S_(37)
        amall = smal_t[:, 40:56]

        pb = [e(nc.psum_tensor(f"pb{i}", [128, 512], F32)) for i in range(6)]
        ptp = [e(nc.psum_tensor(f"ptp{i}", [128, 128], BF16)) for i in range(2)]
        sa = [pb[2], pb[3]]
        ctxps = [pb[4], pb[4]]
        rotps = pb[5]

        sems = {}
        for sname in ["Y", "G", "Tm", "V", "Sc"]:
            sems[sname] = e(nc.semaphore(f"sem{sname}"))
        block = e(nc.Block())

        P = Prog()

        def joiner(name, waits):
            P.step("V", lambda v: v.memset(junk, 0.0), waits=waits, tag=name, inc=("V", 1))

        def cdma(dst, src):
            return lambda g: g.dma_start(out=dst, in_=src)
        consts = [
            (nw_t[:, :], nw_f), (gf_t[:, :], gf_f), (pmk_t[:, :], pmk_h), (pmv_t[:, :], pmv_h),
            (cosb, cos_h), (sinb, sin_h), (ptm_t[:, :], ptm_h), (idb_t[:, :], idb_h),
            (idf_t[:, :], idf_h), (onb_t[:, :], onb_h), (onf_t[:, :], onf_h), (onc_t[:, :], onc_h),
            (mfir_t[:, :], mfirst_h), (mdia_t[:, :], mdiag_h),
        ]
        for i, (dst, src) in enumerate(consts):
            P.step("G", cdma(dst, src), tag=f"c{i}", inc=("G", 16))
        P.step("V", lambda v: v.memset(smal_t[:, :], 0.0), waits=[], tag="sm0", inc=("V", 1), pad=True)
        P.step("Sc", lambda s: s.activation(out=gfa_t[:, :], in_=gf_t[:, :], func=AF.Abs),
               waits=["c1"], tag="gfab", inc=("Sc", 1), pad=True)
        def epsf(v):
            v.memset(eps5_c, 1e-5)
            return v.memset(eps6_c, 1e-6)
        P.step("V", epsf, waits=[], tag="eps", inc=("V", 1))

        for cc in range(16):
            w = [f"amm{cc-4}"] if cc >= 4 else []
            P.step("Y", cdma(xsc[cc % 4][:, :], x_t[128 * cc:128 * (cc + 1), :]),
                   waits=w, tag=f"axl{cc}", inc=("Y", 16))
            P.step("V", (lambda cc: lambda v: v.tensor_copy(out=xbsc, in_=xsc[cc % 4][:, :]))(cc),
                   waits=[f"axl{cc}"] + ([f"amm{cc-1}"] if cc > 0 else []),
                   tag=f"axb{cc}", inc=("V", 1))
            P.step("Sc", (lambda cc: lambda s: s.activation(out=sqsc, in_=xsc[cc % 4][:, :], func=AF.Square))(cc),
                   waits=[f"axl{cc}"] + ([f"amm{cc-1}"] if cc > 0 else []),
                   tag=f"asq{cc}", inc=("Sc", 1))

            def statmm(cc):
                def f(t):
                    first, last = (cc == 0), (cc == 15)
                    t.matmul(out=pb[0][0:1, :], lhsT=onb_t[:, :], rhs=xbsc[:, 0:512], start=first, stop=last)
                    t.matmul(out=pb[1][0:1, :], lhsT=onb_t[:, :], rhs=xbsc[:, 512:1024], start=first, stop=last)
                    t.matmul(out=pb[2][0:1, :], lhsT=onb_t[:, :], rhs=sqsc[:, 0:512], start=first, stop=last)
                    return t.matmul(out=pb[3][0:1, :], lhsT=onb_t[:, :], rhs=sqsc[:, 512:1024], start=first, stop=last)
                return f
            P.step("Tm", statmm(cc), waits=[f"axb{cc}", f"asq{cc}", "c9"],
                   tag=f"amm{cc}", inc=("Tm", 1), pad=(cc == 15))

        for tt in range(2):
            P.step("V", (lambda tt: lambda v: v.tensor_scalar(
                out=mu_r(tt), in0=pb[tt][0:1, :], scalar1=1.0 / 2048.0, scalar2=None, op0=ALU.mult))(tt),
                waits=["amm15"], tag=f"amu{tt}", inc=("V", 1), pad=True)
            P.step("V", (lambda tt: lambda v: v.tensor_scalar(
                out=aux1(tt), in0=pb[2 + tt][0:1, :], scalar1=1.0 / 2048.0, scalar2=None, op0=ALU.mult))(tt),
                waits=["amm15"], tag=f"am2{tt}", inc=("V", 1), pad=True)
            P.step("Sc", (lambda tt: lambda s: s.activation(out=aux2(tt), in_=mu_r(tt), func=AF.Square))(tt),
                   waits=[f"amu{tt}"], tag=f"amsq{tt}", inc=("Sc", 1), pad=True)
            P.step("V", (lambda tt: lambda v: v.tensor_tensor(
                out=aux1(tt), in0=aux1(tt), in1=aux2(tt), op=ALU.subtract))(tt),
                waits=[f"am2{tt}", f"amsq{tt}"], tag=f"avar{tt}", inc=("V", 1), pad=True)
            P.step("Sc", (lambda tt: lambda s: s.activation(out=aux2(tt), in_=aux1(tt), func=AF.Ln, bias=eps5_c[0:1, :]))(tt),
                   waits=[f"avar{tt}", "eps"], tag=f"alnv{tt}", inc=("Sc", 1), pad=True)
            P.step("Sc", (lambda tt: lambda s: s.activation(out=rho_r(tt), in_=aux1(tt), func=AF.Ln))(tt),
                   waits=[f"avar{tt}"], tag=f"alnt{tt}", inc=("Sc", 1), pad=True)
            P.step("Sc", (lambda tt: lambda s: s.activation(out=aux1(tt), in_=aux2(tt), func=AF.Exp, scale=-0.5))(tt),
                   waits=[f"alnv{tt}", f"alnt{tt}"], tag=f"ar1{tt}", inc=("Sc", 1), pad=True)
            P.step("V", (lambda tt: lambda v: v.tensor_tensor(
                out=rho_r(tt), in0=rho_r(tt), in1=aux2(tt), op=ALU.subtract))(tt),
                waits=[f"ar1{tt}"], tag=f"alnd{tt}", inc=("V", 1), pad=True)
            P.step("Sc", (lambda tt: lambda s: s.activation(out=rho_r(tt), in_=rho_r(tt), func=AF.Exp))(tt),
                   waits=[f"alnd{tt}"], tag=f"at2{tt}", inc=("Sc", 1), pad=True)
            P.step("Sc", (lambda tt: lambda s: s.activation(out=rho_r(tt), in_=rho_r(tt), func=AF.Ln, bias=eps6_c[0:1, :]))(tt),
                   waits=[f"at2{tt}", "eps"], tag=f"alt2{tt}", inc=("Sc", 1), pad=True)
            P.step("Sc", (lambda tt: lambda s: s.activation(out=rho_r(tt), in_=rho_r(tt), func=AF.Exp, scale=-0.5))(tt),
                   waits=[f"alt2{tt}"], tag=f"ar2{tt}", inc=("Sc", 1), pad=True)
            P.step("V", (lambda tt: lambda v: v.tensor_tensor(
                out=rho_r(tt), in0=rho_r(tt), in1=aux1(tt), op=ALU.mult))(tt),
                waits=[f"ar2{tt}"], tag=f"arho{tt}", inc=("V", 1), pad=True)

        def bcmm(dst_bank, src, tt):
            def f(t):
                return t.matmul(out=pb[dst_bank][:, :], lhsT=onf_t[:, :], rhs=src(tt), start=True, stop=True)
            return f
        for tt in range(2):
            P.step("Tm", bcmm(tt, mu_r, tt), waits=[f"arho{tt}", f"amu{tt}", "c10"],
                   tag=f"bmu{tt}", inc=("Tm", 1), pad=True)
            P.step("Tm", bcmm(2 + tt, rho_r, tt), waits=[f"arho{tt}"], tag=f"brho{tt}", inc=("Tm", 1), pad=True)
        for tt in range(2):
            P.step("V", (lambda tt: lambda v: v.tensor_copy(out=mubc[:, 512 * tt:512 * (tt + 1)], in_=pb[tt][:, :]))(tt),
                   waits=[f"bmu{tt}"], tag=f"emu{tt}", inc=("V", 1))
            P.step("V", (lambda tt: lambda v: v.tensor_copy(out=rhobc[:, 512 * tt:512 * (tt + 1)], in_=pb[2 + tt][:, :]))(tt),
                   waits=[f"brho{tt}"], tag=f"erho{tt}", inc=("V", 1))

        if debug2:
            P.step("G", lambda g: g.dma_start(out=dbg["st0"], in_=stat_t[0:1, :]),
                   waits=["arho0", "arho1", "amu0", "amu1"], tag="dst0", inc=("G", 16))
        if debug2:
            P.step("G", lambda g: g.dma_start(out=dbg["gfa"], in_=gfa_t[:, :]),
                   waits=["gfab"], tag="dgfa", inc=("G", 16))
            P.step("G", lambda g: g.dma_start(out=dbg["mb"], in_=mubc),
                   waits=["emu0", "emu1"], tag="dmb", inc=("G", 16))
            P.step("G", lambda g: g.dma_start(out=dbg["rb"], in_=rhobc),
                   waits=["erho0", "erho1"], tag="drb", inc=("G", 16))
        for cc in range(16):
            w = [f"asub{cc-4}"] if cc >= 4 else ["amm15"]
            P.step("Y", cdma(xsc[cc % 4][:, :], x_t[128 * cc:128 * (cc + 1), :]),
                   waits=w, tag=f"axr{cc}", inc=("Y", 16))
            P.step("V", (lambda cc: lambda v: v.tensor_tensor(
                out=tsc, in0=xsc[cc % 4][:, :], in1=mubc, op=ALU.subtract))(cc),
                waits=[f"axr{cc}", "emu0", "emu1"] + ([f"axh{cc-1}"] if cc > 0 else []),
                tag=f"asub{cc}", inc=("V", 1))
            P.step("V", (lambda cc: lambda v: v.tensor_tensor(
                out=xhsc, in0=tsc, in1=rhobc, op=ALU.mult))(cc),
                waits=[f"asub{cc}", "erho0", "erho1"], tag=f"axh{cc}", inc=("V", 1))
            P.step("V", (lambda cc: lambda v: v.tensor_scalar(
                out=xab[:, T * cc:T * (cc + 1)], in0=xhsc,
                scalar1=nw_t[:, cc:cc + 1], scalar2=None, op0=ALU.mult))(cc),
                waits=[f"axh{cc}", "c0"], tag=f"axab{cc}", inc=("V", 1))
            P.step("V", (lambda cc: lambda v: v.tensor_reduce(
                out=amall[:, cc:cc + 1], in_=xhsc, axis=AX, op=ALU.max, apply_absolute_value=True))(cc),
                waits=[f"axh{cc}"], tag=f"aam{cc}", inc=("V", 1), pad=True)
        P.step("V", lambda v: v.tensor_tensor(out=amall, in0=amall, in1=gfa_t[:, :], op=ALU.mult),
               waits=[f"aam{cc}" for cc in range(16)] + ["gfab", "axab15"], tag="agm_m", inc=("V", 1), pad=True)
        P.step("V", lambda v: v.tensor_reduce(out=gmax, in_=amall, axis=AX, op=ALU.max),
               waits=["agm_m"], tag="agm", inc=("V", 1), pad=True)
        APPLY_DONE = "agm"
        if debug:
            P.step("G", lambda g: g.dma_start(out=dbg["xab"], in_=xab),
                   waits=["axab15", "axab14"], tag="dbgxab", inc=("G", 16))

        P.step("V", lambda v: v.memset(ccst_t[:, :], NEG), waits=[], tag="ccm", inc=("V", 1))
        P.step("V", lambda v: v.tensor_copy(out=ccst_t[:, 0:1], in_=gmax),
               waits=["ccm", APPLY_DONE], tag="ccc", inc=("V", 1), pad=True)

        wsh = [wst[:, 0:4096], wst[:, 4096:8192]]
        nvg = 8
        for g in range(nvg):
            w = [f"vcv{g-2}"] if g >= 2 else []
            src = wqkvT[:, 2 * D + 256 * g:2 * D + 256 * (g + 1)].rearrange("(c p) m -> p c m", p=128)
            dst3 = wsh[g % 2].rearrange("p (c m) -> p c m", m=256)
            P.step("Y", (lambda dst3, src: lambda y: y.dma_start(out=dst3, in_=src))(dst3, src),
                   waits=w, tag=f"vld{g}", inc=("Y", 16))
            wguard = [APPLY_DONE] if g < 2 else [f"vmm{g-2}_7"]
            cvt = []
            for i in range(16):
                engx = ["G", "Sc", "G"][i % 3]
                P.step(engx, (lambda g, i: lambda en: ecopy(en,
                    wbf[g % 2][:, 256 * i:256 * (i + 1)], wsh[g % 2][:, 256 * i:256 * (i + 1)]))(g, i),
                    waits=[f"vld{g}"] + wguard, tag=f"vcv{g}_{i}", inc=(engx, 1))
                cvt.append(f"vcv{g}_{i}")
            joiner(f"vcv{g}", cvt)
            for tc in range(8):
                bank = tc % 4

                def vmm(g, tc, bank):
                    def f(t):
                        r = None
                        for cc in range(16):
                            r = t.matmul(out=pb[bank][:, 0:256],
                                         lhsT=xab[:, T * cc + 128 * tc:T * cc + 128 * (tc + 1)],
                                         rhs=wbf[g % 2][:, 256 * cc:256 * (cc + 1)],
                                         start=(cc == 0), stop=(cc == 15))
                        return r
                    return f
                w = [f"vcv{g}", APPLY_DONE]
                if g > 0 or tc >= 4:
                    pg, ptc = (g, tc - 4) if tc >= 4 else (g - 1, tc + 4)
                    w.append(f"vev{pg}_{ptc}")
                P.step("Tm", vmm(g, tc, bank), waits=w, tag=f"vmm{g}_{tc}", inc=("Tm", 1))
                P.step("V", (lambda g, tc, bank: lambda v: v.tensor_copy(
                    out=vsb[:, D * tc + 256 * g:D * tc + 256 * (g + 1)], in_=pb[bank][:, 0:256]))(g, tc, bank),
                    waits=[f"vmm{g}_{tc}"], tag=f"vev{g}_{tc}", inc=("V", 1))
        if debug:
            P.step("G", lambda g: g.dma_start(out=dbg["v"], in_=vsb),
                   waits=[f"vev{nvg-1}_{tc}" for tc in range(8)], tag="dbgv", inc=("G", 16))

        P.step("V", lambda v: v.memset(sa[0][:, :], -1.0e30),
               waits=["erho0", "erho1", f"vev{nvg-1}_6"], tag="sainit0", inc=("V", 1))
        P.step("V", lambda v: v.memset(sa[1][:, :], -1.0e30),
               waits=["erho0", "erho1", f"vev{nvg-1}_7"], tag="sainit1", inc=("V", 1))
        sa_user = {0: "sainit0", 1: "sainit1"}
        ptp_user = {0: None, 1: None}
        ctx_user = {0: None, 1: None}
        negm_user = {0: None, 1: None}
        qkbank_ev = {0: f"vev{nvg-1}_6", 1: f"vev{nvg-1}_7"}
        rot_user = {0: None}
        av_prev = [None]
        rope_reader = {}

        def emit_qld(hh):
            hhp = hh % 2
            wq_src = wqkvT[:, 128 * hh:128 * (hh + 1)].rearrange("(c p) m -> p c m", p=128)
            wk_src = wqkvT[:, D + 128 * hh:D + 128 * (hh + 1)].rearrange("(c p) m -> p c m", p=128)
            dsth = wsh[hhp].rearrange("p (c m) -> p c m", m=256)
            wwait = [f"vcv{6+hhp}"] if hh < 2 else [f"qcv{hh-2}"]
            if 2 <= hh <= 9:
                wwait.append(f"swa{hh-2}b")
            P.step("Y", (lambda dsth, wq_src: lambda y: y.dma_start(out=dsth[:, :, 0:128], in_=wq_src))(dsth, wq_src),
                   waits=wwait, tag=f"qld{hh}a", inc=("Y", 16))
            P.step("Y", (lambda dsth, wk_src: lambda y: y.dma_start(out=dsth[:, :, 128:256], in_=wk_src))(dsth, wk_src),
                   waits=[], tag=f"qld{hh}", inc=("Y", 16))

        for h in range(16):
            hp = h % 2
            ah = h - 1

            def attn_A(ah, pos):
                w_ = pos % 2
                qc = pos // 2
                par = w_
                ntok = 128 * (qc + 1)
                ahp = ah % 2
                qslice = qh[ahp][:, 512 * w_ + 128 * qc:512 * w_ + 128 * (qc + 1)]
                kslice = kh[ahp][:, 512 * w_:512 * w_ + ntok]
                wts = [f"ropeq{ah}", f"ropek{ah}", "c2", "c12", "c13", "c7", sa_user[par]]
                msk = mfir_t if qc == 0 else mdia_t

                def qkmm(qslice, kslice, par, ntok, ah, qc, msk):
                    def f(t):
                        t.matmul(out=sa[par][:, 0:ntok], lhsT=qslice, rhs=kslice,
                                 start=True, stop=True)
                        mm = t.matmul(out=sa[par][:, 128 * qc:ntok], lhsT=idb_t[:, :],
                                      rhs=msk[:, :], start=False, stop=True)
                        if qc < 3:
                            mm = t.matmul(out=sa[par][:, ntok:ntok + 16], lhsT=qslice,
                                          rhs=pmk_t[:, 16 * ah:16 * (ah + 1)], start=True, stop=True)
                        return mm
                    return f
                P.step("Tm", qkmm(qslice, kslice, par, ntok, ah, qc, msk),
                       waits=wts, tag=f"qk{ah}_{w_}_{qc}", inc=("Tm", 1))
                area = ntok + 16 if qc < 3 else 512
                P.step("V", (lambda par, area: lambda v: v.tensor_reduce(
                    out=negm_c[par], in_=sa[par][:, 0:area], axis=AX, op=ALU.max, negate=True))(par, area),
                    waits=[f"qk{ah}_{w_}_{qc}"] + ([negm_user[par]] if negm_user[par] else []),
                    tag=f"ngm{ah}_{w_}_{qc}", inc=("V", 1), pad=True)

            def attn_PM3(ah, pos):
                w_ = pos % 2
                par = w_
                pm_bank, pm_c0 = 1 - par, 480
                ahp = ah % 2
                qc = pos // 2
                qslice = qh[ahp][:, 512 * w_ + 128 * qc:512 * w_ + 128 * (qc + 1)]
                P.step("Tm", (lambda qslice, pm_bank, pm_c0, ah: lambda t: t.matmul(
                    out=sa[pm_bank][:, pm_c0:pm_c0 + 16], lhsT=qslice,
                    rhs=pmk_t[:, 16 * ah:16 * (ah + 1)], start=True, stop=True))(qslice, pm_bank, pm_c0, ah),
                    waits=[f"qk{ah}_{w_}_3", sa_user[1 - par]], tag=f"pm3{ah}_{w_}", inc=("Tm", 1))

            def attn_B(ah, pos):
                w_ = pos % 2
                qc = pos // 2
                par = w_
                ntok = 128 * (qc + 1)
                pm_bank, pm_c0 = (par, ntok) if qc < 3 else (1 - par, 480)
                expw = [f"ngm{ah}_{w_}_{qc}"]
                if qc == 3:
                    expw.append(f"pm3{ah}_{w_}")
                prev_pn = (f"pn{ah}_{w_}_{qc-1}" if qc >= 1 else
                           (f"pn{ah-1}_{w_}_3" if ah > 0 else None))
                if prev_pn:
                    expw.append(prev_pn)

                def expf(par, ntok, pm_bank, pm_c0):
                    def f(s):
                        s.activation(out=p_sb[par][:, 16:16 + ntok], in_=sa[par][:, 0:ntok],
                                     func=AF.Exp, bias=negm_c[par], scale=1.0, accum_out=za[par])
                        return s.activation(out=p_sb[par][:, 0:16], in_=sa[pm_bank][:, pm_c0:pm_c0 + 16],
                                            func=AF.Exp, bias=negm_c[par], scale=1.0, accum_out=zb[par])
                    return f
                P.step("Sc", expf(par, ntok, pm_bank, pm_c0),
                       waits=expw, tag=f"exp{ah}_{w_}_{qc}", inc=("Sc", 1), pad=True)
                sa_user[par] = f"exp{ah}_{w_}_{qc}"
                if qc == 3:
                    sa_user[1 - par] = f"exp{ah}_{w_}_{qc}"
                negm_user[par] = f"exp{ah}_{w_}_{qc}"

            def attn_C(ah, pos):
                w_ = pos % 2
                qc = pos // 2
                par = w_
                ntok = 128 * (qc + 1)
                P.step("V", (lambda par: lambda v: v.tensor_tensor(
                    out=zs[par], in0=za[par], in1=zb[par], op=ALU.add))(par),
                    waits=[f"exp{ah}_{w_}_{qc}"], tag=f"zz{ah}_{w_}_{qc}", inc=("V", 1), pad=True, dup=True)
                P.step("V", (lambda par: lambda v: v.reciprocal(out=rz[par], in_=zs[par]))(par),
                       waits=[f"zz{ah}_{w_}_{qc}"], tag=f"rzt{ah}_{w_}_{qc}", inc=("V", 1), pad=True, dup=True)
                pnw = [f"rzt{ah}_{w_}_{qc}"]
                prev_tp = (f"tp{ah}_{w_}_{qc-1}" if qc >= 1 else
                           (f"tp{ah-1}_{w_}_3" if ah > 0 else None))
                if prev_tp:
                    pnw.append(prev_tp)
                P.step("V", (lambda par, ntok: lambda v: v.tensor_scalar(
                    out=p_n[par][:, 0:16 + ntok], in0=p_sb[par][:, 0:16 + ntok],
                    scalar1=rz[par], scalar2=None, op0=ALU.mult))(par, ntok),
                    waits=pnw, tag=f"pn{ah}_{w_}_{qc}", inc=("V", 1))
                if debug:
                    P.step("G", (lambda ah, w_, qc, par: lambda g: g.dma_start(
                        out=dbg["p"][ah, w_, qc], in_=p_n[par]))(ah, w_, qc, par),
                        waits=[f"pn{ah}_{w_}_{qc}"], tag=f"dbgp{ah}_{w_}_{qc}", inc=("G", 16))
                ntp = qc + 2
                tpe_tags = []
                for jj in range(ntp):
                    tpar = jj % 2
                    tw = [f"pn{ah}_{w_}_{qc}", "c7"]
                    if ptp_user[tpar]:
                        tw.append(ptp_user[tpar])
                    if jj == 0:
                        srcp = p_n[par][:, 0:16]
                        dstp = ptp[tpar][0:16, :]
                        dste = pmpt[w_][:, 128 * qc:128 * (qc + 1)]
                    else:
                        j = jj - 1
                        srcp = p_n[par][:, 16 + 128 * j:16 + 128 * (j + 1)]
                        dstp = ptp[tpar][:, :]
                        dste = ptj[w_][j][:, 128 * (qc - j):128 * (qc - j + 1)]
                    P.step("Tm", (lambda srcp, dstp: lambda t: t.transpose(
                        out=dstp, in_=srcp, identity=idb_t[:, :]))(srcp, dstp),
                        waits=tw, tag=f"tpp{ah}_{w_}_{qc}_{jj}", inc=("Tm", 1))
                    ew = [f"tpp{ah}_{w_}_{qc}_{jj}"]
                    if av_prev[0]:
                        ew.append(av_prev[0])
                    if jj % 2 == 0:
                        P.step("Sc", (lambda dstp, dste: lambda s: s.copy(out=dste, in_=dstp))(dstp, dste),
                               waits=ew, tag=f"tpe{ah}_{w_}_{qc}_{jj}", inc=("Sc", 1))
                    else:
                        P.step("V", (lambda dstp, dste: lambda v: v.tensor_copy(out=dste, in_=dstp))(dstp, dste),
                               waits=ew, tag=f"tpe{ah}_{w_}_{qc}_{jj}", inc=("V", 1))
                    ptp_user[tpar] = f"tpe{ah}_{w_}_{qc}_{jj}"
                    tpe_tags.append(f"tpe{ah}_{w_}_{qc}_{jj}")
                joiner(f"tp{ah}_{w_}_{qc}", tpe_tags)

            def attn_AV(ah):
                for w_ in range(2):
                    avw = [f"tp{ah}_{w_}_3", "c3"]
                    avw += [f"vev{ah // 2}_{4 * w_ + j}" for j in range(4)]
                    if ctx_user[0]:
                        avw.append(ctx_user[0])

                    def avf(ah, w_):
                        def f(t):
                            t.matmul(out=ctxps[0][:, :], lhsT=pmv_t[:, 128 * ah:128 * (ah + 1)],
                                     rhs=pmpt[w_][:, :], start=True, stop=False)
                            r = None
                            for j in range(4):
                                tcj = 4 * w_ + j
                                nq = 512 - 128 * j
                                r = t.matmul(out=ctxps[0][:, 128 * j:512],
                                             lhsT=vsb[:, D * tcj + 128 * ah:D * tcj + 128 * (ah + 1)],
                                             rhs=ptj[w_][j][:, 0:nq],
                                             start=False, stop=(j == 3))
                            return r
                        return f
                    P.step("Tm", avf(ah, w_), waits=avw, tag=f"av{ah}_{w_}", inc=("Tm", 1))
                    av_prev[0] = f"av{ah}_{w_}"
                    P.step("V", (lambda ah, w_: lambda v: v.tensor_copy(
                        out=ctxb[:, T * ah + 512 * w_:T * ah + 512 * (w_ + 1)], in_=ctxps[0][:, :]))(ah, w_),
                        waits=[f"av{ah}_{w_}"], tag=f"cev{ah}_{w_}", inc=("V", 1))
                    ctx_user[0] = f"cev{ah}_{w_}"
                rope_reader[ah % 2] = f"pm3{ah}_1"

            def rnd(r, ah):
                if ah < 0:
                    return
                if r == 7:
                    attn_PM3(ah, 6)
                if r == 8:
                    attn_PM3(ah, 7)
                if 1 <= r <= 8:
                    attn_B(ah, r - 1)
                if r < 8:
                    attn_A(ah, r)
                if 2 <= r:
                    attn_C(ah, r - 2)

            def qmm_step(part, name, tt):
                bank = tt
                qt_b = qtmp_b[part]

                def qmm(hp, part, tt, bank):
                    def f(t):
                        r = None
                        wb = wbf[hp]
                        for cc in range(16):
                            r = t.matmul(out=pb[bank][:, :],
                                         lhsT=wb[:, 256 * cc + 128 * part:256 * cc + 128 * (part + 1)],
                                         rhs=xab[:, T * cc + 512 * tt:T * cc + 512 * (tt + 1)],
                                         start=(cc == 0), stop=(cc == 15))
                        return r
                    return f
                P.step("Tm", qmm(hp, part, tt, bank),
                       waits=[f"qcv{h}", qkbank_ev[bank]],
                       tag=f"qmm{name}{h}_{tt}", inc=("Tm", 1))
                evw = [f"qmm{name}{h}_{tt}"]
                if h > 0:
                    evw.append(f"qrot{name}{h-1}_{tt}")
                if part == 0:
                    P.step("V", (lambda tt, bank, qt_b: lambda v: v.tensor_scalar(
                        out=qt_b[:, 512 * tt:512 * (tt + 1)], in0=pb[bank][:, :],
                        scalar1=CSC, scalar2=None, op0=ALU.mult))(tt, bank, qt_b),
                        waits=evw, tag=f"qev{name}{h}_{tt}", inc=("V", 1))
                else:
                    P.step("V", (lambda tt, bank, qt_b: lambda v: v.tensor_copy(
                        out=qt_b[:, 512 * tt:512 * (tt + 1)], in_=pb[bank][:, :]))(tt, bank, qt_b),
                        waits=evw, tag=f"qev{name}{h}_{tt}", inc=("V", 1))
                qkbank_ev[bank] = f"qev{name}{h}_{tt}"

            def rot_step(part, name, tt):
                qt_b = qtmp_b[part]
                ro_b = rots_b[part]
                rw_ = [f"qev{name}{h}_{tt}", "c6"]
                if rot_user[0]:
                    rw_.append(rot_user[0])
                P.step("Tm", (lambda tt, qt_b: lambda t: t.matmul(
                    out=rotps[:, :], lhsT=ptm_t[:, :], rhs=qt_b[:, 512 * tt:512 * (tt + 1)],
                    start=True, stop=True))(tt, qt_b),
                    waits=rw_, tag=f"qrot{name}{h}_{tt}", inc=("Tm", 1))
                sw_ = [f"qrot{name}{h}_{tt}"]
                if h > 0:
                    sw_.append(f"rope{name}{h-1}")
                P.step("Sc", (lambda tt, ro_b: lambda s: s.copy(
                    out=ro_b[:, 512 * tt:512 * (tt + 1)], in_=rotps[:, :]))(tt, ro_b),
                    waits=sw_, tag=f"qrev{name}{h}_{tt}", inc=("Sc", 1))
                rot_user[0] = f"qrev{name}{h}_{tt}"

            def rt1_step(part, name):
                qt_b = qtmp_b[part]
                P.step("V", (lambda qt_b: lambda v: v.tensor_tensor(
                    out=t1b, in0=qt_b, in1=cosb, op=ALU.mult))(qt_b),
                    waits=[f"qev{name}{h}_1", "c4"], tag=f"rt1{name}{h}", inc=("V", 1))

            def rt2_rope_step(part, name):
                dstbuf = qh[hp] if part == 0 else kh[hp]
                ro_b = rots_b[part]
                P.step("V", (lambda ro_b: lambda v: v.tensor_tensor(
                    out=ro_b, in0=ro_b, in1=sinb, op=ALU.mult))(ro_b),
                    waits=[f"qrev{name}{h}_0", f"qrev{name}{h}_1", "c5"], tag=f"rt2{name}{h}", inc=("V", 1))
                ropew = [f"rt1{name}{h}", f"rt2{name}{h}"]
                if h >= 2 and rope_reader.get(hp):
                    ropew.append(rope_reader[hp])
                P.step("V", (lambda dstbuf, ro_b: lambda v: v.tensor_tensor(
                    out=dstbuf, in0=t1b, in1=ro_b, op=ALU.add))(dstbuf, ro_b),
                    waits=ropew, tag=f"rope{name}{h}", inc=("V", 1))

            rnd(0, ah)
            if h == 0:
                emit_qld(0)
                emit_qld(1)
            elif h + 1 <= 15:
                emit_qld(h + 1)
            cvw = [f"vmm{6+hp}_7"] if h < 2 else [f"qmmk{h-2}_1"]
            cvt = []
            for i in range(16):
                engx = "Sc" if i % 4 == 3 else "G"
                P.step(engx, (lambda hp, i: lambda en: ecopy(en,
                    wbf[hp][:, 256 * i:256 * (i + 1)], wsh[hp][:, 256 * i:256 * (i + 1)]))(hp, i),
                    waits=[f"qld{h}"] + cvw, tag=f"qcv{h}_{i}", inc=(engx, 1))
                cvt.append(f"qcv{h}_{i}")
            joiner(f"qcv{h}", cvt)
            if h <= 7:
                k = h
                half = k % 2
                srcw_ = wffnT[:, 256 * k:256 * (k + 1)].rearrange("(c p) m -> p c m", p=128)
                dstw_ = wsh[half].rearrange("p (c m) -> p c m", m=256)
                P.step("Y", (lambda dstw_, srcw_: lambda y: y.dma_start(out=dstw_, in_=srcw_))(dstw_, srcw_),
                       waits=[f"qcv{k}"] + ([f"swa{k-2}b"] if k >= 2 else []),
                       tag=f"swl{k}", inc=("Y", 16))
            rnd(1, ah)
            qmm_step(0, "q", 0)
            rnd(2, ah)
            qmm_step(0, "q", 1)
            rot_step(0, "q", 0)
            rnd(3, ah)
            qmm_step(1, "k", 0)
            rot_step(0, "q", 1)
            rt1_step(0, "q")
            rnd(4, ah)
            qmm_step(1, "k", 1)
            rot_step(1, "k", 0)
            rt2_rope_step(0, "q")
            rnd(5, ah)
            rot_step(1, "k", 1)
            rt1_step(1, "k")
            rnd(6, ah)
            rt2_rope_step(1, "k")
            rnd(7, ah)
            rnd(8, ah)
            rnd(9, ah)
            if ah >= 0:
                attn_AV(ah)
            if h <= 7:
                k = h
                half = k % 2
                P.step("Sc", (lambda half, k: lambda s: s.activation(
                    out=outsb.bitcast(BF16)[:, 0:2048], in_=wsh[half][:, 0:2048], func=AF.Abs,
                    accum_out=swacc[k]))(half, k),
                    waits=[f"swl{k}"] + ([f"swv{k-1}"] if k >= 1 else []), tag=f"swa{k}a", inc=("Sc", 1))
                P.step("Sc", (lambda half, k: lambda s: s.activation(
                    out=outsb.bitcast(BF16)[:, 0:2048], in_=wsh[half][:, 2048:4096], func=AF.Abs,
                    accum_out=swacc_b))(half, k),
                    waits=[f"swa{k}a"], tag=f"swa{k}b", inc=("Sc", 1), pad=True)
                P.step("V", (lambda k: lambda v: v.tensor_tensor(
                    out=swacc[k], in0=swacc[k], in1=swacc_b, op=ALU.add))(k),
                    waits=[f"swa{k}b"], tag=f"swv{k}", inc=("V", 1), pad=True)
            if debug:
                P.step("G", (lambda h: lambda g: g.dma_start(out=dbg["qh"][h], in_=qh[h % 2]))(h),
                       waits=[f"ropeq{h}"], tag=f"dbgq{h}", inc=("G", 16))
                P.step("G", (lambda h: lambda g: g.dma_start(out=dbg["kh"][h], in_=kh[h % 2]))(h),
                       waits=[f"ropek{h}"], tag=f"dbgk{h}", inc=("G", 16))
        for r in range(10):
            rnd(r, 15)
        attn_AV(15)

        if debug:
            P.step("G", lambda g: g.dma_start(out=dbg["ctx"], in_=ctxb),
                   waits=["cev15_1", "cev15_0"], tag="dbgctx", inc=("G", 16))

        P.step("G", lambda g: g.dma_start(out=cc_in, in_=ccst_t[:, :]), waits=["ccc"], tag="ccw", inc=("G", 16))
        P.step("G", lambda g: g.collective_compute(
            "AllReduce", ALU.max, replica_groups=[list(range(NCORES))],
            ins=[cc_in.opt()], outs=[cc_out.opt()]),
            waits=["ccw"], tag="ccr", inc=("G", 1))
        P.step("G", lambda g: g.dma_start(out=ccst_t[:, :], in_=cc_out),
               waits=["ccr"], tag="ccb", inc=("G", 16))

        SA0_FIN = sa_user[0]
        SA1_FIN = sa_user[1]
        QK0_FIN = qkbank_ev[0]
        QK1_FIN = qkbank_ev[1]

        P.step("V", lambda v: v.tensor_reduce(out=swsum, in_=smal_t[:, 15:23], axis=AX, op=ALU.add),
               waits=[f"swv{k}" for k in range(8)], tag="sws", inc=("V", 1), pad=True)
        P.step("Tm", lambda t: t.matmul(out=pb[0][0:1, 0:1], lhsT=swsum, rhs=onc_t[:, :],
                                        start=True, stop=True),
               waits=["sws", "c11", QK0_FIN], tag="swm", inc=("Tm", 1), pad=True)
        P.step("V", lambda v: v.tensor_scalar(out=sw_s[0:1, :], in0=pb[0][0:1, 0:1],
                                              scalar1=1.0 / (2048.0 * 2048.0), scalar2=None, op0=ALU.mult),
               waits=["swm"], tag="swv", inc=("V", 1), pad=True)
        P.step("Tm", lambda t: t.transpose(out=pb[1][0:1, 0:128], in_=ccst_t[:, 0:1], identity=idf_t[:, :]),
               waits=["ccb", "c8", QK1_FIN], tag="qst", inc=("Tm", 1), pad=True)
        P.step("V", lambda v: v.tensor_reduce(out=tmpa[0:1, :], in_=pb[1][0:1, 0:128], axis=AX, op=ALU.max),
               waits=["qst"], tag="qsr", inc=("V", 1), pad=True)
        P.step("V", lambda v: v.tensor_scalar(out=tmpa[0:1, :], in0=tmpa[0:1, :], scalar1=1e-5,
                                              scalar2=None, op0=ALU.max),
               waits=["qsr"], tag="qse", inc=("V", 1), pad=True)
        P.step("Tm", lambda t: t.matmul(out=pb[0][:, 0:1], lhsT=onf_t[:, :], rhs=tmpa[0:1, :],
                                        start=True, stop=True),
               waits=["qse", "swv", "c10"], tag="qsb", inc=("Tm", 1), pad=True)
        P.step("V", lambda v: v.tensor_copy(out=sc_bc, in_=pb[0][:, 0:1]),
               waits=["qsb"], tag="qsc", inc=("V", 1), pad=True)
        P.step("Tm", lambda t: t.matmul(out=pb[1][:, 0:1], lhsT=onf_t[:, :], rhs=sw_s[0:1, :], start=True, stop=True),
               waits=["qsc", "swv"], tag="qwb", inc=("Tm", 1), pad=True)
        P.step("V", lambda v: v.tensor_copy(out=tmpb, in_=pb[1][:, 0:1]),
               waits=["qwb"], tag="qwc", inc=("V", 1), pad=True)
        P.step("V", lambda v: v.reciprocal(out=rz[0], in_=sc_bc), waits=["qsc", "pn15_1_3"], tag="qrc", inc=("V", 1), pad=True)
        P.step("V", lambda v: v.tensor_scalar(out=s127, in0=rz[0], scalar1=127.0, scalar2=None, op0=ALU.mult),
               waits=["qrc"], tag="qs127", inc=("V", 1), pad=True)
        P.step("V", lambda v: v.reciprocal(out=rw_bc, in_=tmpb), waits=["qwc"], tag="qrw", inc=("V", 1), pad=True)
        P.step("V", lambda v: v.tensor_tensor(out=gamc, in0=sc_bc, in1=tmpb, op=ALU.mult),
               waits=["qwc"], tag="qg1", inc=("V", 1), pad=True)
        P.step("V", lambda v: v.tensor_scalar(out=gamc, in0=gamc, scalar1=1.0 / 127.0, scalar2=None, op0=ALU.mult),
               waits=["qg1"], tag="qgam", inc=("V", 1), pad=True)
        if debug:
            def dbgsc_f(v):
                v.tensor_copy(out=outsb[:, 0:1], in_=sc_bc)
                v.tensor_copy(out=outsb[:, 1:2], in_=tmpb)
                v.tensor_copy(out=outsb[:, 2:3], in_=gamc)
                v.tensor_copy(out=outsb[:, 3:4], in_=s127)
                for kk in range(8):
                    v.tensor_copy(out=outsb[:, 5 + kk:6 + kk], in_=swacc[kk])
                v.tensor_copy(out=outsb[:, 13:14], in_=swsum)
                return v.tensor_copy(out=outsb[:, 4:5], in_=gmax)
            P.step("V", dbgsc_f, waits=["qgam", "qs127", "swa7b"], tag="dbgscv", inc=("V", 1))
            P.step("G", lambda g: g.dma_start(out=dbg["sc"], in_=outsb[:, 0:16]),
                   waits=["dbgscv"], tag="dbgsc", inc=("G", 16))

        XAB_FREE = "qmmk15_1"
        for tt in range(2):
            P.step("Tm", bcmm(tt, mu_r, tt), waits=["qsb", "qsc", XAB_FREE], tag=f"b2mu{tt}", inc=("Tm", 1), pad=True)
            P.step("Tm", bcmm(2 + tt, rho_r, tt), waits=[SA0_FIN, SA1_FIN, "qwc"], tag=f"b2rho{tt}", inc=("Tm", 1), pad=True)
        for tt in range(2):
            P.step("V", (lambda tt: lambda v: v.tensor_copy(
                out=mubc2[:, 512 * tt:512 * (tt + 1)], in_=pb[tt][:, :]))(tt),
                waits=[f"b2mu{tt}"], tag=f"e2mu{tt}", inc=("V", 1))
            P.step("V", (lambda tt: lambda v: v.tensor_copy(
                out=rhobc2[:, 512 * tt:512 * (tt + 1)], in_=pb[2 + tt][:, :]))(tt),
                waits=[f"b2rho{tt}"], tag=f"e2rho{tt}", inc=("V", 1))

        xqb = vsb
        for cc in range(16):
            w = [f"exh{cc-2}"] if cc >= 2 else ["e2mu0", "e2mu1", "e2rho0", "e2rho1"]
            P.step("Y", (lambda cc: lambda y: y.dma_start(
                out=xsc3[cc % 2][:, :], in_=x_t[128 * cc:128 * (cc + 1), :]))(cc),
                waits=w, tag=f"exl{cc}", inc=("Y", 16))
            P.step("V", (lambda cc: lambda v: v.tensor_tensor(
                out=tsc3, in0=xsc3[cc % 2][:, :], in1=mubc2, op=ALU.subtract))(cc),
                waits=[f"exl{cc}"] + ([f"exq{cc-1}"] if cc > 0 else []), tag=f"esb{cc}", inc=("V", 1))
            P.step("V", (lambda cc: lambda v: v.tensor_tensor(
                out=xhsc3, in0=tsc3, in1=rhobc2, op=ALU.mult))(cc),
                waits=[f"esb{cc}"], tag=f"exh{cc}", inc=("V", 1))
            P.step("V", (lambda cc: lambda v: v.tensor_scalar(
                out=xgsc, in0=xhsc3, scalar1=gf_t[:, cc:cc + 1],
                scalar2=s127, op0=ALU.mult, op1=ALU.mult))(cc),
                waits=[f"exh{cc}", "qs127", "c1"], tag=f"exg{cc}", inc=("V", 1))
            iw = [f"exg{cc}"]
            if cc == 0:
                iw += ["av15_1", "cev15_1"]
            P.step("V", (lambda cc: lambda v: v.tensor_copy(out=xisc, in_=xgsc))(cc),
                   waits=iw, tag=f"exi{cc}", inc=("V", 1))
            P.step("V", (lambda cc: lambda v: v.tensor_copy(
                out=xqb[:, T * cc:T * (cc + 1)], in_=xisc))(cc),
                waits=[f"exi{cc}"], tag=f"exq{cc}", inc=("V", 1))
        if debug:
            P.step("G", lambda g: g.dma_start(out=dbg["xqb"], in_=xqb),
                   waits=["exq15"], tag="dbgxqb", inc=("G", 16))

        wo_user = {0: "swa6b", 1: "swa7b"}
        wob_user = {0: "qmmk14_1", 1: "qmmk15_1"}
        wfb_user = {0: "tp15_1_3", 1: "tp15_1_3"}
        pbD_user = {0: "e2mu0", 1: "e2mu1"}
        pbF_user = {0: "e2rho0", 1: "e2rho1"}
        for og in range(8):
            par = og % 2
            srco = woutT[:, 256 * og:256 * (og + 1)].rearrange("(c p) m -> p c m", p=128)
            srcf = wffnT[:, 256 * og:256 * (og + 1)].rearrange("(c p) m -> p c m", p=128)
            dsto = wsh[par].rearrange("p (c m) -> p c m", m=256)
            dstf = wsh[1 - par].rearrange("p (c m) -> p c m", m=256)
            P.step("Y", (lambda dsto, srco: lambda y: y.dma_start(out=dsto, in_=srco))(dsto, srco),
                   waits=[wo_user[par]], tag=f"dol{og}", inc=("Y", 16))
            P.step("Y", (lambda dstf, srcf: lambda y: y.dma_start(out=dstf, in_=srcf))(dstf, srcf),
                   waits=[wo_user[1 - par]], tag=f"dfl{og}", inc=("Y", 16))
            cvt = []
            for i in range(16):
                engx = ["G", "Sc"][i % 2]
                P.step(engx, (lambda par, i: lambda en: ecopy(en,
                    wob[par][:, 256 * i:256 * (i + 1)], wsh[par][:, 256 * i:256 * (i + 1)]))(par, i),
                    waits=[f"dol{og}", wob_user[par]], tag=f"dcv{og}_{i}", inc=(engx, 1))
                cvt.append(f"dcv{og}_{i}")
            joiner(f"dcv{og}", cvt)
            wo_user[par] = f"dcv{og}"
            fcvt = []
            for i in range(8):
                c0 = 512 * i
                srcw = wsh[1 - par][:, c0:c0 + 512]
                prevd = f"f2_{og}_{i-1}" if i > 0 else (f"f2_{og-1}_7" if og > 0 else None)
                prevc = f"f3_{og}_{i-1}" if i > 0 else (f"f3_{og-1}_7" if og > 0 else None)
                w1 = [f"dfl{og}", "qrw", wfb_user[par], "exq15"]
                if prevd:
                    w1.append(prevd)
                P.step("V", (lambda srcw: lambda v: v.tensor_scalar(
                    out=xgsc[:, 0:512], in0=srcw, scalar1=rw_bc, scalar2=1.0,
                    op0=ALU.mult, op1=ALU.min))(srcw),
                    waits=w1, tag=f"f1_{og}_{i}", inc=("V", 1))
                w2 = [f"f1_{og}_{i}"]
                if prevc:
                    w2.append(prevc)
                P.step("V", lambda v: v.tensor_scalar(
                    out=tsc3[:, 0:512], in0=xgsc[:, 0:512], scalar1=-1.0, scalar2=None, op0=ALU.max),
                    waits=w2, tag=f"f2_{og}_{i}", inc=("V", 1))
                P.step("V", lambda v: v.tensor_copy(out=xisc[:, 0:512], in_=tsc3[:, 0:512]),
                       waits=[f"f2_{og}_{i}"], tag=f"f3_{og}_{i}", inc=("V", 1))
                P.step("V", (lambda par, i: lambda v: v.tensor_copy(
                    out=wfb[par].rearrange("p (a b) -> p a b", a=8)[:, i, :],
                    in_=xisc[:, 0:512]))(par, i),
                    waits=[f"f3_{og}_{i}"], tag=f"fcv{og}_{i}", inc=("V", 1))
                fcvt.append(f"fcv{og}_{i}")
            joiner(f"fcv{og}", fcvt)
            wo_user[1 - par] = f"fcv{og}"

            for tc in range(8):
                xw = [f"res{og}_{tc-2}"] if tc >= 2 else ([f"res{og-1}_{tc+6}"] if og > 0 else [])
                P.step("G", (lambda og, tc: lambda g: g.dma_start(
                    out=xnst[:, 256 * (tc % 2):256 * (tc % 2) + 256],
                    in_=x_n[128 * tc:128 * (tc + 1), 256 * og:256 * (og + 1)]))(og, tc),
                    waits=xw, tag=f"xnl{og}_{tc}", inc=("G", 16))
                bD = tc % 2
                bF = 2 + tc % 2

                def domm(og, tc, bD, par):
                    def f(t):
                        r = None
                        for hh in range(16):
                            r = t.matmul(out=pb[bD][:, 0:256],
                                         lhsT=ctxb[:, T * hh + 128 * tc:T * hh + 128 * (tc + 1)],
                                         rhs=wob[par][:, 256 * hh:256 * (hh + 1)],
                                         start=(hh == 0), stop=(hh == 15))
                        return r
                    return f

                def femm(og, tc, bF, par):
                    def f(t):
                        r = None
                        for cc in range(16):
                            r = t.matmul(out=pb[bF][:, 0:256],
                                         lhsT=xqb[:, T * cc + 128 * tc:T * cc + 128 * (tc + 1)],
                                         rhs=wfb[par][:, 256 * cc:256 * (cc + 1)],
                                         start=(cc == 0), stop=(cc == 15))
                        return r
                    return f
                P.step("Tm", domm(og, tc, bD, par),
                       waits=[f"dcv{og}", pbD_user[tc % 2], "cev15_1"], tag=f"dmm{og}_{tc}", inc=("Tm", 1))
                P.step("Tm", femm(og, tc, bF, par),
                       waits=[f"fcv{og}", pbF_user[tc % 2], "exq15"], tag=f"fmm{og}_{tc}", inc=("Tm", 1))
                e1w = [f"dmm{og}_{tc}", f"xnl{og}_{tc}"]
                if tc >= 2:
                    e1w.append(f"res{og}_{tc-2}")
                elif og > 0:
                    e1w.append(f"res{og-1}_{tc+6}")
                P.step("V", (lambda og, tc, bD: lambda v: v.tensor_tensor(
                    out=o1b[:, 256 * (tc % 2):256 * (tc % 2) + 256], in0=pb[bD][:, 0:256],
                    in1=xnst[:, 256 * (tc % 2):256 * (tc % 2) + 256], op=ALU.add))(og, tc, bD),
                    waits=e1w, tag=f"ev1{og}_{tc}", inc=("V", 1))
                pbD_user[tc % 2] = f"ev1{og}_{tc}"
                rw2 = [f"ev1{og}_{tc}", f"fmm{og}_{tc}", "qgam"]
                if tc >= 2:
                    rw2.append(f"sto{og}_{tc-2}")
                elif og > 0:
                    rw2.append(f"sto{og-1}_{tc+6}")
                P.step("V", (lambda og, tc, bF: lambda v: v.scalar_tensor_tensor(
                    out=outsb[:, 512 * (tc % 2):512 * (tc % 2) + 256], in0=pb[bF][:, 0:256],
                    scalar=gamc, in1=o1b[:, 256 * (tc % 2):256 * (tc % 2) + 256],
                    op0=ALU.mult, op1=ALU.add))(og, tc, bF),
                    waits=rw2, tag=f"res{og}_{tc}", inc=("V", 1))
                pbF_user[tc % 2] = f"res{og}_{tc}"
                P.step("G", (lambda og, tc: lambda g: g.dma_start(
                    out=out_d[128 * tc:128 * (tc + 1), 256 * og:256 * (og + 1)],
                    in_=outsb[:, 512 * (tc % 2):512 * (tc % 2) + 256]))(og, tc),
                    waits=[f"res{og}_{tc}"], tag=f"sto{og}_{tc}", inc=("G", 16))
            wob_user[par] = f"dmm{og}_7"
            wfb_user[par] = f"fmm{og}_7"

        if debug2:
            ALLST = [f"sto{og}_{tc}" for og in range(8) for tc in range(8)]
            P.step("G", lambda g: g.dma_start(out=dbg["ctx"], in_=ctxb), waits=ALLST, tag="d2a", inc=("G", 16))
            P.step("G", lambda g: g.dma_start(out=dbg["xqb"], in_=xqb), waits=["d2a"], tag="d2b", inc=("G", 16))
            def d2scf(v):
                v.tensor_copy(out=outsb[:, 0:1], in_=sc_bc)
                v.tensor_copy(out=outsb[:, 1:2], in_=tmpb)
                v.tensor_copy(out=outsb[:, 2:3], in_=gamc)
                v.tensor_copy(out=outsb[:, 3:4], in_=s127)
                return v.tensor_copy(out=outsb[:, 4:5], in_=gmax)
            P.step("V", d2scf, waits=ALLST, tag="d2sc", inc=("V", 1))
            P.step("G", lambda g: g.dma_start(out=dbg["sc"], in_=outsb[:, 0:16]), waits=["d2sc"], tag="d2c", inc=("G", 16))
            P.step("G", lambda g: g.dma_start(out=dbg["pn"][0], in_=p_n[0]), waits=["d2c"], tag="d2d", inc=("G", 16))
            P.step("G", lambda g: g.dma_start(out=dbg["pn"][1], in_=p_n[1]), waits=["d2d"], tag="d2e", inc=("G", 16))
            P.step("G", lambda g: g.dma_start(out=dbg["z"], in_=smal_t[:, :]), waits=["d2e"], tag="d2f", inc=("G", 16))
            P.step("G", lambda g: g.dma_start(out=dbg["v"], in_=vsb), waits=["d2f"], tag="d2g", inc=("G", 16))
            P.step("G", lambda g: g.dma_start(out=dbg["st"], in_=stat_t[:, :]), waits=["d2g"], tag="d2h", inc=("G", 16))
            P.step("G", lambda g: g.dma_start(out=dbg["bc"][:, 0:1024], in_=mubc2),
                   waits=["d2h"], tag="d2i0", inc=("G", 16))
            P.step("G", lambda g: g.dma_start(out=dbg["bc"][:, 1024:2048], in_=rhobc2),
                   waits=["d2i0"], tag="d2i", inc=("G", 16))
        P.step("G", None, waits=[f"sto{og}_{tc}" for og in range(8) for tc in range(8)]
               + (["d2i"] if debug2 else []))

        for (e_, fn_, waits_, inc_, pad_, dup_) in P.steps:
            for (sn, val) in waits_:
                assert val <= P.cnt[sn], (sn, val, P.cnt[sn])

        @block.sync
        def _(y):
            P.emit("Y", y, sems)

        @block.vector
        def _(v):
            P.emit("V", v, sems, padfn=lambda en: en.memset(junkv, 0.0))

        @block.scalar
        def _(s):
            def scpad(en):
                en.memzero(junks)
                return en.memzero(junks)
            P.emit("Sc", s, sems, padfn=scpad)

        @block.tensor
        def _(t):
            P.emit("Tm", t, sems,
                   padfn=lambda en: en.transpose(out=ptp[0][0:1, 0:128], in_=onb_t[:, :],
                                                 identity=idb_t[:, :]))

        @block.gpsimd
        def _(g):
            P.emit("G", g, sems, padfn=lambda en: en.memset(junkg, 0.0))

    return nc


def _host_prep(inputs):
    x = np.ascontiguousarray(np.asarray(inputs["L0_attn_x"], dtype=np.float32).reshape(NTOK, D))
    wqkvT = np.ascontiguousarray(np.asarray(inputs["L0_attn_w_qkv"], dtype=np.float32).T)
    woutT = np.ascontiguousarray(np.asarray(inputs["L0_attn_w_out"], dtype=np.float32).T)
    wffnT = np.ascontiguousarray(np.asarray(inputs["L0_ffn_weight"], dtype=np.float32).T)
    nw = np.asarray(inputs["L0_attn_norm_w"], dtype=np.float32)
    gf = np.asarray(inputs["L0_ffn_norm_weight"], dtype=np.float32)
    pm = np.asarray(inputs["L0_attn_pm"], dtype=np.float32)

    nw_f = np.ascontiguousarray(nw.reshape(16, 128).T)
    gf_f = np.ascontiguousarray(gf.reshape(16, 128).T)
    pmk_h = np.ascontiguousarray(pm[0].transpose(2, 0, 1).reshape(128, 256)).astype(ml_dtypes.bfloat16)
    pmv_h = np.ascontiguousarray(pm[1].transpose(1, 0, 2).reshape(16, 2048)).astype(ml_dtypes.bfloat16)

    inv = 1.0 / (10000.0 ** (np.arange(0, DH, 2, dtype=np.float64) / DH))
    ptmat = np.zeros((128, 128), dtype=np.float32)
    for i in range(64):
        ptmat[2 * i, 2 * i + 1] = -1.0
        ptmat[2 * i + 1, 2 * i] = 1.0
    ptm_h = np.ascontiguousarray(ptmat.T).astype(ml_dtypes.bfloat16)
    idb_h = np.eye(128, dtype=np.float32).astype(ml_dtypes.bfloat16)
    idf_h = np.eye(128, dtype=np.float32)
    onb_h = np.ones((128, 1), dtype=np.float32).astype(ml_dtypes.bfloat16)
    onf_h = np.ones((1, 128), dtype=np.float32)
    onc_h = np.ones((128, 1), dtype=np.float32)
    r = np.arange(128)
    c = np.arange(128)
    mdiag = np.where(c[None, :] <= r[:, None], 0.0, -1.0e30).astype(np.float32).astype(ml_dtypes.bfloat16)
    mfirst = np.where((c[None, :] <= r[:, None]) | (c[None, :] < 16), 0.0, -1.0e30).astype(np.float32).astype(ml_dtypes.bfloat16)

    in_maps = []
    for core in range(NCORES):
        sh = x[T * core:T * (core + 1)]
        pos = np.arange(T * core, T * (core + 1), dtype=np.float64)
        fr = pos[:, None] * inv[None, :]
        cosx = np.repeat(np.cos(fr), 2, axis=1).T.astype(np.float32)
        sinx = np.repeat(np.sin(fr), 2, axis=1).T.astype(np.float32)
        in_maps.append(dict(
            x_t=np.ascontiguousarray(sh.T), x_n=np.ascontiguousarray(sh),
            wqkvT=wqkvT, woutT=woutT, wffnT=wffnT, nw_f=nw_f, gf_f=gf_f,
            pmk_h=pmk_h, pmv_h=pmv_h,
            cos_h=np.ascontiguousarray(cosx).astype(ml_dtypes.bfloat16),
            sin_h=np.ascontiguousarray(sinx).astype(ml_dtypes.bfloat16),
            ptm_h=ptm_h, idb_h=idb_h, idf_h=idf_h, onb_h=onb_h, onf_h=onf_h,
            onc_h=onc_h, mfirst_h=mfirst, mdiag_h=mdiag,
        ))
    return in_maps


def kernel(**inputs):
    in_maps = _host_prep(inputs)
    nc = build_nc(debug=DEBUG)
    res = run_bass_kernel_spmd(nc, in_maps, core_ids=list(range(NCORES)))
    out = np.concatenate([r["out"] for r in res.results], axis=0)
    return out.reshape(1, NTOK, D)


if __name__ == "__main__":
    import reference
    inp = reference.setup_inputs(0)
    got = kernel(**{k: np.asarray(v) for k, v in inp.items()})
    ref = np.asarray(reference.reference(**inp))
    err = np.mean((got - ref) ** 2) / np.mean(ref ** 2)
    print("resvar:", err, "maxabs:", np.max(np.abs(got - ref)))


# version 23
# speedup vs baseline: 1.0441x; 1.0100x over previous
import numpy as np
import ml_dtypes
from collections import defaultdict
from contextlib import ExitStack

import concourse.bass as bass
import concourse.mybir as mybir
from concourse.bass_utils import run_bass_kernel_spmd

F32 = mybir.dt.float32
BF16 = mybir.dt.bfloat16
I32 = mybir.dt.int32
AF = mybir.ActivationFunctionType
ALU = mybir.AluOpType
AX = mybir.AxisListType.X

NCORES = 8
T = 1024
D = 2048
H = 16
DH = 128
WIN = 2
CSC = float(DH) ** -0.5
NEG = -3.0e38
NTOK = 8192

DEBUG = False


class Prog:
    def __init__(self):
        self.steps = []
        self.cnt = defaultdict(int)
        self.tags = {}

    def step(self, eng, fn, waits=(), tag=None, inc=None, pad=False, dup=False):
        for t in waits:
            assert t in self.tags, f"wait on undefined tag {t}"
        if inc is not None:
            self.cnt[inc[0]] += inc[1]
            if tag is not None:
                self.tags[tag] = (inc[0], self.cnt[inc[0]])
        else:
            assert tag is None
        self.steps.append((eng, fn, tuple(self.tags[t] for t in waits), inc, pad, dup))

    def emit(self, eng_name, eng, sems, padfn=None):
        for (e, fn, waits, inc, pad, dup) in self.steps:
            if e != eng_name:
                continue
            byname = {}
            for (sname, val) in waits:
                byname[sname] = max(byname.get(sname, 0), val)
            for sname, val in byname.items():
                eng.wait_ge(sems[sname], val)
            if fn is None:
                continue
            inst = fn(eng)
            if dup:
                inst = fn(eng)
            if pad:
                assert padfn is not None
                inst = padfn(eng)
            if inc is not None:
                assert inst is not None
                inst.then_inc(sems[inc[0]], inc[1])


def ecopy(en, out, in_):
    if hasattr(en, "tensor_copy"):
        return en.tensor_copy(out=out, in_=in_)
    return en.copy(out=out, in_=in_)


def build_nc(debug=False, debug2=False):
    nc = bass.Bass("TRN2", target_bir_lowering=False)

    x_t = nc.dram_tensor("x_t", [D, T], F32, kind="ExternalInput").ap()
    x_n = nc.dram_tensor("x_n", [T, D], F32, kind="ExternalInput").ap()
    wqkvT = nc.dram_tensor("wqkvT", [D, 3 * D], F32, kind="ExternalInput").ap()
    woutT = nc.dram_tensor("woutT", [D, D], F32, kind="ExternalInput").ap()
    wffnT = nc.dram_tensor("wffnT", [D, D], F32, kind="ExternalInput").ap()
    nw_f = nc.dram_tensor("nw_f", [128, 16], F32, kind="ExternalInput").ap()
    gf_f = nc.dram_tensor("gf_f", [128, 16], F32, kind="ExternalInput").ap()
    pmk_h = nc.dram_tensor("pmk_h", [128, 256], BF16, kind="ExternalInput").ap()
    pmv_h = nc.dram_tensor("pmv_h", [16, 2048], BF16, kind="ExternalInput").ap()
    cos_h = nc.dram_tensor("cos_h", [128, T], BF16, kind="ExternalInput").ap()
    sin_h = nc.dram_tensor("sin_h", [128, T], BF16, kind="ExternalInput").ap()
    ptm_h = nc.dram_tensor("ptm_h", [128, 128], BF16, kind="ExternalInput").ap()
    idb_h = nc.dram_tensor("idb_h", [128, 128], BF16, kind="ExternalInput").ap()
    idf_h = nc.dram_tensor("idf_h", [128, 128], F32, kind="ExternalInput").ap()
    onb_h = nc.dram_tensor("onb_h", [128, 1], BF16, kind="ExternalInput").ap()
    onf_h = nc.dram_tensor("onf_h", [1, 128], F32, kind="ExternalInput").ap()
    onc_h = nc.dram_tensor("onc_h", [128, 1], F32, kind="ExternalInput").ap()
    mfirst_h = nc.dram_tensor("mfirst_h", [128, 128], BF16, kind="ExternalInput").ap()
    mdiag_h = nc.dram_tensor("mdiag_h", [128, 128], BF16, kind="ExternalInput").ap()
    out_d = nc.dram_tensor("out", [T, D], F32, kind="ExternalOutput").ap()
    cc_in = nc.dram_tensor("cc_in", [128, 128], F32, kind="Internal").ap()
    cc_out = nc.dram_tensor("cc_out", [128, 128], F32, kind="Internal").ap()

    dbg = {}
    if debug2:
        dbg["ctx"] = nc.dram_tensor("dbg_ctx", [128, H * T], BF16, kind="ExternalOutput").ap()
        dbg["xqb"] = nc.dram_tensor("dbg_xqb", [128, 16 * T], BF16, kind="ExternalOutput").ap()
        dbg["sc"] = nc.dram_tensor("dbg_sc", [128, 16], F32, kind="ExternalOutput").ap()
        dbg["pn"] = nc.dram_tensor("dbg_pn", [2, 128, 544], BF16, kind="ExternalOutput").ap()
        dbg["z"] = nc.dram_tensor("dbg_z", [128, 128], F32, kind="ExternalOutput").ap()
        dbg["v"] = nc.dram_tensor("dbg_v", [128, 8 * D], BF16, kind="ExternalOutput").ap()
        dbg["st"] = nc.dram_tensor("dbg_st", [128, 2048], F32, kind="ExternalOutput").ap()
        dbg["bc"] = nc.dram_tensor("dbg_bc", [128, 2048], F32, kind="ExternalOutput").ap()
        dbg["st0"] = nc.dram_tensor("dbg_st0", [1, 2048], F32, kind="ExternalOutput").ap()
        dbg["gfa"] = nc.dram_tensor("dbg_gfa", [128, 16], F32, kind="ExternalOutput").ap()
        dbg["mb"] = nc.dram_tensor("dbg_mb", [128, 1024], F32, kind="ExternalOutput").ap()
        dbg["rb"] = nc.dram_tensor("dbg_rb", [128, 1024], F32, kind="ExternalOutput").ap()
    if debug:
        dbg["xab"] = nc.dram_tensor("dbg_xab", [128, 16 * T], BF16, kind="ExternalOutput").ap()
        dbg["qh"] = nc.dram_tensor("dbg_qh", [H, 128, T], BF16, kind="ExternalOutput").ap()
        dbg["kh"] = nc.dram_tensor("dbg_kh", [H, 128, T], BF16, kind="ExternalOutput").ap()
        dbg["v"] = nc.dram_tensor("dbg_v", [128, 8 * D], BF16, kind="ExternalOutput").ap()
        dbg["st"] = nc.dram_tensor("dbg_st", [128, 2048], F32, kind="ExternalOutput").ap()
        dbg["bc"] = nc.dram_tensor("dbg_bc", [128, 2048], F32, kind="ExternalOutput").ap()
        dbg["st0"] = nc.dram_tensor("dbg_st0", [1, 2048], F32, kind="ExternalOutput").ap()
        dbg["gfa"] = nc.dram_tensor("dbg_gfa", [128, 16], F32, kind="ExternalOutput").ap()
        dbg["mb"] = nc.dram_tensor("dbg_mb", [128, 1024], F32, kind="ExternalOutput").ap()
        dbg["rb"] = nc.dram_tensor("dbg_rb", [128, 1024], F32, kind="ExternalOutput").ap()
        dbg["p"] = nc.dram_tensor("dbg_p", [H, WIN, 4, 128, 544], BF16, kind="ExternalOutput").ap()
        dbg["ctx"] = nc.dram_tensor("dbg_ctx", [128, H * T], BF16, kind="ExternalOutput").ap()
        dbg["xqb"] = nc.dram_tensor("dbg_xqb", [128, 16 * T], BF16, kind="ExternalOutput").ap()
        dbg["sc"] = nc.dram_tensor("dbg_sc", [128, 16], F32, kind="ExternalOutput").ap()

    with ExitStack() as ctx:
        e = ctx.enter_context

        xab_t = e(nc.sbuf_tensor("xab", [128, 16 * T], BF16))
        vsb_t = e(nc.sbuf_tensor("vsb", [128, 8 * D], BF16))
        ctx_t = e(nc.sbuf_tensor("ctxb", [128, H * T], BF16))
        wst_t = e(nc.sbuf_tensor("wstage", [128, 8192], F32))
        wbf0_t = e(nc.sbuf_tensor("wbf0", [128, 4096], BF16))
        wbf1_t = e(nc.sbuf_tensor("wbf1", [128, 4096], BF16))
        mega_t = e(nc.sbuf_tensor("mega", [128, 10240], BF16))
        stat_t = e(nc.sbuf_tensor("statsl", [128, 2048], F32))
        smal_t = e(nc.sbuf_tensor("smal", [128, 128], F32))
        outs_t = e(nc.sbuf_tensor("outsb", [128, 1024], F32))
        xnst_t = e(nc.sbuf_tensor("xnst", [128, 512], F32))
        o1b_t = e(nc.sbuf_tensor("o1b", [128, 512], F32))
        mfir_t = e(nc.sbuf_tensor("mfirst", [128, 128], BF16))
        mdia_t = e(nc.sbuf_tensor("mdiag", [128, 128], BF16))
        nw_t = e(nc.sbuf_tensor("nwsb", [128, 16], F32))
        gf_t = e(nc.sbuf_tensor("gfsb", [128, 16], F32))
        gfa_t = e(nc.sbuf_tensor("gfab", [128, 16], F32))
        pmk_t = e(nc.sbuf_tensor("pmk", [128, 256], BF16))
        pmv_t = e(nc.sbuf_tensor("pmv", [16, 2048], BF16))
        ptm_t = e(nc.sbuf_tensor("ptm", [128, 128], BF16))
        idb_t = e(nc.sbuf_tensor("idb", [128, 128], BF16))
        idf_t = e(nc.sbuf_tensor("idf", [128, 128], F32))
        onb_t = e(nc.sbuf_tensor("onb", [128, 1], BF16))
        onf_t = e(nc.sbuf_tensor("onf", [1, 128], F32))
        onc_t = e(nc.sbuf_tensor("onc", [128, 1], F32))
        ccst_t = e(nc.sbuf_tensor("ccst", [128, 128], F32))
        pmpt_t = [e(nc.sbuf_tensor(f"pmpt{w}", [16, 512], BF16)) for w in range(2)]
        ptj_t = [[e(nc.sbuf_tensor(f"ptj{w}_{j}", [128, 512 - 128 * j], BF16)) for j in range(4)]
                 for w in range(2)]
        qtmp_t = e(nc.sbuf_tensor("qtmp", [128, 1024], BF16))
        qtmk_t = e(nc.sbuf_tensor("qtmk", [128, 1024], BF16))
        rots_t = e(nc.sbuf_tensor("rotsb", [128, 1024], BF16))
        rotk_t = e(nc.sbuf_tensor("rotkb", [128, 1024], BF16))
        t1b_t = e(nc.sbuf_tensor("t1b", [128, 1024], BF16))

        xab = xab_t[:, :]
        vsb = vsb_t[:, :]
        ctxb = ctx_t[:, :]
        wst = wst_t[:, :]
        mega = mega_t[:, :]
        outsb = outs_t[:, :]
        xnst = xnst_t[:, :]
        o1b = o1b_t[:, :]
        qtmp_b = [qtmp_t[:, :], qtmk_t[:, :]]
        rots_b = [rots_t[:, :], rotk_t[:, :]]
        t1b = t1b_t[:, :]
        wbf = [wbf0_t[:, :], wbf1_t[:, :]]
        ptj = [[p[:, :] for p in row] for row in ptj_t]
        pmpt = [p[:, :] for p in pmpt_t]

        qh = [mega[:, 0:1024], mega[:, 1024:2048]]
        kh = [mega[:, 2048:3072], mega[:, 3072:4096]]
        cosb = mega[:, 4096:5120]
        sinb = mega[:, 5120:6144]
        p_sb = [mega[:, 6144:6688], mega[:, 6688:7232]]
        p_n = [mega[:, 7232:7776], mega[:, 7776:8320]]
        wfb = [mega[:, 0:4096], mega[:, 4096:8192]]
        wob = [wbf[0], wbf[1]]

        xsc = [ctxb[:, 2048 * s:2048 * (s + 1)].bitcast(F32) for s in range(4)]
        tsc = ctxb[:, 8192:10240].bitcast(F32)
        xhsc = ctxb[:, 10240:12288].bitcast(F32)
        xbsc = ctxb[:, 12288:13312]
        sqsc = ctxb[:, 13312:14336]
        mubc = wbf[0][:, 0:2048].bitcast(F32)
        rhobc = wbf[1][:, 0:2048].bitcast(F32)
        xsc3 = [xab[:, 2048 * s:2048 * (s + 1)].bitcast(F32) for s in range(2)]
        tsc3 = xab[:, 4096:6144].bitcast(F32)
        xhsc3 = xab[:, 6144:8192].bitcast(F32)
        mubc2 = xab[:, 8192:10240].bitcast(F32)
        rhobc2 = xab[:, 10240:12288].bitcast(F32)
        xgsc = xab[:, 12288:14336].bitcast(F32)
        xisc = xab[:, 14336:16384].bitcast(I32)

        def sl(row, a, b):
            return stat_t[row:row + 1, a:b]
        mu_r = lambda tt: sl(0, 512 * tt, 512 * (tt + 1))
        rho_r = lambda tt: sl(0, 1024 + 512 * tt, 1024 + 512 * (tt + 1))
        aux1 = lambda tt: outs_t[0:1, 512 * tt:512 * (tt + 1)]
        aux2 = lambda tt: (xnst_t if tt == 0 else o1b_t)[0:1, 0:512]

        S_ = lambda i: smal_t[:, i:i + 1]
        gmax = S_(0)
        am_c, am2_c = S_(1), S_(2)
        m_c = [S_(3), S_(4)]
        m2_c = [S_(5), S_(6)]
        negm_c = [S_(7), S_(8)]
        za = [S_(9), S_(10)]
        zb = [S_(11), S_(12)]
        zs = [S_(13), S_(38)]
        rz = [S_(14), S_(39)]
        swacc = [S_(15 + i) for i in range(8)]
        swacc_b = S_(23)
        swsum, sw_s, rw_bc, sc_bc, s127, gamc, tmpa, tmpb, junk = (
            S_(24), S_(25), S_(26), S_(27), S_(28), S_(29), S_(30), S_(31), S_(32))
        eps5_c, eps6_c = S_(33), S_(34)
        junkv, junks, junkg = S_(35), S_(36), S_(37)
        amall = smal_t[:, 40:56]

        pb = [e(nc.psum_tensor(f"pb{i}", [128, 512], F32)) for i in range(6)]
        ptp = [e(nc.psum_tensor(f"ptp{i}", [128, 128], BF16)) for i in range(2)]
        sa = [pb[2], pb[3]]
        ctxps = [pb[4], pb[4]]
        rotps = pb[5]

        sems = {}
        for sname in ["Y", "G", "Tm", "V", "Sc"]:
            sems[sname] = e(nc.semaphore(f"sem{sname}"))
        block = e(nc.Block())

        P = Prog()

        def joiner(name, waits):
            P.step("V", lambda v: v.memset(junk, 0.0), waits=waits, tag=name, inc=("V", 1))

        def cdma(dst, src):
            return lambda g: g.dma_start(out=dst, in_=src)
        consts = [
            (nw_t[:, :], nw_f), (gf_t[:, :], gf_f), (pmk_t[:, :], pmk_h), (pmv_t[:, :], pmv_h),
            (cosb, cos_h), (sinb, sin_h), (ptm_t[:, :], ptm_h), (idb_t[:, :], idb_h),
            (idf_t[:, :], idf_h), (onb_t[:, :], onb_h), (onf_t[:, :], onf_h), (onc_t[:, :], onc_h),
            (mfir_t[:, :], mfirst_h), (mdia_t[:, :], mdiag_h),
        ]
        for i, (dst, src) in enumerate(consts):
            P.step("G", cdma(dst, src), tag=f"c{i}", inc=("G", 16))
        P.step("V", lambda v: v.memset(smal_t[:, :], 0.0), waits=[], tag="sm0", inc=("V", 1), pad=True)
        P.step("Sc", lambda s: s.activation(out=gfa_t[:, :], in_=gf_t[:, :], func=AF.Abs),
               waits=["c1"], tag="gfab", inc=("Sc", 1), pad=True)
        def epsf(v):
            v.memset(eps5_c, 1e-5)
            return v.memset(eps6_c, 1e-6)
        P.step("V", epsf, waits=[], tag="eps", inc=("V", 1))

        for cc in range(16):
            w = [f"amm{cc-4}"] if cc >= 4 else []
            P.step("Y", cdma(xsc[cc % 4][:, :], x_t[128 * cc:128 * (cc + 1), :]),
                   waits=w, tag=f"axl{cc}", inc=("Y", 16))
            P.step("V", (lambda cc: lambda v: v.tensor_copy(out=xbsc, in_=xsc[cc % 4][:, :]))(cc),
                   waits=[f"axl{cc}"] + ([f"amm{cc-1}"] if cc > 0 else []),
                   tag=f"axb{cc}", inc=("V", 1))
            P.step("Sc", (lambda cc: lambda s: s.activation(out=sqsc, in_=xsc[cc % 4][:, :], func=AF.Square))(cc),
                   waits=[f"axl{cc}"] + ([f"amm{cc-1}"] if cc > 0 else []),
                   tag=f"asq{cc}", inc=("Sc", 1))

            def statmm(cc):
                def f(t):
                    first, last = (cc == 0), (cc == 15)
                    t.matmul(out=pb[0][0:1, :], lhsT=onb_t[:, :], rhs=xbsc[:, 0:512], start=first, stop=last)
                    t.matmul(out=pb[1][0:1, :], lhsT=onb_t[:, :], rhs=xbsc[:, 512:1024], start=first, stop=last)
                    t.matmul(out=pb[2][0:1, :], lhsT=onb_t[:, :], rhs=sqsc[:, 0:512], start=first, stop=last)
                    return t.matmul(out=pb[3][0:1, :], lhsT=onb_t[:, :], rhs=sqsc[:, 512:1024], start=first, stop=last)
                return f
            P.step("Tm", statmm(cc), waits=[f"axb{cc}", f"asq{cc}", "c9"],
                   tag=f"amm{cc}", inc=("Tm", 1), pad=(cc == 15))

        for tt in range(2):
            P.step("V", (lambda tt: lambda v: v.tensor_scalar(
                out=mu_r(tt), in0=pb[tt][0:1, :], scalar1=1.0 / 2048.0, scalar2=None, op0=ALU.mult))(tt),
                waits=["amm15"], tag=f"amu{tt}", inc=("V", 1), pad=True)
            P.step("V", (lambda tt: lambda v: v.tensor_scalar(
                out=aux1(tt), in0=pb[2 + tt][0:1, :], scalar1=1.0 / 2048.0, scalar2=None, op0=ALU.mult))(tt),
                waits=["amm15"], tag=f"am2{tt}", inc=("V", 1), pad=True)
            P.step("Sc", (lambda tt: lambda s: s.activation(out=aux2(tt), in_=mu_r(tt), func=AF.Square))(tt),
                   waits=[f"amu{tt}"], tag=f"amsq{tt}", inc=("Sc", 1), pad=True)
            P.step("V", (lambda tt: lambda v: v.tensor_tensor(
                out=aux1(tt), in0=aux1(tt), in1=aux2(tt), op=ALU.subtract))(tt),
                waits=[f"am2{tt}", f"amsq{tt}"], tag=f"avar{tt}", inc=("V", 1), pad=True)
            P.step("Sc", (lambda tt: lambda s: s.activation(out=aux2(tt), in_=aux1(tt), func=AF.Ln, bias=eps5_c[0:1, :]))(tt),
                   waits=[f"avar{tt}", "eps"], tag=f"alnv{tt}", inc=("Sc", 1), pad=True)
            P.step("Sc", (lambda tt: lambda s: s.activation(out=rho_r(tt), in_=aux1(tt), func=AF.Ln))(tt),
                   waits=[f"avar{tt}"], tag=f"alnt{tt}", inc=("Sc", 1), pad=True)
            P.step("Sc", (lambda tt: lambda s: s.activation(out=aux1(tt), in_=aux2(tt), func=AF.Exp, scale=-0.5))(tt),
                   waits=[f"alnv{tt}", f"alnt{tt}"], tag=f"ar1{tt}", inc=("Sc", 1), pad=True)
            P.step("V", (lambda tt: lambda v: v.tensor_tensor(
                out=rho_r(tt), in0=rho_r(tt), in1=aux2(tt), op=ALU.subtract))(tt),
                waits=[f"ar1{tt}"], tag=f"alnd{tt}", inc=("V", 1), pad=True)
            P.step("Sc", (lambda tt: lambda s: s.activation(out=rho_r(tt), in_=rho_r(tt), func=AF.Exp))(tt),
                   waits=[f"alnd{tt}"], tag=f"at2{tt}", inc=("Sc", 1), pad=True)
            P.step("Sc", (lambda tt: lambda s: s.activation(out=rho_r(tt), in_=rho_r(tt), func=AF.Ln, bias=eps6_c[0:1, :]))(tt),
                   waits=[f"at2{tt}", "eps"], tag=f"alt2{tt}", inc=("Sc", 1), pad=True)
            P.step("Sc", (lambda tt: lambda s: s.activation(out=rho_r(tt), in_=rho_r(tt), func=AF.Exp, scale=-0.5))(tt),
                   waits=[f"alt2{tt}"], tag=f"ar2{tt}", inc=("Sc", 1), pad=True)
            P.step("V", (lambda tt: lambda v: v.tensor_tensor(
                out=rho_r(tt), in0=rho_r(tt), in1=aux1(tt), op=ALU.mult))(tt),
                waits=[f"ar2{tt}"], tag=f"arho{tt}", inc=("V", 1), pad=True)

        def bcmm(dst_bank, src, tt):
            def f(t):
                return t.matmul(out=pb[dst_bank][:, :], lhsT=onf_t[:, :], rhs=src(tt), start=True, stop=True)
            return f
        for tt in range(2):
            P.step("Tm", bcmm(tt, mu_r, tt), waits=[f"arho{tt}", f"amu{tt}", "c10"],
                   tag=f"bmu{tt}", inc=("Tm", 1), pad=True)
            P.step("Tm", bcmm(2 + tt, rho_r, tt), waits=[f"arho{tt}"], tag=f"brho{tt}", inc=("Tm", 1), pad=True)
        for tt in range(2):
            P.step("V", (lambda tt: lambda v: v.tensor_copy(out=mubc[:, 512 * tt:512 * (tt + 1)], in_=pb[tt][:, :]))(tt),
                   waits=[f"bmu{tt}"], tag=f"emu{tt}", inc=("V", 1))
            P.step("V", (lambda tt: lambda v: v.tensor_copy(out=rhobc[:, 512 * tt:512 * (tt + 1)], in_=pb[2 + tt][:, :]))(tt),
                   waits=[f"brho{tt}"], tag=f"erho{tt}", inc=("V", 1))

        if debug2:
            P.step("G", lambda g: g.dma_start(out=dbg["st0"], in_=stat_t[0:1, :]),
                   waits=["arho0", "arho1", "amu0", "amu1"], tag="dst0", inc=("G", 16))
        if debug2:
            P.step("G", lambda g: g.dma_start(out=dbg["gfa"], in_=gfa_t[:, :]),
                   waits=["gfab"], tag="dgfa", inc=("G", 16))
            P.step("G", lambda g: g.dma_start(out=dbg["mb"], in_=mubc),
                   waits=["emu0", "emu1"], tag="dmb", inc=("G", 16))
            P.step("G", lambda g: g.dma_start(out=dbg["rb"], in_=rhobc),
                   waits=["erho0", "erho1"], tag="drb", inc=("G", 16))
        for cc in range(16):
            w = [f"asub{cc-4}"] if cc >= 4 else ["amm15"]
            P.step("Y", cdma(xsc[cc % 4][:, :], x_t[128 * cc:128 * (cc + 1), :]),
                   waits=w, tag=f"axr{cc}", inc=("Y", 16))
            P.step("V", (lambda cc: lambda v: v.tensor_tensor(
                out=tsc, in0=xsc[cc % 4][:, :], in1=mubc, op=ALU.subtract))(cc),
                waits=[f"axr{cc}", "emu0", "emu1"] + ([f"axh{cc-1}"] if cc > 0 else []),
                tag=f"asub{cc}", inc=("V", 1))
            P.step("V", (lambda cc: lambda v: v.tensor_tensor(
                out=xhsc, in0=tsc, in1=rhobc, op=ALU.mult))(cc),
                waits=[f"asub{cc}", "erho0", "erho1"], tag=f"axh{cc}", inc=("V", 1))
            P.step("V", (lambda cc: lambda v: v.tensor_scalar(
                out=xab[:, T * cc:T * (cc + 1)], in0=xhsc,
                scalar1=nw_t[:, cc:cc + 1], scalar2=None, op0=ALU.mult))(cc),
                waits=[f"axh{cc}", "c0"], tag=f"axab{cc}", inc=("V", 1))
            P.step("V", (lambda cc: lambda v: v.tensor_reduce(
                out=amall[:, cc:cc + 1], in_=xhsc, axis=AX, op=ALU.max, apply_absolute_value=True))(cc),
                waits=[f"axh{cc}"], tag=f"aam{cc}", inc=("V", 1), pad=True)
        P.step("V", lambda v: v.tensor_tensor(out=amall, in0=amall, in1=gfa_t[:, :], op=ALU.mult),
               waits=[f"aam{cc}" for cc in range(16)] + ["gfab", "axab15"], tag="agm_m", inc=("V", 1), pad=True)
        P.step("V", lambda v: v.tensor_reduce(out=gmax, in_=amall, axis=AX, op=ALU.max),
               waits=["agm_m"], tag="agm", inc=("V", 1), pad=True)
        APPLY_DONE = "agm"
        if debug:
            P.step("G", lambda g: g.dma_start(out=dbg["xab"], in_=xab),
                   waits=["axab15", "axab14"], tag="dbgxab", inc=("G", 16))

        P.step("V", lambda v: v.memset(ccst_t[:, :], NEG), waits=[], tag="ccm", inc=("V", 1))
        P.step("V", lambda v: v.tensor_copy(out=ccst_t[:, 0:1], in_=gmax),
               waits=["ccm", APPLY_DONE], tag="ccc", inc=("V", 1), pad=True)

        wsh = [wst[:, 0:4096], wst[:, 4096:8192]]
        nvg = 8
        for g in range(nvg):
            w = [f"vcv{g-2}"] if g >= 2 else []
            src = wqkvT[:, 2 * D + 256 * g:2 * D + 256 * (g + 1)].rearrange("(c p) m -> p c m", p=128)
            dst3 = wsh[g % 2].rearrange("p (c m) -> p c m", m=256)
            P.step("Y", (lambda dst3, src: lambda y: y.dma_start(out=dst3, in_=src))(dst3, src),
                   waits=w, tag=f"vld{g}", inc=("Y", 16))
            wguard = [APPLY_DONE] if g < 2 else [f"vmm{g-2}_7"]
            cvt = []
            for i in range(16):
                engx = ["G", "Sc", "G"][i % 3]
                P.step(engx, (lambda g, i: lambda en: ecopy(en,
                    wbf[g % 2][:, 256 * i:256 * (i + 1)], wsh[g % 2][:, 256 * i:256 * (i + 1)]))(g, i),
                    waits=[f"vld{g}"] + wguard, tag=f"vcv{g}_{i}", inc=(engx, 1))
                cvt.append(f"vcv{g}_{i}")
            joiner(f"vcv{g}", cvt)
            for tc in range(8):
                bank = tc % 4

                def vmm(g, tc, bank):
                    def f(t):
                        r = None
                        for cc in range(16):
                            r = t.matmul(out=pb[bank][:, 0:256],
                                         lhsT=xab[:, T * cc + 128 * tc:T * cc + 128 * (tc + 1)],
                                         rhs=wbf[g % 2][:, 256 * cc:256 * (cc + 1)],
                                         start=(cc == 0), stop=(cc == 15))
                        return r
                    return f
                w = [f"vcv{g}", APPLY_DONE]
                if g > 0 or tc >= 4:
                    pg, ptc = (g, tc - 4) if tc >= 4 else (g - 1, tc + 4)
                    w.append(f"vev{pg}_{ptc}")
                P.step("Tm", vmm(g, tc, bank), waits=w, tag=f"vmm{g}_{tc}", inc=("Tm", 1))
                P.step("V", (lambda g, tc, bank: lambda v: v.tensor_copy(
                    out=vsb[:, D * tc + 256 * g:D * tc + 256 * (g + 1)], in_=pb[bank][:, 0:256]))(g, tc, bank),
                    waits=[f"vmm{g}_{tc}"], tag=f"vev{g}_{tc}", inc=("V", 1))
        if debug:
            P.step("G", lambda g: g.dma_start(out=dbg["v"], in_=vsb),
                   waits=[f"vev{nvg-1}_{tc}" for tc in range(8)], tag="dbgv", inc=("G", 16))

        P.step("V", lambda v: v.memset(sa[0][:, :], -1.0e30),
               waits=["erho0", "erho1", f"vev{nvg-1}_6"], tag="sainit0", inc=("V", 1))
        P.step("V", lambda v: v.memset(sa[1][:, :], -1.0e30),
               waits=["erho0", "erho1", f"vev{nvg-1}_7"], tag="sainit1", inc=("V", 1))
        sa_user = {0: "sainit0", 1: "sainit1"}
        ptp_user = {0: None, 1: None}
        ctx_user = {0: None, 1: None}
        negm_user = {0: None, 1: None}
        qkbank_ev = {0: f"vev{nvg-1}_6", 1: f"vev{nvg-1}_7"}
        rot_user = {0: None}
        av_prev = [None]
        rope_reader = {}

        def emit_qld(hh):
            hhp = hh % 2
            wq_src = wqkvT[:, 128 * hh:128 * (hh + 1)].rearrange("(c p) m -> p c m", p=128)
            wk_src = wqkvT[:, D + 128 * hh:D + 128 * (hh + 1)].rearrange("(c p) m -> p c m", p=128)
            dsth = wsh[hhp].rearrange("p (c m) -> p c m", m=256)
            wwait = [f"vcv{6+hhp}"] if hh < 2 else [f"qcv{hh-2}"]
            if 2 <= hh <= 9:
                wwait.append(f"swa{hh-2}b")
            P.step("Y", (lambda dsth, wq_src: lambda y: y.dma_start(out=dsth[:, :, 0:128], in_=wq_src))(dsth, wq_src),
                   waits=wwait, tag=f"qld{hh}a", inc=("Y", 16))
            P.step("Y", (lambda dsth, wk_src: lambda y: y.dma_start(out=dsth[:, :, 128:256], in_=wk_src))(dsth, wk_src),
                   waits=[], tag=f"qld{hh}", inc=("Y", 16))

        for h in range(16):
            hp = h % 2
            ah = h - 1

            def attn_A(ah, pos):
                w_ = pos % 2
                qc = pos // 2
                par = w_
                ntok = 128 * (qc + 1)
                ahp = ah % 2
                qslice = qh[ahp][:, 512 * w_ + 128 * qc:512 * w_ + 128 * (qc + 1)]
                kslice = kh[ahp][:, 512 * w_:512 * w_ + ntok]
                wts = [f"ropeq{ah}", f"ropek{ah}", "c2", "c12", "c13", "c7", sa_user[par]]
                msk = mfir_t if qc == 0 else mdia_t

                def qkmm(qslice, kslice, par, ntok, ah, qc, msk):
                    def f(t):
                        t.matmul(out=sa[par][:, 0:ntok], lhsT=qslice, rhs=kslice,
                                 start=True, stop=True)
                        mm = t.matmul(out=sa[par][:, 128 * qc:ntok], lhsT=idb_t[:, :],
                                      rhs=msk[:, :], start=False, stop=True)
                        if qc < 3:
                            mm = t.matmul(out=sa[par][:, ntok:ntok + 16], lhsT=qslice,
                                          rhs=pmk_t[:, 16 * ah:16 * (ah + 1)], start=True, stop=True)
                        return mm
                    return f
                P.step("Tm", qkmm(qslice, kslice, par, ntok, ah, qc, msk),
                       waits=wts, tag=f"qk{ah}_{w_}_{qc}", inc=("Tm", 1))
                area = ntok + 16 if qc < 3 else 512
                P.step("V", (lambda par, area: lambda v: v.tensor_reduce(
                    out=negm_c[par], in_=sa[par][:, 0:area], axis=AX, op=ALU.max, negate=True))(par, area),
                    waits=[f"qk{ah}_{w_}_{qc}"] + ([negm_user[par]] if negm_user[par] else []),
                    tag=f"ngm{ah}_{w_}_{qc}", inc=("V", 1), pad=True)

            def attn_PM3(ah, pos):
                w_ = pos % 2
                par = w_
                pm_bank, pm_c0 = 1 - par, 480
                ahp = ah % 2
                qc = pos // 2
                qslice = qh[ahp][:, 512 * w_ + 128 * qc:512 * w_ + 128 * (qc + 1)]
                P.step("Tm", (lambda qslice, pm_bank, pm_c0, ah: lambda t: t.matmul(
                    out=sa[pm_bank][:, pm_c0:pm_c0 + 16], lhsT=qslice,
                    rhs=pmk_t[:, 16 * ah:16 * (ah + 1)], start=True, stop=True))(qslice, pm_bank, pm_c0, ah),
                    waits=[f"qk{ah}_{w_}_3", sa_user[1 - par]], tag=f"pm3{ah}_{w_}", inc=("Tm", 1))

            def attn_B(ah, pos):
                w_ = pos % 2
                qc = pos // 2
                par = w_
                ntok = 128 * (qc + 1)
                pm_bank, pm_c0 = (par, ntok) if qc < 3 else (1 - par, 480)
                expw = [f"ngm{ah}_{w_}_{qc}"]
                if qc == 3:
                    expw.append(f"pm3{ah}_{w_}")
                prev_pn = (f"pn{ah}_{w_}_{qc-1}" if qc >= 1 else
                           (f"pn{ah-1}_{w_}_3" if ah > 0 else None))
                if prev_pn:
                    expw.append(prev_pn)

                def expf(par, ntok, pm_bank, pm_c0):
                    def f(s):
                        s.activation(out=p_sb[par][:, 16:16 + ntok], in_=sa[par][:, 0:ntok],
                                     func=AF.Exp, bias=negm_c[par], scale=1.0)
                        return s.activation(out=p_sb[par][:, 0:16], in_=sa[pm_bank][:, pm_c0:pm_c0 + 16],
                                            func=AF.Exp, bias=negm_c[par], scale=1.0)
                    return f
                P.step("Sc", expf(par, ntok, pm_bank, pm_c0),
                       waits=expw, tag=f"exp{ah}_{w_}_{qc}", inc=("Sc", 1))
                sa_user[par] = f"exp{ah}_{w_}_{qc}"
                if qc == 3:
                    sa_user[1 - par] = f"exp{ah}_{w_}_{qc}"
                negm_user[par] = f"exp{ah}_{w_}_{qc}"

            def attn_C(ah, pos):
                w_ = pos % 2
                qc = pos // 2
                par = w_
                ntok = 128 * (qc + 1)
                P.step("V", (lambda par, ntok: lambda v: v.tensor_reduce(
                    out=zs[par], in_=p_sb[par][:, 0:16 + ntok], axis=AX, op=ALU.add))(par, ntok),
                    waits=[f"exp{ah}_{w_}_{qc}"], tag=f"zz{ah}_{w_}_{qc}", inc=("V", 1), pad=True)
                P.step("V", (lambda par: lambda v: v.reciprocal(out=rz[par], in_=zs[par]))(par),
                       waits=[f"zz{ah}_{w_}_{qc}"], tag=f"rzt{ah}_{w_}_{qc}", inc=("V", 1), pad=True, dup=True)
                pnw = [f"rzt{ah}_{w_}_{qc}"]
                prev_tp = (f"tp{ah}_{w_}_{qc-1}" if qc >= 1 else
                           (f"tp{ah-1}_{w_}_3" if ah > 0 else None))
                if prev_tp:
                    pnw.append(prev_tp)
                P.step("V", (lambda par, ntok: lambda v: v.tensor_scalar(
                    out=p_n[par][:, 0:16 + ntok], in0=p_sb[par][:, 0:16 + ntok],
                    scalar1=rz[par], scalar2=None, op0=ALU.mult))(par, ntok),
                    waits=pnw, tag=f"pn{ah}_{w_}_{qc}", inc=("V", 1))
                if debug:
                    P.step("G", (lambda ah, w_, qc, par: lambda g: g.dma_start(
                        out=dbg["p"][ah, w_, qc], in_=p_n[par]))(ah, w_, qc, par),
                        waits=[f"pn{ah}_{w_}_{qc}"], tag=f"dbgp{ah}_{w_}_{qc}", inc=("G", 16))
                ntp = qc + 2
                tpe_tags = []
                for jj in range(ntp):
                    tpar = jj % 2
                    tw = [f"pn{ah}_{w_}_{qc}", "c7"]
                    if ptp_user[tpar]:
                        tw.append(ptp_user[tpar])
                    if jj == 0:
                        srcp = p_n[par][:, 0:16]
                        dstp = ptp[tpar][0:16, :]
                        dste = pmpt[w_][:, 128 * qc:128 * (qc + 1)]
                    else:
                        j = jj - 1
                        srcp = p_n[par][:, 16 + 128 * j:16 + 128 * (j + 1)]
                        dstp = ptp[tpar][:, :]
                        dste = ptj[w_][j][:, 128 * (qc - j):128 * (qc - j + 1)]
                    P.step("Tm", (lambda srcp, dstp: lambda t: t.transpose(
                        out=dstp, in_=srcp, identity=idb_t[:, :]))(srcp, dstp),
                        waits=tw, tag=f"tpp{ah}_{w_}_{qc}_{jj}", inc=("Tm", 1))
                    ew = [f"tpp{ah}_{w_}_{qc}_{jj}"]
                    if av_prev[0]:
                        ew.append(av_prev[0])
                    if jj % 2 == 0:
                        P.step("Sc", (lambda dstp, dste: lambda s: s.copy(out=dste, in_=dstp))(dstp, dste),
                               waits=ew, tag=f"tpe{ah}_{w_}_{qc}_{jj}", inc=("Sc", 1))
                    else:
                        P.step("V", (lambda dstp, dste: lambda v: v.tensor_copy(out=dste, in_=dstp))(dstp, dste),
                               waits=ew, tag=f"tpe{ah}_{w_}_{qc}_{jj}", inc=("V", 1))
                    ptp_user[tpar] = f"tpe{ah}_{w_}_{qc}_{jj}"
                    tpe_tags.append(f"tpe{ah}_{w_}_{qc}_{jj}")
                joiner(f"tp{ah}_{w_}_{qc}", tpe_tags)

            def attn_AV(ah):
                for w_ in range(2):
                    avw = [f"tp{ah}_{w_}_3", "c3"]
                    avw += [f"vev{ah // 2}_{4 * w_ + j}" for j in range(4)]
                    if ctx_user[0]:
                        avw.append(ctx_user[0])

                    def avf(ah, w_):
                        def f(t):
                            t.matmul(out=ctxps[0][:, :], lhsT=pmv_t[:, 128 * ah:128 * (ah + 1)],
                                     rhs=pmpt[w_][:, :], start=True, stop=False)
                            r = None
                            for j in range(4):
                                tcj = 4 * w_ + j
                                nq = 512 - 128 * j
                                r = t.matmul(out=ctxps[0][:, 128 * j:512],
                                             lhsT=vsb[:, D * tcj + 128 * ah:D * tcj + 128 * (ah + 1)],
                                             rhs=ptj[w_][j][:, 0:nq],
                                             start=False, stop=(j == 3))
                            return r
                        return f
                    P.step("Tm", avf(ah, w_), waits=avw, tag=f"av{ah}_{w_}", inc=("Tm", 1))
                    av_prev[0] = f"av{ah}_{w_}"
                    P.step("V", (lambda ah, w_: lambda v: v.tensor_copy(
                        out=ctxb[:, T * ah + 512 * w_:T * ah + 512 * (w_ + 1)], in_=ctxps[0][:, :]))(ah, w_),
                        waits=[f"av{ah}_{w_}"], tag=f"cev{ah}_{w_}", inc=("V", 1))
                    ctx_user[0] = f"cev{ah}_{w_}"
                rope_reader[ah % 2] = f"pm3{ah}_1"

            def rnd(r, ah):
                if ah < 0:
                    return
                if r == 7:
                    attn_PM3(ah, 6)
                if r == 8:
                    attn_PM3(ah, 7)
                if 1 <= r <= 8:
                    attn_B(ah, r - 1)
                if r < 8:
                    attn_A(ah, r)
                if 2 <= r:
                    attn_C(ah, r - 2)

            def qmm_step(part, name, tt):
                bank = tt
                qt_b = qtmp_b[part]

                def qmm(hp, part, tt, bank):
                    def f(t):
                        r = None
                        wb = wbf[hp]
                        for cc in range(16):
                            r = t.matmul(out=pb[bank][:, :],
                                         lhsT=wb[:, 256 * cc + 128 * part:256 * cc + 128 * (part + 1)],
                                         rhs=xab[:, T * cc + 512 * tt:T * cc + 512 * (tt + 1)],
                                         start=(cc == 0), stop=(cc == 15))
                        return r
                    return f
                P.step("Tm", qmm(hp, part, tt, bank),
                       waits=[f"qcv{h}", qkbank_ev[bank]],
                       tag=f"qmm{name}{h}_{tt}", inc=("Tm", 1))
                evw = [f"qmm{name}{h}_{tt}"]
                if h > 0:
                    evw.append(f"qrot{name}{h-1}_{tt}")
                if part == 0:
                    P.step("V", (lambda tt, bank, qt_b: lambda v: v.tensor_scalar(
                        out=qt_b[:, 512 * tt:512 * (tt + 1)], in0=pb[bank][:, :],
                        scalar1=CSC, scalar2=None, op0=ALU.mult))(tt, bank, qt_b),
                        waits=evw, tag=f"qev{name}{h}_{tt}", inc=("V", 1))
                else:
                    P.step("V", (lambda tt, bank, qt_b: lambda v: v.tensor_copy(
                        out=qt_b[:, 512 * tt:512 * (tt + 1)], in_=pb[bank][:, :]))(tt, bank, qt_b),
                        waits=evw, tag=f"qev{name}{h}_{tt}", inc=("V", 1))
                qkbank_ev[bank] = f"qev{name}{h}_{tt}"

            def rot_step(part, name, tt):
                qt_b = qtmp_b[part]
                ro_b = rots_b[part]
                rw_ = [f"qev{name}{h}_{tt}", "c6"]
                if rot_user[0]:
                    rw_.append(rot_user[0])
                P.step("Tm", (lambda tt, qt_b: lambda t: t.matmul(
                    out=rotps[:, :], lhsT=ptm_t[:, :], rhs=qt_b[:, 512 * tt:512 * (tt + 1)],
                    start=True, stop=True))(tt, qt_b),
                    waits=rw_, tag=f"qrot{name}{h}_{tt}", inc=("Tm", 1))
                sw_ = [f"qrot{name}{h}_{tt}"]
                if h > 0:
                    sw_.append(f"rope{name}{h-1}")
                P.step("Sc", (lambda tt, ro_b: lambda s: s.copy(
                    out=ro_b[:, 512 * tt:512 * (tt + 1)], in_=rotps[:, :]))(tt, ro_b),
                    waits=sw_, tag=f"qrev{name}{h}_{tt}", inc=("Sc", 1))
                rot_user[0] = f"qrev{name}{h}_{tt}"

            def rt1_step(part, name):
                qt_b = qtmp_b[part]
                P.step("V", (lambda qt_b: lambda v: v.tensor_tensor(
                    out=t1b, in0=qt_b, in1=cosb, op=ALU.mult))(qt_b),
                    waits=[f"qev{name}{h}_1", "c4"], tag=f"rt1{name}{h}", inc=("V", 1))

            def rt2_rope_step(part, name):
                dstbuf = qh[hp] if part == 0 else kh[hp]
                ro_b = rots_b[part]
                P.step("V", (lambda ro_b: lambda v: v.tensor_tensor(
                    out=ro_b, in0=ro_b, in1=sinb, op=ALU.mult))(ro_b),
                    waits=[f"qrev{name}{h}_0", f"qrev{name}{h}_1", "c5"], tag=f"rt2{name}{h}", inc=("V", 1))
                ropew = [f"rt1{name}{h}", f"rt2{name}{h}"]
                if h >= 2 and rope_reader.get(hp):
                    ropew.append(rope_reader[hp])
                P.step("V", (lambda dstbuf, ro_b: lambda v: v.tensor_tensor(
                    out=dstbuf, in0=t1b, in1=ro_b, op=ALU.add))(dstbuf, ro_b),
                    waits=ropew, tag=f"rope{name}{h}", inc=("V", 1))

            rnd(0, ah)
            if h == 0:
                emit_qld(0)
                emit_qld(1)
            elif h + 1 <= 15:
                emit_qld(h + 1)
            cvw = [f"vmm{6+hp}_7"] if h < 2 else [f"qmmk{h-2}_1"]
            cvt = []
            for i in range(16):
                engx = "Sc" if i % 4 == 3 else "G"
                P.step(engx, (lambda hp, i: lambda en: ecopy(en,
                    wbf[hp][:, 256 * i:256 * (i + 1)], wsh[hp][:, 256 * i:256 * (i + 1)]))(hp, i),
                    waits=[f"qld{h}"] + cvw, tag=f"qcv{h}_{i}", inc=(engx, 1))
                cvt.append(f"qcv{h}_{i}")
            joiner(f"qcv{h}", cvt)
            if h <= 7:
                k = h
                half = k % 2
                srcw_ = wffnT[:, 256 * k:256 * (k + 1)].rearrange("(c p) m -> p c m", p=128)
                dstw_ = wsh[half].rearrange("p (c m) -> p c m", m=256)
                P.step("Y", (lambda dstw_, srcw_: lambda y: y.dma_start(out=dstw_, in_=srcw_))(dstw_, srcw_),
                       waits=[f"qcv{k}"] + ([f"swa{k-2}b"] if k >= 2 else []),
                       tag=f"swl{k}", inc=("Y", 16))
            rnd(1, ah)
            qmm_step(0, "q", 0)
            rnd(2, ah)
            qmm_step(0, "q", 1)
            rot_step(0, "q", 0)
            rnd(3, ah)
            qmm_step(1, "k", 0)
            rot_step(0, "q", 1)
            rt1_step(0, "q")
            rnd(4, ah)
            qmm_step(1, "k", 1)
            rot_step(1, "k", 0)
            rt2_rope_step(0, "q")
            rnd(5, ah)
            rot_step(1, "k", 1)
            rt1_step(1, "k")
            rnd(6, ah)
            rt2_rope_step(1, "k")
            rnd(7, ah)
            rnd(8, ah)
            rnd(9, ah)
            if ah >= 0:
                attn_AV(ah)
            if h <= 7:
                k = h
                half = k % 2
                P.step("Sc", (lambda half, k: lambda s: s.activation(
                    out=outsb.bitcast(BF16)[:, 0:2048], in_=wsh[half][:, 0:2048], func=AF.Abs,
                    accum_out=swacc[k]))(half, k),
                    waits=[f"swl{k}"] + ([f"swv{k-1}"] if k >= 1 else []), tag=f"swa{k}a", inc=("Sc", 1))
                P.step("Sc", (lambda half, k: lambda s: s.activation(
                    out=outsb.bitcast(BF16)[:, 0:2048], in_=wsh[half][:, 2048:4096], func=AF.Abs,
                    accum_out=swacc_b))(half, k),
                    waits=[f"swa{k}a"], tag=f"swa{k}b", inc=("Sc", 1), pad=True)
                P.step("V", (lambda k: lambda v: v.tensor_tensor(
                    out=swacc[k], in0=swacc[k], in1=swacc_b, op=ALU.add))(k),
                    waits=[f"swa{k}b"], tag=f"swv{k}", inc=("V", 1), pad=True)
            if debug:
                P.step("G", (lambda h: lambda g: g.dma_start(out=dbg["qh"][h], in_=qh[h % 2]))(h),
                       waits=[f"ropeq{h}"], tag=f"dbgq{h}", inc=("G", 16))
                P.step("G", (lambda h: lambda g: g.dma_start(out=dbg["kh"][h], in_=kh[h % 2]))(h),
                       waits=[f"ropek{h}"], tag=f"dbgk{h}", inc=("G", 16))
        for r in range(10):
            rnd(r, 15)
        attn_AV(15)

        if debug:
            P.step("G", lambda g: g.dma_start(out=dbg["ctx"], in_=ctxb),
                   waits=["cev15_1", "cev15_0"], tag="dbgctx", inc=("G", 16))

        P.step("G", lambda g: g.dma_start(out=cc_in, in_=ccst_t[:, :]), waits=["ccc"], tag="ccw", inc=("G", 16))
        P.step("G", lambda g: g.collective_compute(
            "AllReduce", ALU.max, replica_groups=[list(range(NCORES))],
            ins=[cc_in.opt()], outs=[cc_out.opt()]),
            waits=["ccw"], tag="ccr", inc=("G", 1))
        P.step("G", lambda g: g.dma_start(out=ccst_t[:, :], in_=cc_out),
               waits=["ccr"], tag="ccb", inc=("G", 16))

        SA0_FIN = sa_user[0]
        SA1_FIN = sa_user[1]
        QK0_FIN = qkbank_ev[0]
        QK1_FIN = qkbank_ev[1]

        P.step("V", lambda v: v.tensor_reduce(out=swsum, in_=smal_t[:, 15:23], axis=AX, op=ALU.add),
               waits=[f"swv{k}" for k in range(8)], tag="sws", inc=("V", 1), pad=True)
        P.step("Tm", lambda t: t.matmul(out=pb[0][0:1, 0:1], lhsT=swsum, rhs=onc_t[:, :],
                                        start=True, stop=True),
               waits=["sws", "c11", QK0_FIN], tag="swm", inc=("Tm", 1), pad=True)
        P.step("V", lambda v: v.tensor_scalar(out=sw_s[0:1, :], in0=pb[0][0:1, 0:1],
                                              scalar1=1.0 / (2048.0 * 2048.0), scalar2=None, op0=ALU.mult),
               waits=["swm"], tag="swv", inc=("V", 1), pad=True)
        P.step("Tm", lambda t: t.transpose(out=pb[1][0:1, 0:128], in_=ccst_t[:, 0:1], identity=idf_t[:, :]),
               waits=["ccb", "c8", QK1_FIN], tag="qst", inc=("Tm", 1), pad=True)
        P.step("V", lambda v: v.tensor_reduce(out=tmpa[0:1, :], in_=pb[1][0:1, 0:128], axis=AX, op=ALU.max),
               waits=["qst"], tag="qsr", inc=("V", 1), pad=True)
        P.step("V", lambda v: v.tensor_scalar(out=tmpa[0:1, :], in0=tmpa[0:1, :], scalar1=1e-5,
                                              scalar2=None, op0=ALU.max),
               waits=["qsr"], tag="qse", inc=("V", 1), pad=True)
        P.step("Tm", lambda t: t.matmul(out=pb[0][:, 0:1], lhsT=onf_t[:, :], rhs=tmpa[0:1, :],
                                        start=True, stop=True),
               waits=["qse", "swv", "c10"], tag="qsb", inc=("Tm", 1), pad=True)
        P.step("V", lambda v: v.tensor_copy(out=sc_bc, in_=pb[0][:, 0:1]),
               waits=["qsb"], tag="qsc", inc=("V", 1), pad=True)
        P.step("Tm", lambda t: t.matmul(out=pb[1][:, 0:1], lhsT=onf_t[:, :], rhs=sw_s[0:1, :], start=True, stop=True),
               waits=["qsc", "swv"], tag="qwb", inc=("Tm", 1), pad=True)
        P.step("V", lambda v: v.tensor_copy(out=tmpb, in_=pb[1][:, 0:1]),
               waits=["qwb"], tag="qwc", inc=("V", 1), pad=True)
        P.step("V", lambda v: v.reciprocal(out=rz[0], in_=sc_bc), waits=["qsc", "pn15_1_3"], tag="qrc", inc=("V", 1), pad=True)
        P.step("V", lambda v: v.tensor_scalar(out=s127, in0=rz[0], scalar1=127.0, scalar2=None, op0=ALU.mult),
               waits=["qrc"], tag="qs127", inc=("V", 1), pad=True)
        P.step("V", lambda v: v.reciprocal(out=rw_bc, in_=tmpb), waits=["qwc"], tag="qrw", inc=("V", 1), pad=True)
        P.step("V", lambda v: v.tensor_tensor(out=gamc, in0=sc_bc, in1=tmpb, op=ALU.mult),
               waits=["qwc"], tag="qg1", inc=("V", 1), pad=True)
        P.step("V", lambda v: v.tensor_scalar(out=gamc, in0=gamc, scalar1=1.0 / 127.0, scalar2=None, op0=ALU.mult),
               waits=["qg1"], tag="qgam", inc=("V", 1), pad=True)
        if debug:
            def dbgsc_f(v):
                v.tensor_copy(out=outsb[:, 0:1], in_=sc_bc)
                v.tensor_copy(out=outsb[:, 1:2], in_=tmpb)
                v.tensor_copy(out=outsb[:, 2:3], in_=gamc)
                v.tensor_copy(out=outsb[:, 3:4], in_=s127)
                for kk in range(8):
                    v.tensor_copy(out=outsb[:, 5 + kk:6 + kk], in_=swacc[kk])
                v.tensor_copy(out=outsb[:, 13:14], in_=swsum)
                return v.tensor_copy(out=outsb[:, 4:5], in_=gmax)
            P.step("V", dbgsc_f, waits=["qgam", "qs127", "swa7b"], tag="dbgscv", inc=("V", 1))
            P.step("G", lambda g: g.dma_start(out=dbg["sc"], in_=outsb[:, 0:16]),
                   waits=["dbgscv"], tag="dbgsc", inc=("G", 16))

        XAB_FREE = "qmmk15_1"
        for tt in range(2):
            P.step("Tm", bcmm(tt, mu_r, tt), waits=["qsb", "qsc", XAB_FREE], tag=f"b2mu{tt}", inc=("Tm", 1), pad=True)
            P.step("Tm", bcmm(2 + tt, rho_r, tt), waits=[SA0_FIN, SA1_FIN, "qwc"], tag=f"b2rho{tt}", inc=("Tm", 1), pad=True)
        for tt in range(2):
            P.step("V", (lambda tt: lambda v: v.tensor_copy(
                out=mubc2[:, 512 * tt:512 * (tt + 1)], in_=pb[tt][:, :]))(tt),
                waits=[f"b2mu{tt}"], tag=f"e2mu{tt}", inc=("V", 1))
            P.step("V", (lambda tt: lambda v: v.tensor_copy(
                out=rhobc2[:, 512 * tt:512 * (tt + 1)], in_=pb[2 + tt][:, :]))(tt),
                waits=[f"b2rho{tt}"], tag=f"e2rho{tt}", inc=("V", 1))

        xqb = vsb
        for cc in range(16):
            w = [f"exh{cc-2}"] if cc >= 2 else ["e2mu0", "e2mu1", "e2rho0", "e2rho1"]
            P.step("Y", (lambda cc: lambda y: y.dma_start(
                out=xsc3[cc % 2][:, :], in_=x_t[128 * cc:128 * (cc + 1), :]))(cc),
                waits=w, tag=f"exl{cc}", inc=("Y", 16))
            P.step("V", (lambda cc: lambda v: v.tensor_tensor(
                out=tsc3, in0=xsc3[cc % 2][:, :], in1=mubc2, op=ALU.subtract))(cc),
                waits=[f"exl{cc}"] + ([f"exq{cc-1}"] if cc > 0 else []), tag=f"esb{cc}", inc=("V", 1))
            P.step("V", (lambda cc: lambda v: v.tensor_tensor(
                out=xhsc3, in0=tsc3, in1=rhobc2, op=ALU.mult))(cc),
                waits=[f"esb{cc}"], tag=f"exh{cc}", inc=("V", 1))
            P.step("V", (lambda cc: lambda v: v.tensor_scalar(
                out=xgsc, in0=xhsc3, scalar1=gf_t[:, cc:cc + 1],
                scalar2=s127, op0=ALU.mult, op1=ALU.mult))(cc),
                waits=[f"exh{cc}", "qs127", "c1"], tag=f"exg{cc}", inc=("V", 1))
            iw = [f"exg{cc}"]
            if cc == 0:
                iw += ["av15_1", "cev15_1"]
            P.step("V", (lambda cc: lambda v: v.tensor_copy(out=xisc, in_=xgsc))(cc),
                   waits=iw, tag=f"exi{cc}", inc=("V", 1))
            P.step("V", (lambda cc: lambda v: v.tensor_copy(
                out=xqb[:, T * cc:T * (cc + 1)], in_=xisc))(cc),
                waits=[f"exi{cc}"], tag=f"exq{cc}", inc=("V", 1))
        if debug:
            P.step("G", lambda g: g.dma_start(out=dbg["xqb"], in_=xqb),
                   waits=["exq15"], tag="dbgxqb", inc=("G", 16))

        wo_user = {0: "swa6b", 1: "swa7b"}
        wob_user = {0: "qmmk14_1", 1: "qmmk15_1"}
        wfb_user = {0: "tp15_1_3", 1: "tp15_1_3"}
        pbD_user = {0: "e2mu0", 1: "e2mu1"}
        pbF_user = {0: "e2rho0", 1: "e2rho1"}
        for og in range(8):
            par = og % 2
            srco = woutT[:, 256 * og:256 * (og + 1)].rearrange("(c p) m -> p c m", p=128)
            srcf = wffnT[:, 256 * og:256 * (og + 1)].rearrange("(c p) m -> p c m", p=128)
            dsto = wsh[par].rearrange("p (c m) -> p c m", m=256)
            dstf = wsh[1 - par].rearrange("p (c m) -> p c m", m=256)
            P.step("Y", (lambda dsto, srco: lambda y: y.dma_start(out=dsto, in_=srco))(dsto, srco),
                   waits=[wo_user[par]], tag=f"dol{og}", inc=("Y", 16))
            P.step("Y", (lambda dstf, srcf: lambda y: y.dma_start(out=dstf, in_=srcf))(dstf, srcf),
                   waits=[wo_user[1 - par]], tag=f"dfl{og}", inc=("Y", 16))
            cvt = []
            for i in range(16):
                engx = ["G", "Sc"][i % 2]
                P.step(engx, (lambda par, i: lambda en: ecopy(en,
                    wob[par][:, 256 * i:256 * (i + 1)], wsh[par][:, 256 * i:256 * (i + 1)]))(par, i),
                    waits=[f"dol{og}", wob_user[par]], tag=f"dcv{og}_{i}", inc=(engx, 1))
                cvt.append(f"dcv{og}_{i}")
            joiner(f"dcv{og}", cvt)
            wo_user[par] = f"dcv{og}"
            fcvt = []
            for i in range(8):
                c0 = 512 * i
                srcw = wsh[1 - par][:, c0:c0 + 512]
                prevd = f"f2_{og}_{i-1}" if i > 0 else (f"f2_{og-1}_7" if og > 0 else None)
                prevc = f"f3_{og}_{i-1}" if i > 0 else (f"f3_{og-1}_7" if og > 0 else None)
                w1 = [f"dfl{og}", "qrw", wfb_user[par], "exq15"]
                if prevd:
                    w1.append(prevd)
                P.step("V", (lambda srcw: lambda v: v.tensor_scalar(
                    out=xgsc[:, 0:512], in0=srcw, scalar1=rw_bc, scalar2=1.0,
                    op0=ALU.mult, op1=ALU.min))(srcw),
                    waits=w1, tag=f"f1_{og}_{i}", inc=("V", 1))
                w2 = [f"f1_{og}_{i}"]
                if prevc:
                    w2.append(prevc)
                P.step("V", lambda v: v.tensor_scalar(
                    out=tsc3[:, 0:512], in0=xgsc[:, 0:512], scalar1=-1.0, scalar2=None, op0=ALU.max),
                    waits=w2, tag=f"f2_{og}_{i}", inc=("V", 1))
                P.step("V", lambda v: v.tensor_copy(out=xisc[:, 0:512], in_=tsc3[:, 0:512]),
                       waits=[f"f2_{og}_{i}"], tag=f"f3_{og}_{i}", inc=("V", 1))
                P.step("V", (lambda par, i: lambda v: v.tensor_copy(
                    out=wfb[par].rearrange("p (a b) -> p a b", a=8)[:, i, :],
                    in_=xisc[:, 0:512]))(par, i),
                    waits=[f"f3_{og}_{i}"], tag=f"fcv{og}_{i}", inc=("V", 1))
                fcvt.append(f"fcv{og}_{i}")
            joiner(f"fcv{og}", fcvt)
            wo_user[1 - par] = f"fcv{og}"

            for tc in range(8):
                xw = [f"res{og}_{tc-2}"] if tc >= 2 else ([f"res{og-1}_{tc+6}"] if og > 0 else [])
                P.step("G", (lambda og, tc: lambda g: g.dma_start(
                    out=xnst[:, 256 * (tc % 2):256 * (tc % 2) + 256],
                    in_=x_n[128 * tc:128 * (tc + 1), 256 * og:256 * (og + 1)]))(og, tc),
                    waits=xw, tag=f"xnl{og}_{tc}", inc=("G", 16))
                bD = tc % 2
                bF = 2 + tc % 2

                def domm(og, tc, bD, par):
                    def f(t):
                        r = None
                        for hh in range(16):
                            r = t.matmul(out=pb[bD][:, 0:256],
                                         lhsT=ctxb[:, T * hh + 128 * tc:T * hh + 128 * (tc + 1)],
                                         rhs=wob[par][:, 256 * hh:256 * (hh + 1)],
                                         start=(hh == 0), stop=(hh == 15))
                        return r
                    return f

                def femm(og, tc, bF, par):
                    def f(t):
                        r = None
                        for cc in range(16):
                            r = t.matmul(out=pb[bF][:, 0:256],
                                         lhsT=xqb[:, T * cc + 128 * tc:T * cc + 128 * (tc + 1)],
                                         rhs=wfb[par][:, 256 * cc:256 * (cc + 1)],
                                         start=(cc == 0), stop=(cc == 15))
                        return r
                    return f
                P.step("Tm", domm(og, tc, bD, par),
                       waits=[f"dcv{og}", pbD_user[tc % 2], "cev15_1"], tag=f"dmm{og}_{tc}", inc=("Tm", 1))
                P.step("Tm", femm(og, tc, bF, par),
                       waits=[f"fcv{og}", pbF_user[tc % 2], "exq15"], tag=f"fmm{og}_{tc}", inc=("Tm", 1))
                e1w = [f"dmm{og}_{tc}", f"xnl{og}_{tc}"]
                if tc >= 2:
                    e1w.append(f"res{og}_{tc-2}")
                elif og > 0:
                    e1w.append(f"res{og-1}_{tc+6}")
                P.step("V", (lambda og, tc, bD: lambda v: v.tensor_tensor(
                    out=o1b[:, 256 * (tc % 2):256 * (tc % 2) + 256], in0=pb[bD][:, 0:256],
                    in1=xnst[:, 256 * (tc % 2):256 * (tc % 2) + 256], op=ALU.add))(og, tc, bD),
                    waits=e1w, tag=f"ev1{og}_{tc}", inc=("V", 1))
                pbD_user[tc % 2] = f"ev1{og}_{tc}"
                rw2 = [f"ev1{og}_{tc}", f"fmm{og}_{tc}", "qgam"]
                if tc >= 2:
                    rw2.append(f"sto{og}_{tc-2}")
                elif og > 0:
                    rw2.append(f"sto{og-1}_{tc+6}")
                P.step("V", (lambda og, tc, bF: lambda v: v.scalar_tensor_tensor(
                    out=outsb[:, 512 * (tc % 2):512 * (tc % 2) + 256], in0=pb[bF][:, 0:256],
                    scalar=gamc, in1=o1b[:, 256 * (tc % 2):256 * (tc % 2) + 256],
                    op0=ALU.mult, op1=ALU.add))(og, tc, bF),
                    waits=rw2, tag=f"res{og}_{tc}", inc=("V", 1))
                pbF_user[tc % 2] = f"res{og}_{tc}"
                P.step("G", (lambda og, tc: lambda g: g.dma_start(
                    out=out_d[128 * tc:128 * (tc + 1), 256 * og:256 * (og + 1)],
                    in_=outsb[:, 512 * (tc % 2):512 * (tc % 2) + 256]))(og, tc),
                    waits=[f"res{og}_{tc}"], tag=f"sto{og}_{tc}", inc=("G", 16))
            wob_user[par] = f"dmm{og}_7"
            wfb_user[par] = f"fmm{og}_7"

        if debug2:
            ALLST = [f"sto{og}_{tc}" for og in range(8) for tc in range(8)]
            P.step("G", lambda g: g.dma_start(out=dbg["ctx"], in_=ctxb), waits=ALLST, tag="d2a", inc=("G", 16))
            P.step("G", lambda g: g.dma_start(out=dbg["xqb"], in_=xqb), waits=["d2a"], tag="d2b", inc=("G", 16))
            def d2scf(v):
                v.tensor_copy(out=outsb[:, 0:1], in_=sc_bc)
                v.tensor_copy(out=outsb[:, 1:2], in_=tmpb)
                v.tensor_copy(out=outsb[:, 2:3], in_=gamc)
                v.tensor_copy(out=outsb[:, 3:4], in_=s127)
                return v.tensor_copy(out=outsb[:, 4:5], in_=gmax)
            P.step("V", d2scf, waits=ALLST, tag="d2sc", inc=("V", 1))
            P.step("G", lambda g: g.dma_start(out=dbg["sc"], in_=outsb[:, 0:16]), waits=["d2sc"], tag="d2c", inc=("G", 16))
            P.step("G", lambda g: g.dma_start(out=dbg["pn"][0], in_=p_n[0]), waits=["d2c"], tag="d2d", inc=("G", 16))
            P.step("G", lambda g: g.dma_start(out=dbg["pn"][1], in_=p_n[1]), waits=["d2d"], tag="d2e", inc=("G", 16))
            P.step("G", lambda g: g.dma_start(out=dbg["z"], in_=smal_t[:, :]), waits=["d2e"], tag="d2f", inc=("G", 16))
            P.step("G", lambda g: g.dma_start(out=dbg["v"], in_=vsb), waits=["d2f"], tag="d2g", inc=("G", 16))
            P.step("G", lambda g: g.dma_start(out=dbg["st"], in_=stat_t[:, :]), waits=["d2g"], tag="d2h", inc=("G", 16))
            P.step("G", lambda g: g.dma_start(out=dbg["bc"][:, 0:1024], in_=mubc2),
                   waits=["d2h"], tag="d2i0", inc=("G", 16))
            P.step("G", lambda g: g.dma_start(out=dbg["bc"][:, 1024:2048], in_=rhobc2),
                   waits=["d2i0"], tag="d2i", inc=("G", 16))
        P.step("G", None, waits=[f"sto{og}_{tc}" for og in range(8) for tc in range(8)]
               + (["d2i"] if debug2 else []))

        for (e_, fn_, waits_, inc_, pad_, dup_) in P.steps:
            for (sn, val) in waits_:
                assert val <= P.cnt[sn], (sn, val, P.cnt[sn])

        @block.sync
        def _(y):
            P.emit("Y", y, sems)

        @block.vector
        def _(v):
            P.emit("V", v, sems, padfn=lambda en: en.memset(junkv, 0.0))

        @block.scalar
        def _(s):
            def scpad(en):
                en.memzero(junks)
                return en.memzero(junks)
            P.emit("Sc", s, sems, padfn=scpad)

        @block.tensor
        def _(t):
            P.emit("Tm", t, sems,
                   padfn=lambda en: en.transpose(out=ptp[0][0:1, 0:128], in_=onb_t[:, :],
                                                 identity=idb_t[:, :]))

        @block.gpsimd
        def _(g):
            P.emit("G", g, sems, padfn=lambda en: en.memset(junkg, 0.0))

    return nc


def _host_prep(inputs):
    x = np.ascontiguousarray(np.asarray(inputs["L0_attn_x"], dtype=np.float32).reshape(NTOK, D))
    wqkvT = np.ascontiguousarray(np.asarray(inputs["L0_attn_w_qkv"], dtype=np.float32).T)
    woutT = np.ascontiguousarray(np.asarray(inputs["L0_attn_w_out"], dtype=np.float32).T)
    wffnT = np.ascontiguousarray(np.asarray(inputs["L0_ffn_weight"], dtype=np.float32).T)
    nw = np.asarray(inputs["L0_attn_norm_w"], dtype=np.float32)
    gf = np.asarray(inputs["L0_ffn_norm_weight"], dtype=np.float32)
    pm = np.asarray(inputs["L0_attn_pm"], dtype=np.float32)

    nw_f = np.ascontiguousarray(nw.reshape(16, 128).T)
    gf_f = np.ascontiguousarray(gf.reshape(16, 128).T)
    pmk_h = np.ascontiguousarray(pm[0].transpose(2, 0, 1).reshape(128, 256)).astype(ml_dtypes.bfloat16)
    pmv_h = np.ascontiguousarray(pm[1].transpose(1, 0, 2).reshape(16, 2048)).astype(ml_dtypes.bfloat16)

    inv = 1.0 / (10000.0 ** (np.arange(0, DH, 2, dtype=np.float64) / DH))
    ptmat = np.zeros((128, 128), dtype=np.float32)
    for i in range(64):
        ptmat[2 * i, 2 * i + 1] = -1.0
        ptmat[2 * i + 1, 2 * i] = 1.0
    ptm_h = np.ascontiguousarray(ptmat.T).astype(ml_dtypes.bfloat16)
    idb_h = np.eye(128, dtype=np.float32).astype(ml_dtypes.bfloat16)
    idf_h = np.eye(128, dtype=np.float32)
    onb_h = np.ones((128, 1), dtype=np.float32).astype(ml_dtypes.bfloat16)
    onf_h = np.ones((1, 128), dtype=np.float32)
    onc_h = np.ones((128, 1), dtype=np.float32)
    r = np.arange(128)
    c = np.arange(128)
    mdiag = np.where(c[None, :] <= r[:, None], 0.0, -1.0e30).astype(np.float32).astype(ml_dtypes.bfloat16)
    mfirst = np.where((c[None, :] <= r[:, None]) | (c[None, :] < 16), 0.0, -1.0e30).astype(np.float32).astype(ml_dtypes.bfloat16)

    in_maps = []
    for core in range(NCORES):
        sh = x[T * core:T * (core + 1)]
        pos = np.arange(T * core, T * (core + 1), dtype=np.float64)
        fr = pos[:, None] * inv[None, :]
        cosx = np.repeat(np.cos(fr), 2, axis=1).T.astype(np.float32)
        sinx = np.repeat(np.sin(fr), 2, axis=1).T.astype(np.float32)
        in_maps.append(dict(
            x_t=np.ascontiguousarray(sh.T), x_n=np.ascontiguousarray(sh),
            wqkvT=wqkvT, woutT=woutT, wffnT=wffnT, nw_f=nw_f, gf_f=gf_f,
            pmk_h=pmk_h, pmv_h=pmv_h,
            cos_h=np.ascontiguousarray(cosx).astype(ml_dtypes.bfloat16),
            sin_h=np.ascontiguousarray(sinx).astype(ml_dtypes.bfloat16),
            ptm_h=ptm_h, idb_h=idb_h, idf_h=idf_h, onb_h=onb_h, onf_h=onf_h,
            onc_h=onc_h, mfirst_h=mfirst, mdiag_h=mdiag,
        ))
    return in_maps


def kernel(**inputs):
    in_maps = _host_prep(inputs)
    nc = build_nc(debug=DEBUG)
    res = run_bass_kernel_spmd(nc, in_maps, core_ids=list(range(NCORES)))
    out = np.concatenate([r["out"] for r in res.results], axis=0)
    return out.reshape(1, NTOK, D)


if __name__ == "__main__":
    import reference
    inp = reference.setup_inputs(0)
    got = kernel(**{k: np.asarray(v) for k, v in inp.items()})
    ref = np.asarray(reference.reference(**inp))
    err = np.mean((got - ref) ** 2) / np.mean(ref ** 2)
    print("resvar:", err, "maxabs:", np.max(np.abs(got - ref)))


# version 25
# speedup vs baseline: 1.0850x; 1.0392x over previous
import numpy as np
import ml_dtypes
from collections import defaultdict
from contextlib import ExitStack

import concourse.bass as bass
import concourse.mybir as mybir
from concourse.bass_utils import run_bass_kernel_spmd

F32 = mybir.dt.float32
BF16 = mybir.dt.bfloat16
I32 = mybir.dt.int32
AF = mybir.ActivationFunctionType
ALU = mybir.AluOpType
AX = mybir.AxisListType.X

NCORES = 8
T = 1024
D = 2048
H = 16
DH = 128
WIN = 2
CSC = float(DH) ** -0.5
NEG = -3.0e38
NTOK = 8192

DEBUG = False


class Prog:
    def __init__(self):
        self.steps = []
        self.cnt = defaultdict(int)
        self.tags = {}

    def step(self, eng, fn, waits=(), tag=None, inc=None, pad=False, dup=False):
        for t in waits:
            assert t in self.tags, f"wait on undefined tag {t}"
        if inc is not None:
            self.cnt[inc[0]] += inc[1]
            if tag is not None:
                self.tags[tag] = (inc[0], self.cnt[inc[0]])
        else:
            assert tag is None
        self.steps.append((eng, fn, tuple(self.tags[t] for t in waits), inc, pad, dup))

    def emit(self, eng_name, eng, sems, padfn=None):
        for (e, fn, waits, inc, pad, dup) in self.steps:
            if e != eng_name:
                continue
            byname = {}
            for (sname, val) in waits:
                byname[sname] = max(byname.get(sname, 0), val)
            for sname, val in byname.items():
                eng.wait_ge(sems[sname], val)
            if fn is None:
                continue
            inst = fn(eng)
            if dup:
                inst = fn(eng)
            if pad:
                assert padfn is not None
                inst = padfn(eng)
            if inc is not None:
                assert inst is not None
                inst.then_inc(sems[inc[0]], inc[1])


def ecopy(en, out, in_):
    if hasattr(en, "tensor_copy"):
        return en.tensor_copy(out=out, in_=in_)
    return en.copy(out=out, in_=in_)


def build_nc(debug=False, debug2=False):
    nc = bass.Bass("TRN2", target_bir_lowering=False)

    x_t = nc.dram_tensor("x_t", [D, T], F32, kind="ExternalInput").ap()
    x_n = nc.dram_tensor("x_n", [T, D], F32, kind="ExternalInput").ap()
    wqkvT = nc.dram_tensor("wqkvT", [D, 3 * D], F32, kind="ExternalInput").ap()
    woutT = nc.dram_tensor("woutT", [D, D], F32, kind="ExternalInput").ap()
    wffnT = nc.dram_tensor("wffnT", [D, D], F32, kind="ExternalInput").ap()
    nw_f = nc.dram_tensor("nw_f", [128, 16], F32, kind="ExternalInput").ap()
    gf_f = nc.dram_tensor("gf_f", [128, 16], F32, kind="ExternalInput").ap()
    pmk_h = nc.dram_tensor("pmk_h", [128, 256], BF16, kind="ExternalInput").ap()
    pmv_h = nc.dram_tensor("pmv_h", [16, 2048], BF16, kind="ExternalInput").ap()
    cos_h = nc.dram_tensor("cos_h", [128, T], BF16, kind="ExternalInput").ap()
    sin_h = nc.dram_tensor("sin_h", [128, T], BF16, kind="ExternalInput").ap()
    ptm_h = nc.dram_tensor("ptm_h", [128, 128], BF16, kind="ExternalInput").ap()
    idb_h = nc.dram_tensor("idb_h", [128, 128], BF16, kind="ExternalInput").ap()
    idf_h = nc.dram_tensor("idf_h", [128, 128], F32, kind="ExternalInput").ap()
    onb_h = nc.dram_tensor("onb_h", [128, 1], BF16, kind="ExternalInput").ap()
    onf_h = nc.dram_tensor("onf_h", [1, 128], F32, kind="ExternalInput").ap()
    onc_h = nc.dram_tensor("onc_h", [128, 1], F32, kind="ExternalInput").ap()
    mfirst_h = nc.dram_tensor("mfirst_h", [128, 128], BF16, kind="ExternalInput").ap()
    mdiag_h = nc.dram_tensor("mdiag_h", [128, 128], BF16, kind="ExternalInput").ap()
    out_d = nc.dram_tensor("out", [T, D], F32, kind="ExternalOutput").ap()
    cc_in = nc.dram_tensor("cc_in", [128, 128], F32, kind="Internal").ap()
    cc_out = nc.dram_tensor("cc_out", [128, 128], F32, kind="Internal").ap()

    dbg = {}
    if debug2:
        dbg["ctx"] = nc.dram_tensor("dbg_ctx", [128, H * T], BF16, kind="ExternalOutput").ap()
        dbg["xqb"] = nc.dram_tensor("dbg_xqb", [128, 16 * T], BF16, kind="ExternalOutput").ap()
        dbg["sc"] = nc.dram_tensor("dbg_sc", [128, 16], F32, kind="ExternalOutput").ap()
        dbg["pn"] = nc.dram_tensor("dbg_pn", [2, 128, 544], BF16, kind="ExternalOutput").ap()
        dbg["z"] = nc.dram_tensor("dbg_z", [128, 128], F32, kind="ExternalOutput").ap()
        dbg["v"] = nc.dram_tensor("dbg_v", [128, 8 * D], BF16, kind="ExternalOutput").ap()
        dbg["st"] = nc.dram_tensor("dbg_st", [128, 2048], F32, kind="ExternalOutput").ap()
        dbg["bc"] = nc.dram_tensor("dbg_bc", [128, 2048], F32, kind="ExternalOutput").ap()
        dbg["st0"] = nc.dram_tensor("dbg_st0", [1, 2048], F32, kind="ExternalOutput").ap()
        dbg["gfa"] = nc.dram_tensor("dbg_gfa", [128, 16], F32, kind="ExternalOutput").ap()
        dbg["mb"] = nc.dram_tensor("dbg_mb", [128, 1024], F32, kind="ExternalOutput").ap()
        dbg["rb"] = nc.dram_tensor("dbg_rb", [128, 1024], F32, kind="ExternalOutput").ap()
    if debug:
        dbg["xab"] = nc.dram_tensor("dbg_xab", [128, 16 * T], BF16, kind="ExternalOutput").ap()
        dbg["qh"] = nc.dram_tensor("dbg_qh", [H, 128, T], BF16, kind="ExternalOutput").ap()
        dbg["kh"] = nc.dram_tensor("dbg_kh", [H, 128, T], BF16, kind="ExternalOutput").ap()
        dbg["v"] = nc.dram_tensor("dbg_v", [128, 8 * D], BF16, kind="ExternalOutput").ap()
        dbg["st"] = nc.dram_tensor("dbg_st", [128, 2048], F32, kind="ExternalOutput").ap()
        dbg["bc"] = nc.dram_tensor("dbg_bc", [128, 2048], F32, kind="ExternalOutput").ap()
        dbg["st0"] = nc.dram_tensor("dbg_st0", [1, 2048], F32, kind="ExternalOutput").ap()
        dbg["gfa"] = nc.dram_tensor("dbg_gfa", [128, 16], F32, kind="ExternalOutput").ap()
        dbg["mb"] = nc.dram_tensor("dbg_mb", [128, 1024], F32, kind="ExternalOutput").ap()
        dbg["rb"] = nc.dram_tensor("dbg_rb", [128, 1024], F32, kind="ExternalOutput").ap()
        dbg["p"] = nc.dram_tensor("dbg_p", [H, WIN, 4, 128, 544], BF16, kind="ExternalOutput").ap()
        dbg["ctx"] = nc.dram_tensor("dbg_ctx", [128, H * T], BF16, kind="ExternalOutput").ap()
        dbg["xqb"] = nc.dram_tensor("dbg_xqb", [128, 16 * T], BF16, kind="ExternalOutput").ap()
        dbg["sc"] = nc.dram_tensor("dbg_sc", [128, 16], F32, kind="ExternalOutput").ap()

    with ExitStack() as ctx:
        e = ctx.enter_context

        xab_t = e(nc.sbuf_tensor("xab", [128, 16 * T], BF16))
        vsb_t = e(nc.sbuf_tensor("vsb", [128, 8 * D], BF16))
        ctx_t = e(nc.sbuf_tensor("ctxb", [128, H * T], BF16))
        wst_t = e(nc.sbuf_tensor("wstage", [128, 8192], F32))
        wbf0_t = e(nc.sbuf_tensor("wbf0", [128, 4096], BF16))
        wbf1_t = e(nc.sbuf_tensor("wbf1", [128, 4096], BF16))
        mega_t = e(nc.sbuf_tensor("mega", [128, 10240], BF16))
        stat_t = e(nc.sbuf_tensor("statsl", [128, 2048], F32))
        smal_t = e(nc.sbuf_tensor("smal", [128, 128], F32))
        outs_t = e(nc.sbuf_tensor("outsb", [128, 1024], F32))
        xnst_t = e(nc.sbuf_tensor("xnst", [128, 512], F32))
        o1b_t = e(nc.sbuf_tensor("o1b", [128, 512], F32))
        mfir_t = e(nc.sbuf_tensor("mfirst", [128, 128], BF16))
        mdia_t = e(nc.sbuf_tensor("mdiag", [128, 128], BF16))
        nw_t = e(nc.sbuf_tensor("nwsb", [128, 16], F32))
        gf_t = e(nc.sbuf_tensor("gfsb", [128, 16], F32))
        gfa_t = e(nc.sbuf_tensor("gfab", [128, 16], F32))
        pmk_t = e(nc.sbuf_tensor("pmk", [128, 256], BF16))
        pmv_t = e(nc.sbuf_tensor("pmv", [16, 2048], BF16))
        ptm_t = e(nc.sbuf_tensor("ptm", [128, 128], BF16))
        idb_t = e(nc.sbuf_tensor("idb", [128, 128], BF16))
        idf_t = e(nc.sbuf_tensor("idf", [128, 128], F32))
        onb_t = e(nc.sbuf_tensor("onb", [128, 1], BF16))
        onf_t = e(nc.sbuf_tensor("onf", [1, 128], F32))
        onc_t = e(nc.sbuf_tensor("onc", [128, 1], F32))
        ccst_t = e(nc.sbuf_tensor("ccst", [128, 128], F32))
        pmpt_t = [e(nc.sbuf_tensor(f"pmpt{w}", [16, 512], BF16)) for w in range(2)]
        ptj_t = [[e(nc.sbuf_tensor(f"ptj{w}_{j}", [128, 512 - 128 * j], BF16)) for j in range(4)]
                 for w in range(2)]
        qtmp_t = e(nc.sbuf_tensor("qtmp", [128, 1024], BF16))
        qtmk_t = e(nc.sbuf_tensor("qtmk", [128, 1024], BF16))
        rots_t = e(nc.sbuf_tensor("rotsb", [128, 1024], BF16))
        rotk_t = e(nc.sbuf_tensor("rotkb", [128, 1024], BF16))
        t1b_t = e(nc.sbuf_tensor("t1b", [128, 1024], BF16))

        xab = xab_t[:, :]
        vsb = vsb_t[:, :]
        ctxb = ctx_t[:, :]
        wst = wst_t[:, :]
        mega = mega_t[:, :]
        outsb = outs_t[:, :]
        xnst = xnst_t[:, :]
        o1b = o1b_t[:, :]
        qtmp_b = [qtmp_t[:, :], qtmk_t[:, :]]
        rots_b = [rots_t[:, :], rotk_t[:, :]]
        t1b = t1b_t[:, :]
        wbf = [wbf0_t[:, :], wbf1_t[:, :]]
        ptj = [[p[:, :] for p in row] for row in ptj_t]
        pmpt = [p[:, :] for p in pmpt_t]

        qh = [mega[:, 0:1024], mega[:, 1024:2048]]
        kh = [mega[:, 2048:3072], mega[:, 3072:4096]]
        cosb = mega[:, 4096:5120]
        sinb = mega[:, 5120:6144]
        p_sb = [mega[:, 6144:6688], mega[:, 6688:7232]]
        p_n = [mega[:, 7232:7776], mega[:, 7776:8320]]
        wfb = [mega[:, 0:4096], mega[:, 4096:8192]]
        wob = [wbf[0], wbf[1]]

        xsc = [ctxb[:, 2048 * s:2048 * (s + 1)].bitcast(F32) for s in range(4)]
        tsc = ctxb[:, 8192:10240].bitcast(F32)
        xhsc = ctxb[:, 10240:12288].bitcast(F32)
        xbsc = ctxb[:, 12288:13312]
        sqsc = ctxb[:, 13312:14336]
        mubc = wbf[0][:, 0:2048].bitcast(F32)
        rhobc = wbf[1][:, 0:2048].bitcast(F32)
        xsc3 = [xab[:, 2048 * s:2048 * (s + 1)].bitcast(F32) for s in range(2)]
        tsc3 = xab[:, 4096:6144].bitcast(F32)
        xhsc3 = xab[:, 6144:8192].bitcast(F32)
        mubc2 = xab[:, 8192:10240].bitcast(F32)
        rhobc2 = xab[:, 10240:12288].bitcast(F32)
        xgsc = xab[:, 12288:14336].bitcast(F32)
        xisc = xab[:, 14336:16384].bitcast(I32)

        def sl(row, a, b):
            return stat_t[row:row + 1, a:b]
        mu_r = lambda tt: sl(0, 512 * tt, 512 * (tt + 1))
        rho_r = lambda tt: sl(0, 1024 + 512 * tt, 1024 + 512 * (tt + 1))
        aux1 = lambda tt: outs_t[0:1, 512 * tt:512 * (tt + 1)]
        aux2 = lambda tt: (xnst_t if tt == 0 else o1b_t)[0:1, 0:512]

        S_ = lambda i: smal_t[:, i:i + 1]
        gmax = S_(0)
        am_c, am2_c = S_(1), S_(2)
        m_c = [S_(3), S_(4)]
        m2_c = [S_(5), S_(6)]
        negm_c = [S_(7), S_(8)]
        za = [S_(9), S_(10)]
        zb = [S_(11), S_(12)]
        zs = [S_(13), S_(38)]
        rz = [S_(14), S_(39)]
        swacc = [S_(15 + i) for i in range(8)]
        swacc_b = S_(23)
        swsum, sw_s, rw_bc, sc_bc, s127, gamc, tmpa, tmpb, junk = (
            S_(24), S_(25), S_(26), S_(27), S_(28), S_(29), S_(30), S_(31), S_(32))
        eps5_c, eps6_c = S_(33), S_(34)
        junkv, junks, junkg = S_(35), S_(36), S_(37)
        amall = smal_t[:, 40:56]

        pb = [e(nc.psum_tensor(f"pb{i}", [128, 512], F32)) for i in range(6)]
        ptp = [e(nc.psum_tensor(f"ptp{i}", [128, 128], BF16)) for i in range(2)]
        sa = [pb[2], pb[3]]
        ctxps = [pb[4], pb[4]]
        rotps = pb[5]

        sems = {}
        for sname in ["Y", "G", "Tm", "V", "Sc"]:
            sems[sname] = e(nc.semaphore(f"sem{sname}"))
        block = e(nc.Block())

        P = Prog()

        def joiner(name, waits):
            P.step("V", lambda v: v.memset(junk, 0.0), waits=waits, tag=name, inc=("V", 1))

        def cdma(dst, src):
            return lambda g: g.dma_start(out=dst, in_=src)
        consts = [
            (nw_t[:, :], nw_f), (gf_t[:, :], gf_f), (pmk_t[:, :], pmk_h), (pmv_t[:, :], pmv_h),
            (cosb, cos_h), (sinb, sin_h), (ptm_t[:, :], ptm_h), (idb_t[:, :], idb_h),
            (idf_t[:, :], idf_h), (onb_t[:, :], onb_h), (onf_t[:, :], onf_h), (onc_t[:, :], onc_h),
            (mfir_t[:, :], mfirst_h), (mdia_t[:, :], mdiag_h),
        ]
        for i, (dst, src) in enumerate(consts):
            P.step("G", cdma(dst, src), tag=f"c{i}", inc=("G", 16))
        P.step("V", lambda v: v.memset(smal_t[:, :], 0.0), waits=[], tag="sm0", inc=("V", 1), pad=True)
        P.step("Sc", lambda s: s.activation(out=gfa_t[:, :], in_=gf_t[:, :], func=AF.Abs),
               waits=["c1"], tag="gfab", inc=("Sc", 1), pad=True)
        def epsf(v):
            v.memset(eps5_c, 1e-5)
            return v.memset(eps6_c, 1e-6)
        P.step("V", epsf, waits=[], tag="eps", inc=("V", 1))

        for cc in range(16):
            w = [f"amm{cc-4}"] if cc >= 4 else []
            P.step("Y", cdma(xsc[cc % 4][:, :], x_t[128 * cc:128 * (cc + 1), :]),
                   waits=w, tag=f"axl{cc}", inc=("Y", 16))
            P.step("V", (lambda cc: lambda v: v.tensor_copy(out=xbsc, in_=xsc[cc % 4][:, :]))(cc),
                   waits=[f"axl{cc}"] + ([f"amm{cc-1}"] if cc > 0 else []),
                   tag=f"axb{cc}", inc=("V", 1))
            P.step("Sc", (lambda cc: lambda s: s.activation(out=sqsc, in_=xsc[cc % 4][:, :], func=AF.Square))(cc),
                   waits=[f"axl{cc}"] + ([f"amm{cc-1}"] if cc > 0 else []),
                   tag=f"asq{cc}", inc=("Sc", 1))

            def statmm(cc):
                def f(t):
                    first, last = (cc == 0), (cc == 15)
                    t.matmul(out=pb[0][0:1, :], lhsT=onb_t[:, :], rhs=xbsc[:, 0:512], start=first, stop=last)
                    t.matmul(out=pb[1][0:1, :], lhsT=onb_t[:, :], rhs=xbsc[:, 512:1024], start=first, stop=last)
                    t.matmul(out=pb[2][0:1, :], lhsT=onb_t[:, :], rhs=sqsc[:, 0:512], start=first, stop=last)
                    return t.matmul(out=pb[3][0:1, :], lhsT=onb_t[:, :], rhs=sqsc[:, 512:1024], start=first, stop=last)
                return f
            P.step("Tm", statmm(cc), waits=[f"axb{cc}", f"asq{cc}", "c9"],
                   tag=f"amm{cc}", inc=("Tm", 1), pad=(cc == 15))

        for tt in range(2):
            P.step("V", (lambda tt: lambda v: v.tensor_scalar(
                out=mu_r(tt), in0=pb[tt][0:1, :], scalar1=1.0 / 2048.0, scalar2=None, op0=ALU.mult))(tt),
                waits=["amm15"], tag=f"amu{tt}", inc=("V", 1), pad=True)
            P.step("V", (lambda tt: lambda v: v.tensor_scalar(
                out=aux1(tt), in0=pb[2 + tt][0:1, :], scalar1=1.0 / 2048.0, scalar2=None, op0=ALU.mult))(tt),
                waits=["amm15"], tag=f"am2{tt}", inc=("V", 1), pad=True)
            P.step("Sc", (lambda tt: lambda s: s.activation(out=aux2(tt), in_=mu_r(tt), func=AF.Square))(tt),
                   waits=[f"amu{tt}"], tag=f"amsq{tt}", inc=("Sc", 1), pad=True)
            P.step("V", (lambda tt: lambda v: v.tensor_tensor(
                out=aux1(tt), in0=aux1(tt), in1=aux2(tt), op=ALU.subtract))(tt),
                waits=[f"am2{tt}", f"amsq{tt}"], tag=f"avar{tt}", inc=("V", 1), pad=True)
            P.step("Sc", (lambda tt: lambda s: s.activation(out=aux2(tt), in_=aux1(tt), func=AF.Ln, bias=eps5_c[0:1, :]))(tt),
                   waits=[f"avar{tt}", "eps"], tag=f"alnv{tt}", inc=("Sc", 1), pad=True)
            P.step("Sc", (lambda tt: lambda s: s.activation(out=rho_r(tt), in_=aux1(tt), func=AF.Ln))(tt),
                   waits=[f"avar{tt}"], tag=f"alnt{tt}", inc=("Sc", 1), pad=True)
            P.step("Sc", (lambda tt: lambda s: s.activation(out=aux1(tt), in_=aux2(tt), func=AF.Exp, scale=-0.5))(tt),
                   waits=[f"alnv{tt}", f"alnt{tt}"], tag=f"ar1{tt}", inc=("Sc", 1), pad=True)
            P.step("V", (lambda tt: lambda v: v.tensor_tensor(
                out=rho_r(tt), in0=rho_r(tt), in1=aux2(tt), op=ALU.subtract))(tt),
                waits=[f"ar1{tt}"], tag=f"alnd{tt}", inc=("V", 1), pad=True)
            P.step("Sc", (lambda tt: lambda s: s.activation(out=rho_r(tt), in_=rho_r(tt), func=AF.Exp))(tt),
                   waits=[f"alnd{tt}"], tag=f"at2{tt}", inc=("Sc", 1), pad=True)
            P.step("Sc", (lambda tt: lambda s: s.activation(out=rho_r(tt), in_=rho_r(tt), func=AF.Ln, bias=eps6_c[0:1, :]))(tt),
                   waits=[f"at2{tt}", "eps"], tag=f"alt2{tt}", inc=("Sc", 1), pad=True)
            P.step("Sc", (lambda tt: lambda s: s.activation(out=rho_r(tt), in_=rho_r(tt), func=AF.Exp, scale=-0.5))(tt),
                   waits=[f"alt2{tt}"], tag=f"ar2{tt}", inc=("Sc", 1), pad=True)
            P.step("V", (lambda tt: lambda v: v.tensor_tensor(
                out=rho_r(tt), in0=rho_r(tt), in1=aux1(tt), op=ALU.mult))(tt),
                waits=[f"ar2{tt}"], tag=f"arho{tt}", inc=("V", 1), pad=True)

        def bcmm(dst_bank, src, tt):
            def f(t):
                return t.matmul(out=pb[dst_bank][:, :], lhsT=onf_t[:, :], rhs=src(tt), start=True, stop=True)
            return f
        for tt in range(2):
            P.step("Tm", bcmm(tt, mu_r, tt), waits=[f"arho{tt}", f"amu{tt}", "c10"],
                   tag=f"bmu{tt}", inc=("Tm", 1), pad=True)
            P.step("Tm", bcmm(2 + tt, rho_r, tt), waits=[f"arho{tt}"], tag=f"brho{tt}", inc=("Tm", 1), pad=True)
        for tt in range(2):
            P.step("V", (lambda tt: lambda v: v.tensor_copy(out=mubc[:, 512 * tt:512 * (tt + 1)], in_=pb[tt][:, :]))(tt),
                   waits=[f"bmu{tt}"], tag=f"emu{tt}", inc=("V", 1))
            P.step("V", (lambda tt: lambda v: v.tensor_copy(out=rhobc[:, 512 * tt:512 * (tt + 1)], in_=pb[2 + tt][:, :]))(tt),
                   waits=[f"brho{tt}"], tag=f"erho{tt}", inc=("V", 1))

        if debug2:
            P.step("G", lambda g: g.dma_start(out=dbg["st0"], in_=stat_t[0:1, :]),
                   waits=["arho0", "arho1", "amu0", "amu1"], tag="dst0", inc=("G", 16))
        if debug2:
            P.step("G", lambda g: g.dma_start(out=dbg["gfa"], in_=gfa_t[:, :]),
                   waits=["gfab"], tag="dgfa", inc=("G", 16))
            P.step("G", lambda g: g.dma_start(out=dbg["mb"], in_=mubc),
                   waits=["emu0", "emu1"], tag="dmb", inc=("G", 16))
            P.step("G", lambda g: g.dma_start(out=dbg["rb"], in_=rhobc),
                   waits=["erho0", "erho1"], tag="drb", inc=("G", 16))
        for cc in range(16):
            w = [f"asub{cc-4}"] if cc >= 4 else ["amm15"]
            P.step("Y", cdma(xsc[cc % 4][:, :], x_t[128 * cc:128 * (cc + 1), :]),
                   waits=w, tag=f"axr{cc}", inc=("Y", 16))
            P.step("V", (lambda cc: lambda v: v.tensor_tensor(
                out=tsc, in0=xsc[cc % 4][:, :], in1=mubc, op=ALU.subtract))(cc),
                waits=[f"axr{cc}", "emu0", "emu1"] + ([f"axh{cc-1}"] if cc > 0 else []),
                tag=f"asub{cc}", inc=("V", 1))
            P.step("V", (lambda cc: lambda v: v.tensor_tensor(
                out=xhsc, in0=tsc, in1=rhobc, op=ALU.mult))(cc),
                waits=[f"asub{cc}", "erho0", "erho1"], tag=f"axh{cc}", inc=("V", 1))
            P.step("V", (lambda cc: lambda v: v.tensor_scalar(
                out=xab[:, T * cc:T * (cc + 1)], in0=xhsc,
                scalar1=nw_t[:, cc:cc + 1], scalar2=None, op0=ALU.mult))(cc),
                waits=[f"axh{cc}", "c0"], tag=f"axab{cc}", inc=("V", 1))
            P.step("V", (lambda cc: lambda v: v.tensor_reduce(
                out=amall[:, cc:cc + 1], in_=xhsc, axis=AX, op=ALU.max, apply_absolute_value=True))(cc),
                waits=[f"axh{cc}"], tag=f"aam{cc}", inc=("V", 1), pad=True)
        P.step("V", lambda v: v.tensor_tensor(out=amall, in0=amall, in1=gfa_t[:, :], op=ALU.mult),
               waits=[f"aam{cc}" for cc in range(16)] + ["gfab", "axab15"], tag="agm_m", inc=("V", 1), pad=True)
        P.step("V", lambda v: v.tensor_reduce(out=gmax, in_=amall, axis=AX, op=ALU.max),
               waits=["agm_m"], tag="agm", inc=("V", 1), pad=True)
        APPLY_DONE = "agm"
        if debug:
            P.step("G", lambda g: g.dma_start(out=dbg["xab"], in_=xab),
                   waits=["axab15", "axab14"], tag="dbgxab", inc=("G", 16))

        P.step("V", lambda v: v.memset(ccst_t[:, :], NEG), waits=[], tag="ccm", inc=("V", 1))
        P.step("V", lambda v: v.tensor_copy(out=ccst_t[:, 0:1], in_=gmax),
               waits=["ccm", APPLY_DONE], tag="ccc", inc=("V", 1), pad=True)

        wsh = [wst[:, 0:4096], wst[:, 4096:8192]]
        nvg = 8
        for g in range(nvg):
            w = [f"vcv{g-2}"] if g >= 2 else []
            src = wqkvT[:, 2 * D + 256 * g:2 * D + 256 * (g + 1)].rearrange("(c p) m -> p c m", p=128)
            dst3 = wsh[g % 2].rearrange("p (c m) -> p c m", m=256)
            P.step("Y", (lambda dst3, src: lambda y: y.dma_start(out=dst3, in_=src))(dst3, src),
                   waits=w, tag=f"vld{g}", inc=("Y", 16))
            wguard = [APPLY_DONE] if g < 2 else [f"vmm{g-2}_7"]
            cvt = []
            for i in range(16):
                engx = ["G", "Sc", "G"][i % 3]
                P.step(engx, (lambda g, i: lambda en: ecopy(en,
                    wbf[g % 2][:, 256 * i:256 * (i + 1)], wsh[g % 2][:, 256 * i:256 * (i + 1)]))(g, i),
                    waits=[f"vld{g}"] + wguard, tag=f"vcv{g}_{i}", inc=(engx, 1))
                cvt.append(f"vcv{g}_{i}")
            joiner(f"vcv{g}", cvt)
            for tc in range(8):
                bank = tc % 4

                def vmm(g, tc, bank):
                    def f(t):
                        r = None
                        for cc in range(16):
                            r = t.matmul(out=pb[bank][:, 0:256],
                                         lhsT=xab[:, T * cc + 128 * tc:T * cc + 128 * (tc + 1)],
                                         rhs=wbf[g % 2][:, 256 * cc:256 * (cc + 1)],
                                         start=(cc == 0), stop=(cc == 15))
                        return r
                    return f
                w = [f"vcv{g}", APPLY_DONE]
                if g > 0 or tc >= 4:
                    pg, ptc = (g, tc - 4) if tc >= 4 else (g - 1, tc + 4)
                    w.append(f"vev{pg}_{ptc}")
                P.step("Tm", vmm(g, tc, bank), waits=w, tag=f"vmm{g}_{tc}", inc=("Tm", 1))
                P.step("V", (lambda g, tc, bank: lambda v: v.tensor_copy(
                    out=vsb[:, D * tc + 256 * g:D * tc + 256 * (g + 1)], in_=pb[bank][:, 0:256]))(g, tc, bank),
                    waits=[f"vmm{g}_{tc}"], tag=f"vev{g}_{tc}", inc=("V", 1))
        if debug:
            P.step("G", lambda g: g.dma_start(out=dbg["v"], in_=vsb),
                   waits=[f"vev{nvg-1}_{tc}" for tc in range(8)], tag="dbgv", inc=("G", 16))

        P.step("V", lambda v: v.memset(sa[0][:, :], -1.0e30),
               waits=["erho0", "erho1", f"vev{nvg-1}_6"], tag="sainit0", inc=("V", 1))
        P.step("V", lambda v: v.memset(sa[1][:, :], -1.0e30),
               waits=["erho0", "erho1", f"vev{nvg-1}_7"], tag="sainit1", inc=("V", 1))
        sa_user = {0: "sainit0", 1: "sainit1"}
        ptp_user = {0: None, 1: None}
        ctx_user = {0: None, 1: None}
        negm_user = {0: None, 1: None}
        qkbank_ev = {0: f"vev{nvg-1}_6", 1: f"vev{nvg-1}_7"}
        rot_user = {0: None}
        av_prev = [None]
        rope_reader = {}

        def emit_qld(hh):
            hhp = hh % 2
            wq_src = wqkvT[:, 128 * hh:128 * (hh + 1)].rearrange("(c p) m -> p c m", p=128)
            wk_src = wqkvT[:, D + 128 * hh:D + 128 * (hh + 1)].rearrange("(c p) m -> p c m", p=128)
            dsth = wsh[hhp].rearrange("p (c m) -> p c m", m=256)
            wwait = [f"vcv{6+hhp}"] if hh < 2 else [f"qcv{hh-2}"]
            if 2 <= hh <= 9:
                wwait.append(f"swa{hh-2}b")
            P.step("Y", (lambda dsth, wq_src: lambda y: y.dma_start(out=dsth[:, :, 0:128], in_=wq_src))(dsth, wq_src),
                   waits=wwait, tag=f"qld{hh}a", inc=("Y", 16))
            P.step("Y", (lambda dsth, wk_src: lambda y: y.dma_start(out=dsth[:, :, 128:256], in_=wk_src))(dsth, wk_src),
                   waits=[], tag=f"qld{hh}", inc=("Y", 16))

        for h in range(16):
            hp = h % 2
            ah = h - 1

            def attn_A(ah, pos):
                w_ = pos % 2
                qc = pos // 2
                par = w_
                ntok = 128 * (qc + 1)
                ahp = ah % 2
                qslice = qh[ahp][:, 512 * w_ + 128 * qc:512 * w_ + 128 * (qc + 1)]
                kslice = kh[ahp][:, 512 * w_:512 * w_ + ntok]
                wts = [f"ropeq{ah}", f"ropek{ah}", "c2", "c12", "c13", "c7", sa_user[par]]
                msk = mfir_t if qc == 0 else mdia_t

                def qkmm(qslice, kslice, par, ntok, ah, qc, msk):
                    def f(t):
                        t.matmul(out=sa[par][:, 0:ntok], lhsT=qslice, rhs=kslice,
                                 start=True, stop=True)
                        mm = t.matmul(out=sa[par][:, 128 * qc:ntok], lhsT=idb_t[:, :],
                                      rhs=msk[:, :], start=False, stop=True)
                        if qc < 3:
                            mm = t.matmul(out=sa[par][:, ntok:ntok + 16], lhsT=qslice,
                                          rhs=pmk_t[:, 16 * ah:16 * (ah + 1)], start=True, stop=True)
                        return mm
                    return f
                P.step("Tm", qkmm(qslice, kslice, par, ntok, ah, qc, msk),
                       waits=wts, tag=f"qk{ah}_{w_}_{qc}", inc=("Tm", 1))
                area = ntok + 16 if qc < 3 else 512
                P.step("V", (lambda par, area: lambda v: v.tensor_reduce(
                    out=negm_c[par], in_=sa[par][:, 0:area], axis=AX, op=ALU.max, negate=True))(par, area),
                    waits=[f"qk{ah}_{w_}_{qc}"] + ([negm_user[par]] if negm_user[par] else []),
                    tag=f"ngm{ah}_{w_}_{qc}", inc=("V", 1), pad=True)

            def attn_PM3(ah, pos):
                w_ = pos % 2
                par = w_
                pm_bank, pm_c0 = 1 - par, 480
                ahp = ah % 2
                qc = pos // 2
                qslice = qh[ahp][:, 512 * w_ + 128 * qc:512 * w_ + 128 * (qc + 1)]
                P.step("Tm", (lambda qslice, pm_bank, pm_c0, ah: lambda t: t.matmul(
                    out=sa[pm_bank][:, pm_c0:pm_c0 + 16], lhsT=qslice,
                    rhs=pmk_t[:, 16 * ah:16 * (ah + 1)], start=True, stop=True))(qslice, pm_bank, pm_c0, ah),
                    waits=[f"qk{ah}_{w_}_3", sa_user[1 - par]], tag=f"pm3{ah}_{w_}", inc=("Tm", 1))

            def attn_B(ah, pos):
                w_ = pos % 2
                qc = pos // 2
                par = w_
                ntok = 128 * (qc + 1)
                pm_bank, pm_c0 = (par, ntok) if qc < 3 else (1 - par, 480)
                expw = [f"ngm{ah}_{w_}_{qc}"]
                if qc == 3:
                    expw.append(f"pm3{ah}_{w_}")
                prev_pn = (f"pn{ah}_{w_}_{qc-1}" if qc >= 1 else
                           (f"pn{ah-1}_{w_}_3" if ah > 0 else None))
                if prev_pn:
                    expw.append(prev_pn)

                def expf(par, ntok, pm_bank, pm_c0):
                    def f(s):
                        s.activation(out=p_sb[par][:, 16:16 + ntok], in_=sa[par][:, 0:ntok],
                                     func=AF.Exp, bias=negm_c[par], scale=1.0, accum_out=za[par])
                        return s.activation(out=p_sb[par][:, 0:16], in_=sa[pm_bank][:, pm_c0:pm_c0 + 16],
                                            func=AF.Exp, bias=negm_c[par], scale=1.0, accum_out=zb[par])
                    return f
                P.step("Sc", expf(par, ntok, pm_bank, pm_c0),
                       waits=expw, tag=f"exp{ah}_{w_}_{qc}", inc=("Sc", 1), pad=True)
                sa_user[par] = f"exp{ah}_{w_}_{qc}"
                if qc == 3:
                    sa_user[1 - par] = f"exp{ah}_{w_}_{qc}"
                negm_user[par] = f"exp{ah}_{w_}_{qc}"

            def attn_C(ah, pos):
                w_ = pos % 2
                qc = pos // 2
                par = w_
                ntok = 128 * (qc + 1)
                P.step("V", (lambda par: lambda v: v.tensor_tensor(
                    out=zs[par], in0=za[par], in1=zb[par], op=ALU.add))(par),
                    waits=[f"exp{ah}_{w_}_{qc}"], tag=f"zz{ah}_{w_}_{qc}", inc=("V", 1), pad=True, dup=True)
                P.step("V", (lambda par: lambda v: v.reciprocal(out=rz[par], in_=zs[par]))(par),
                       waits=[f"zz{ah}_{w_}_{qc}"], tag=f"rzt{ah}_{w_}_{qc}", inc=("V", 1), pad=True, dup=True)
                pnw = [f"rzt{ah}_{w_}_{qc}"]
                prev_tp = (f"tp{ah}_{w_}_{qc-1}" if qc >= 1 else
                           (f"tp{ah-1}_{w_}_3" if ah > 0 else None))
                if prev_tp:
                    pnw.append(prev_tp)
                P.step("V", (lambda par, ntok: lambda v: v.tensor_scalar(
                    out=p_n[par][:, 0:16 + ntok], in0=p_sb[par][:, 0:16 + ntok],
                    scalar1=rz[par], scalar2=None, op0=ALU.mult))(par, ntok),
                    waits=pnw, tag=f"pn{ah}_{w_}_{qc}", inc=("V", 1))
                if debug:
                    P.step("G", (lambda ah, w_, qc, par: lambda g: g.dma_start(
                        out=dbg["p"][ah, w_, qc], in_=p_n[par]))(ah, w_, qc, par),
                        waits=[f"pn{ah}_{w_}_{qc}"], tag=f"dbgp{ah}_{w_}_{qc}", inc=("G", 16))
                ntp = qc + 2
                tpe_tags = []
                for jj in range(ntp):
                    tpar = jj % 2
                    tw = [f"pn{ah}_{w_}_{qc}", "c7"]
                    if ptp_user[tpar]:
                        tw.append(ptp_user[tpar])
                    if jj == 0:
                        srcp = p_n[par][:, 0:16]
                        dstp = ptp[tpar][0:16, :]
                        dste = pmpt[w_][:, 128 * qc:128 * (qc + 1)]
                    else:
                        j = jj - 1
                        srcp = p_n[par][:, 16 + 128 * j:16 + 128 * (j + 1)]
                        dstp = ptp[tpar][:, :]
                        dste = ptj[w_][j][:, 128 * (qc - j):128 * (qc - j + 1)]
                    P.step("Tm", (lambda srcp, dstp: lambda t: t.transpose(
                        out=dstp, in_=srcp, identity=idb_t[:, :]))(srcp, dstp),
                        waits=tw, tag=f"tpp{ah}_{w_}_{qc}_{jj}", inc=("Tm", 1))
                    ew = [f"tpp{ah}_{w_}_{qc}_{jj}"]
                    if av_prev[0]:
                        ew.append(av_prev[0])
                    if jj % 2 == 0:
                        P.step("Sc", (lambda dstp, dste: lambda s: s.copy(out=dste, in_=dstp))(dstp, dste),
                               waits=ew, tag=f"tpe{ah}_{w_}_{qc}_{jj}", inc=("Sc", 1))
                    else:
                        P.step("V", (lambda dstp, dste: lambda v: v.tensor_copy(out=dste, in_=dstp))(dstp, dste),
                               waits=ew, tag=f"tpe{ah}_{w_}_{qc}_{jj}", inc=("V", 1))
                    ptp_user[tpar] = f"tpe{ah}_{w_}_{qc}_{jj}"
                    tpe_tags.append(f"tpe{ah}_{w_}_{qc}_{jj}")
                joiner(f"tp{ah}_{w_}_{qc}", tpe_tags)

            def attn_AV(ah):
                for w_ in range(2):
                    avw = [f"tp{ah}_{w_}_3", "c3"]
                    avw += [f"vev{ah // 2}_{4 * w_ + j}" for j in range(4)]
                    if ctx_user[0]:
                        avw.append(ctx_user[0])

                    def avf(ah, w_):
                        def f(t):
                            t.matmul(out=ctxps[0][:, :], lhsT=pmv_t[:, 128 * ah:128 * (ah + 1)],
                                     rhs=pmpt[w_][:, :], start=True, stop=False)
                            r = None
                            for j in range(4):
                                tcj = 4 * w_ + j
                                nq = 512 - 128 * j
                                r = t.matmul(out=ctxps[0][:, 128 * j:512],
                                             lhsT=vsb[:, D * tcj + 128 * ah:D * tcj + 128 * (ah + 1)],
                                             rhs=ptj[w_][j][:, 0:nq],
                                             start=False, stop=(j == 3))
                            return r
                        return f
                    P.step("Tm", avf(ah, w_), waits=avw, tag=f"av{ah}_{w_}", inc=("Tm", 1))
                    av_prev[0] = f"av{ah}_{w_}"
                    P.step("V", (lambda ah, w_: lambda v: v.tensor_copy(
                        out=ctxb[:, T * ah + 512 * w_:T * ah + 512 * (w_ + 1)], in_=ctxps[0][:, :]))(ah, w_),
                        waits=[f"av{ah}_{w_}"], tag=f"cev{ah}_{w_}", inc=("V", 1))
                    ctx_user[0] = f"cev{ah}_{w_}"
                rope_reader[ah % 2] = f"pm3{ah}_1"

            def rnd(r, ah):
                if ah < 0:
                    return
                if r == 7:
                    attn_PM3(ah, 6)
                if r == 8:
                    attn_PM3(ah, 7)
                if 1 <= r <= 8:
                    attn_B(ah, r - 1)
                if r < 8:
                    attn_A(ah, r)
                if 2 <= r:
                    attn_C(ah, r - 2)

            def qmm_step(part, name, tt):
                bank = tt
                qt_b = qtmp_b[part]

                def qmm(hp, part, tt, bank):
                    def f(t):
                        r = None
                        wb = wbf[hp]
                        for cc in range(16):
                            r = t.matmul(out=pb[bank][:, :],
                                         lhsT=wb[:, 256 * cc + 128 * part:256 * cc + 128 * (part + 1)],
                                         rhs=xab[:, T * cc + 512 * tt:T * cc + 512 * (tt + 1)],
                                         start=(cc == 0), stop=(cc == 15))
                        return r
                    return f
                P.step("Tm", qmm(hp, part, tt, bank),
                       waits=[f"qcv{h}", qkbank_ev[bank]],
                       tag=f"qmm{name}{h}_{tt}", inc=("Tm", 1))
                evw = [f"qmm{name}{h}_{tt}"]
                if h > 0:
                    evw.append(f"qrot{name}{h-1}_{tt}")
                if part == 0:
                    P.step("V", (lambda tt, bank, qt_b: lambda v: v.tensor_scalar(
                        out=qt_b[:, 512 * tt:512 * (tt + 1)], in0=pb[bank][:, :],
                        scalar1=CSC, scalar2=None, op0=ALU.mult))(tt, bank, qt_b),
                        waits=evw, tag=f"qev{name}{h}_{tt}", inc=("V", 1))
                else:
                    P.step("V", (lambda tt, bank, qt_b: lambda v: v.tensor_copy(
                        out=qt_b[:, 512 * tt:512 * (tt + 1)], in_=pb[bank][:, :]))(tt, bank, qt_b),
                        waits=evw, tag=f"qev{name}{h}_{tt}", inc=("V", 1))
                qkbank_ev[bank] = f"qev{name}{h}_{tt}"

            def rot_step(part, name, tt):
                qt_b = qtmp_b[part]
                ro_b = rots_b[part]
                rw_ = [f"qev{name}{h}_{tt}", "c6"]
                if rot_user[0]:
                    rw_.append(rot_user[0])
                P.step("Tm", (lambda tt, qt_b: lambda t: t.matmul(
                    out=rotps[:, :], lhsT=ptm_t[:, :], rhs=qt_b[:, 512 * tt:512 * (tt + 1)],
                    start=True, stop=True))(tt, qt_b),
                    waits=rw_, tag=f"qrot{name}{h}_{tt}", inc=("Tm", 1))
                sw_ = [f"qrot{name}{h}_{tt}"]
                if h > 0:
                    sw_.append(f"rope{name}{h-1}")
                P.step("Sc", (lambda tt, ro_b: lambda s: s.copy(
                    out=ro_b[:, 512 * tt:512 * (tt + 1)], in_=rotps[:, :]))(tt, ro_b),
                    waits=sw_, tag=f"qrev{name}{h}_{tt}", inc=("Sc", 1))
                rot_user[0] = f"qrev{name}{h}_{tt}"

            def rt1_step(part, name):
                qt_b = qtmp_b[part]
                P.step("V", (lambda qt_b: lambda v: v.tensor_tensor(
                    out=t1b, in0=qt_b, in1=cosb, op=ALU.mult))(qt_b),
                    waits=[f"qev{name}{h}_1", "c4"], tag=f"rt1{name}{h}", inc=("V", 1))

            def rt2_rope_step(part, name):
                dstbuf = qh[hp] if part == 0 else kh[hp]
                ro_b = rots_b[part]
                P.step("V", (lambda ro_b: lambda v: v.tensor_tensor(
                    out=ro_b, in0=ro_b, in1=sinb, op=ALU.mult))(ro_b),
                    waits=[f"qrev{name}{h}_0", f"qrev{name}{h}_1", "c5"], tag=f"rt2{name}{h}", inc=("V", 1))
                ropew = [f"rt1{name}{h}", f"rt2{name}{h}"]
                if h >= 2 and rope_reader.get(hp):
                    ropew.append(rope_reader[hp])
                P.step("V", (lambda dstbuf, ro_b: lambda v: v.tensor_tensor(
                    out=dstbuf, in0=t1b, in1=ro_b, op=ALU.add))(dstbuf, ro_b),
                    waits=ropew, tag=f"rope{name}{h}", inc=("V", 1))

            rnd(0, ah)
            if h == 0:
                emit_qld(0)
                emit_qld(1)
            elif h + 1 <= 15:
                emit_qld(h + 1)
            cvw = [f"vmm{6+hp}_7"] if h < 2 else [f"qmmk{h-2}_1"]
            cvt = []
            for i in range(16):
                engx = "Sc" if i % 4 == 3 else "G"
                P.step(engx, (lambda hp, i: lambda en: ecopy(en,
                    wbf[hp][:, 256 * i:256 * (i + 1)], wsh[hp][:, 256 * i:256 * (i + 1)]))(hp, i),
                    waits=[f"qld{h}"] + cvw, tag=f"qcv{h}_{i}", inc=(engx, 1))
                cvt.append(f"qcv{h}_{i}")
            joiner(f"qcv{h}", cvt)
            if h <= 7:
                k = h
                half = k % 2
                srcw_ = wffnT[:, 256 * k:256 * (k + 1)].rearrange("(c p) m -> p c m", p=128)
                dstw_ = wsh[half].rearrange("p (c m) -> p c m", m=256)
                P.step("Y", (lambda dstw_, srcw_: lambda y: y.dma_start(out=dstw_, in_=srcw_))(dstw_, srcw_),
                       waits=[f"qcv{k}"] + ([f"swa{k-2}b"] if k >= 2 else []),
                       tag=f"swl{k}", inc=("Y", 16))
            rnd(1, ah)
            qmm_step(0, "q", 0)
            rnd(2, ah)
            qmm_step(0, "q", 1)
            rot_step(0, "q", 0)
            rnd(3, ah)
            qmm_step(1, "k", 0)
            rot_step(0, "q", 1)
            rt1_step(0, "q")
            rnd(4, ah)
            qmm_step(1, "k", 1)
            rot_step(1, "k", 0)
            rt2_rope_step(0, "q")
            rnd(5, ah)
            rot_step(1, "k", 1)
            rt1_step(1, "k")
            rnd(6, ah)
            rt2_rope_step(1, "k")
            rnd(7, ah)
            rnd(8, ah)
            rnd(9, ah)
            if ah >= 0:
                attn_AV(ah)
            if h <= 7:
                k = h
                half = k % 2
                P.step("Sc", (lambda half, k: lambda s: s.activation(
                    out=outsb.bitcast(BF16)[:, 0:2048], in_=wsh[half][:, 0:2048], func=AF.Abs,
                    accum_out=swacc[k]))(half, k),
                    waits=[f"swl{k}"] + ([f"swv{k-1}"] if k >= 1 else []), tag=f"swa{k}a", inc=("Sc", 1))
                P.step("Sc", (lambda half, k: lambda s: s.activation(
                    out=outsb.bitcast(BF16)[:, 0:2048], in_=wsh[half][:, 2048:4096], func=AF.Abs,
                    accum_out=swacc_b))(half, k),
                    waits=[f"swa{k}a"], tag=f"swa{k}b", inc=("Sc", 1), pad=True)
                P.step("V", (lambda k: lambda v: v.tensor_tensor(
                    out=swacc[k], in0=swacc[k], in1=swacc_b, op=ALU.add))(k),
                    waits=[f"swa{k}b"], tag=f"swv{k}", inc=("V", 1), pad=True)
            if debug:
                P.step("G", (lambda h: lambda g: g.dma_start(out=dbg["qh"][h], in_=qh[h % 2]))(h),
                       waits=[f"ropeq{h}"], tag=f"dbgq{h}", inc=("G", 16))
                P.step("G", (lambda h: lambda g: g.dma_start(out=dbg["kh"][h], in_=kh[h % 2]))(h),
                       waits=[f"ropek{h}"], tag=f"dbgk{h}", inc=("G", 16))
        for r in range(10):
            rnd(r, 15)
        attn_AV(15)

        if debug:
            P.step("G", lambda g: g.dma_start(out=dbg["ctx"], in_=ctxb),
                   waits=["cev15_1", "cev15_0"], tag="dbgctx", inc=("G", 16))

        P.step("G", lambda g: g.dma_start(out=cc_in, in_=ccst_t[:, :]), waits=["ccc"], tag="ccw", inc=("G", 16))
        P.step("G", lambda g: g.collective_compute(
            "AllReduce", ALU.max, replica_groups=[list(range(NCORES))],
            ins=[cc_in.opt()], outs=[cc_out.opt()]),
            waits=["ccw"], tag="ccr", inc=("G", 1))
        P.step("G", lambda g: g.dma_start(out=ccst_t[:, :], in_=cc_out),
               waits=["ccr"], tag="ccb", inc=("G", 16))

        SA0_FIN = sa_user[0]
        SA1_FIN = sa_user[1]
        QK0_FIN = qkbank_ev[0]
        QK1_FIN = qkbank_ev[1]

        P.step("V", lambda v: v.tensor_reduce(out=swsum, in_=smal_t[:, 15:23], axis=AX, op=ALU.add),
               waits=[f"swv{k}" for k in range(8)], tag="sws", inc=("V", 1), pad=True)
        P.step("Tm", lambda t: t.matmul(out=pb[0][0:1, 0:1], lhsT=swsum, rhs=onc_t[:, :],
                                        start=True, stop=True),
               waits=["sws", "c11", QK0_FIN], tag="swm", inc=("Tm", 1), pad=True)
        P.step("V", lambda v: v.tensor_scalar(out=sw_s[0:1, :], in0=pb[0][0:1, 0:1],
                                              scalar1=1.0 / (2048.0 * 2048.0), scalar2=None, op0=ALU.mult),
               waits=["swm"], tag="swv", inc=("V", 1), pad=True)
        P.step("Tm", lambda t: t.transpose(out=pb[1][0:1, 0:128], in_=ccst_t[:, 0:1], identity=idf_t[:, :]),
               waits=["ccb", "c8", QK1_FIN], tag="qst", inc=("Tm", 1), pad=True)
        P.step("V", lambda v: v.tensor_reduce(out=tmpa[0:1, :], in_=pb[1][0:1, 0:128], axis=AX, op=ALU.max),
               waits=["qst"], tag="qsr", inc=("V", 1), pad=True)
        P.step("V", lambda v: v.tensor_scalar(out=tmpa[0:1, :], in0=tmpa[0:1, :], scalar1=1e-5,
                                              scalar2=None, op0=ALU.max),
               waits=["qsr"], tag="qse", inc=("V", 1), pad=True)
        P.step("Tm", lambda t: t.matmul(out=pb[0][:, 0:1], lhsT=onf_t[:, :], rhs=tmpa[0:1, :],
                                        start=True, stop=True),
               waits=["qse", "swv", "c10"], tag="qsb", inc=("Tm", 1), pad=True)
        P.step("V", lambda v: v.tensor_copy(out=sc_bc, in_=pb[0][:, 0:1]),
               waits=["qsb"], tag="qsc", inc=("V", 1), pad=True)
        P.step("Tm", lambda t: t.matmul(out=pb[1][:, 0:1], lhsT=onf_t[:, :], rhs=sw_s[0:1, :], start=True, stop=True),
               waits=["qsc", "swv"], tag="qwb", inc=("Tm", 1), pad=True)
        P.step("V", lambda v: v.tensor_copy(out=tmpb, in_=pb[1][:, 0:1]),
               waits=["qwb"], tag="qwc", inc=("V", 1), pad=True)
        P.step("V", lambda v: v.reciprocal(out=rz[0], in_=sc_bc), waits=["qsc", "pn15_1_3"], tag="qrc", inc=("V", 1), pad=True)
        P.step("V", lambda v: v.tensor_scalar(out=s127, in0=rz[0], scalar1=127.0, scalar2=None, op0=ALU.mult),
               waits=["qrc"], tag="qs127", inc=("V", 1), pad=True)
        P.step("V", lambda v: v.reciprocal(out=rw_bc, in_=tmpb), waits=["qwc"], tag="qrw", inc=("V", 1), pad=True)
        P.step("V", lambda v: v.tensor_tensor(out=gamc, in0=sc_bc, in1=tmpb, op=ALU.mult),
               waits=["qwc"], tag="qg1", inc=("V", 1), pad=True)
        P.step("V", lambda v: v.tensor_scalar(out=gamc, in0=gamc, scalar1=1.0 / 127.0, scalar2=None, op0=ALU.mult),
               waits=["qg1"], tag="qgam", inc=("V", 1), pad=True)
        if debug:
            def dbgsc_f(v):
                v.tensor_copy(out=outsb[:, 0:1], in_=sc_bc)
                v.tensor_copy(out=outsb[:, 1:2], in_=tmpb)
                v.tensor_copy(out=outsb[:, 2:3], in_=gamc)
                v.tensor_copy(out=outsb[:, 3:4], in_=s127)
                for kk in range(8):
                    v.tensor_copy(out=outsb[:, 5 + kk:6 + kk], in_=swacc[kk])
                v.tensor_copy(out=outsb[:, 13:14], in_=swsum)
                return v.tensor_copy(out=outsb[:, 4:5], in_=gmax)
            P.step("V", dbgsc_f, waits=["qgam", "qs127", "swa7b"], tag="dbgscv", inc=("V", 1))
            P.step("G", lambda g: g.dma_start(out=dbg["sc"], in_=outsb[:, 0:16]),
                   waits=["dbgscv"], tag="dbgsc", inc=("G", 16))

        XAB_FREE = "qmmk15_1"
        for tt in range(2):
            P.step("Tm", bcmm(tt, mu_r, tt), waits=["qsb", "qsc", XAB_FREE], tag=f"b2mu{tt}", inc=("Tm", 1), pad=True)
            P.step("Tm", bcmm(2 + tt, rho_r, tt), waits=[SA0_FIN, SA1_FIN, "qwc"], tag=f"b2rho{tt}", inc=("Tm", 1), pad=True)
        for tt in range(2):
            P.step("V", (lambda tt: lambda v: v.tensor_copy(
                out=mubc2[:, 512 * tt:512 * (tt + 1)], in_=pb[tt][:, :]))(tt),
                waits=[f"b2mu{tt}"], tag=f"e2mu{tt}", inc=("V", 1))
            P.step("V", (lambda tt: lambda v: v.tensor_copy(
                out=rhobc2[:, 512 * tt:512 * (tt + 1)], in_=pb[2 + tt][:, :]))(tt),
                waits=[f"b2rho{tt}"], tag=f"e2rho{tt}", inc=("V", 1))

        xqb = vsb
        for cc in range(16):
            w = [f"exh{cc-2}"] if cc >= 2 else ["e2mu0", "e2mu1", "e2rho0", "e2rho1"]
            P.step("Y", (lambda cc: lambda y: y.dma_start(
                out=xsc3[cc % 2][:, :], in_=x_t[128 * cc:128 * (cc + 1), :]))(cc),
                waits=w, tag=f"exl{cc}", inc=("Y", 16))
            P.step("V", (lambda cc: lambda v: v.tensor_tensor(
                out=tsc3, in0=xsc3[cc % 2][:, :], in1=mubc2, op=ALU.subtract))(cc),
                waits=[f"exl{cc}"] + ([f"exq{cc-1}"] if cc > 0 else []), tag=f"esb{cc}", inc=("V", 1))
            P.step("V", (lambda cc: lambda v: v.tensor_tensor(
                out=xhsc3, in0=tsc3, in1=rhobc2, op=ALU.mult))(cc),
                waits=[f"esb{cc}"], tag=f"exh{cc}", inc=("V", 1))
            P.step("V", (lambda cc: lambda v: v.tensor_scalar(
                out=xgsc, in0=xhsc3, scalar1=gf_t[:, cc:cc + 1],
                scalar2=s127, op0=ALU.mult, op1=ALU.mult))(cc),
                waits=[f"exh{cc}", "qs127", "c1"], tag=f"exg{cc}", inc=("V", 1))
            iw = [f"exg{cc}"]
            if cc == 0:
                iw += ["av15_1", "cev15_1"]
            P.step("V", (lambda cc: lambda v: v.tensor_copy(out=xisc, in_=xgsc))(cc),
                   waits=iw, tag=f"exi{cc}", inc=("V", 1))
            P.step("V", (lambda cc: lambda v: v.tensor_copy(
                out=xqb[:, T * cc:T * (cc + 1)], in_=xisc))(cc),
                waits=[f"exi{cc}"], tag=f"exq{cc}", inc=("V", 1))
        if debug:
            P.step("G", lambda g: g.dma_start(out=dbg["xqb"], in_=xqb),
                   waits=["exq15"], tag="dbgxqb", inc=("G", 16))

        wo_user = {0: "swa6b", 1: "swa7b"}
        wob_user = {0: "qmmk14_1", 1: "qmmk15_1"}
        wfb_user = {0: "tp15_1_3", 1: "tp15_1_3"}
        pbD_user = {0: "e2mu0", 1: "e2mu1"}
        pbF_user = {0: "e2rho0", 1: "e2rho1"}
        prep_state = {}

        def d_prep(og, ph):
            if og > 7:
                return
            par = og % 2
            st = prep_state.setdefault(og, {"cvt": []})

            def emit_f(i):
                c0 = 512 * i
                srcw = wsh[1 - par][:, c0:c0 + 512]
                prevd = f"f2_{og}_{i-1}" if i > 0 else (f"f2_{og-1}_7" if og > 0 else None)
                prevc = f"f3_{og}_{i-1}" if i > 0 else (f"f3_{og-1}_7" if og > 0 else None)
                w1 = [f"dfl{og}", "qrw", wfb_user[par], "exq15"]
                if prevd:
                    w1.append(prevd)
                P.step("V", (lambda srcw: lambda v: v.tensor_scalar(
                    out=xgsc[:, 0:512], in0=srcw, scalar1=rw_bc, scalar2=1.0,
                    op0=ALU.mult, op1=ALU.min))(srcw),
                    waits=w1, tag=f"f1_{og}_{i}", inc=("V", 1))
                w2 = [f"f1_{og}_{i}"]
                if prevc:
                    w2.append(prevc)
                P.step("V", lambda v: v.tensor_scalar(
                    out=tsc3[:, 0:512], in0=xgsc[:, 0:512], scalar1=-1.0, scalar2=None, op0=ALU.max),
                    waits=w2, tag=f"f2_{og}_{i}", inc=("V", 1))
                P.step("V", lambda v: v.tensor_copy(out=xisc[:, 0:512], in_=tsc3[:, 0:512]),
                       waits=[f"f2_{og}_{i}"], tag=f"f3_{og}_{i}", inc=("V", 1))
                P.step("V", (lambda par, i: lambda v: v.tensor_copy(
                    out=wfb[par].rearrange("p (a b) -> p a b", a=8)[:, i, :],
                    in_=xisc[:, 0:512]))(par, i),
                    waits=[f"f3_{og}_{i}"], tag=f"fcv{og}_{i}", inc=("V", 1))

            def emit_conv(ii):
                engx = ["G", "Sc"][ii % 2]
                P.step(engx, (lambda par, ii: lambda en: ecopy(en,
                    wob[par][:, 256 * ii:256 * (ii + 1)], wsh[par][:, 256 * ii:256 * (ii + 1)]))(par, ii),
                    waits=[f"dol{og}", wob_user[par]], tag=f"dcv{og}_{ii}", inc=(engx, 1))
                st["cvt"].append(f"dcv{og}_{ii}")

            if ph == 0:
                srco = woutT[:, 256 * og:256 * (og + 1)].rearrange("(c p) m -> p c m", p=128)
                srcf = wffnT[:, 256 * og:256 * (og + 1)].rearrange("(c p) m -> p c m", p=128)
                dsto = wsh[par].rearrange("p (c m) -> p c m", m=256)
                dstf = wsh[1 - par].rearrange("p (c m) -> p c m", m=256)
                P.step("Y", (lambda dsto, srco: lambda y: y.dma_start(out=dsto, in_=srco))(dsto, srco),
                       waits=[wo_user[par]], tag=f"dol{og}", inc=("Y", 16))
                P.step("Y", (lambda dstf, srcf: lambda y: y.dma_start(out=dstf, in_=srcf))(dstf, srcf),
                       waits=[wo_user[1 - par]], tag=f"dfl{og}", inc=("Y", 16))
            elif ph == 1:
                for ii in range(0, 6):
                    emit_conv(ii)
            elif ph == 2:
                for ii in range(6, 11):
                    emit_conv(ii)
                emit_f(0)
            elif ph == 3:
                for ii in range(11, 16):
                    emit_conv(ii)
                joiner(f"dcv{og}", st["cvt"])
                wo_user[par] = f"dcv{og}"
                emit_f(1)
            elif ph in (4, 5):
                emit_f(2 * ph - 6)
                emit_f(2 * ph - 5)
            elif ph == 6:
                emit_f(6)
                emit_f(7)
                joiner(f"fcv{og}", [f"fcv{og}_{i}" for i in range(8)])
                wo_user[1 - par] = f"fcv{og}"

        for ph in range(8):
            d_prep(0, ph)

        for og in range(8):
            par = og % 2
            for tc in range(8):
                xw = [f"res{og}_{tc-2}"] if tc >= 2 else ([f"res{og-1}_{tc+6}"] if og > 0 else [])
                P.step("G", (lambda og, tc: lambda g: g.dma_start(
                    out=xnst[:, 256 * (tc % 2):256 * (tc % 2) + 256],
                    in_=x_n[128 * tc:128 * (tc + 1), 256 * og:256 * (og + 1)]))(og, tc),
                    waits=xw, tag=f"xnl{og}_{tc}", inc=("G", 16))
                bD = tc % 2
                bF = 2 + tc % 2

                def domm(og, tc, bD, par):
                    def f(t):
                        r = None
                        for hh in range(16):
                            r = t.matmul(out=pb[bD][:, 0:256],
                                         lhsT=ctxb[:, T * hh + 128 * tc:T * hh + 128 * (tc + 1)],
                                         rhs=wob[par][:, 256 * hh:256 * (hh + 1)],
                                         start=(hh == 0), stop=(hh == 15))
                        return r
                    return f

                def femm(og, tc, bF, par):
                    def f(t):
                        r = None
                        for cc in range(16):
                            r = t.matmul(out=pb[bF][:, 0:256],
                                         lhsT=xqb[:, T * cc + 128 * tc:T * cc + 128 * (tc + 1)],
                                         rhs=wfb[par][:, 256 * cc:256 * (cc + 1)],
                                         start=(cc == 0), stop=(cc == 15))
                        return r
                    return f
                P.step("Tm", domm(og, tc, bD, par),
                       waits=[f"dcv{og}", pbD_user[tc % 2], "cev15_1"], tag=f"dmm{og}_{tc}", inc=("Tm", 1))
                P.step("Tm", femm(og, tc, bF, par),
                       waits=[f"fcv{og}", pbF_user[tc % 2], "exq15"], tag=f"fmm{og}_{tc}", inc=("Tm", 1))
                e1w = [f"dmm{og}_{tc}", f"xnl{og}_{tc}"]
                if tc >= 2:
                    e1w.append(f"res{og}_{tc-2}")
                elif og > 0:
                    e1w.append(f"res{og-1}_{tc+6}")
                P.step("V", (lambda og, tc, bD: lambda v: v.tensor_tensor(
                    out=o1b[:, 256 * (tc % 2):256 * (tc % 2) + 256], in0=pb[bD][:, 0:256],
                    in1=xnst[:, 256 * (tc % 2):256 * (tc % 2) + 256], op=ALU.add))(og, tc, bD),
                    waits=e1w, tag=f"ev1{og}_{tc}", inc=("V", 1))
                pbD_user[tc % 2] = f"ev1{og}_{tc}"
                rw2 = [f"ev1{og}_{tc}", f"fmm{og}_{tc}", "qgam"]
                if tc >= 2:
                    rw2.append(f"sto{og}_{tc-2}")
                elif og > 0:
                    rw2.append(f"sto{og-1}_{tc+6}")
                P.step("V", (lambda og, tc, bF: lambda v: v.scalar_tensor_tensor(
                    out=outsb[:, 512 * (tc % 2):512 * (tc % 2) + 256], in0=pb[bF][:, 0:256],
                    scalar=gamc, in1=o1b[:, 256 * (tc % 2):256 * (tc % 2) + 256],
                    op0=ALU.mult, op1=ALU.add))(og, tc, bF),
                    waits=rw2, tag=f"res{og}_{tc}", inc=("V", 1))
                pbF_user[tc % 2] = f"res{og}_{tc}"
                P.step("G", (lambda og, tc: lambda g: g.dma_start(
                    out=out_d[128 * tc:128 * (tc + 1), 256 * og:256 * (og + 1)],
                    in_=outsb[:, 512 * (tc % 2):512 * (tc % 2) + 256]))(og, tc),
                    waits=[f"res{og}_{tc}"], tag=f"sto{og}_{tc}", inc=("G", 16))
                d_prep(og + 1, tc)
            wob_user[par] = f"dmm{og}_7"
            wfb_user[par] = f"fmm{og}_7"

        if debug2:
            ALLST = [f"sto{og}_{tc}" for og in range(8) for tc in range(8)]
            P.step("G", lambda g: g.dma_start(out=dbg["ctx"], in_=ctxb), waits=ALLST, tag="d2a", inc=("G", 16))
            P.step("G", lambda g: g.dma_start(out=dbg["xqb"], in_=xqb), waits=["d2a"], tag="d2b", inc=("G", 16))
            def d2scf(v):
                v.tensor_copy(out=outsb[:, 0:1], in_=sc_bc)
                v.tensor_copy(out=outsb[:, 1:2], in_=tmpb)
                v.tensor_copy(out=outsb[:, 2:3], in_=gamc)
                v.tensor_copy(out=outsb[:, 3:4], in_=s127)
                return v.tensor_copy(out=outsb[:, 4:5], in_=gmax)
            P.step("V", d2scf, waits=ALLST, tag="d2sc", inc=("V", 1))
            P.step("G", lambda g: g.dma_start(out=dbg["sc"], in_=outsb[:, 0:16]), waits=["d2sc"], tag="d2c", inc=("G", 16))
            P.step("G", lambda g: g.dma_start(out=dbg["pn"][0], in_=p_n[0]), waits=["d2c"], tag="d2d", inc=("G", 16))
            P.step("G", lambda g: g.dma_start(out=dbg["pn"][1], in_=p_n[1]), waits=["d2d"], tag="d2e", inc=("G", 16))
            P.step("G", lambda g: g.dma_start(out=dbg["z"], in_=smal_t[:, :]), waits=["d2e"], tag="d2f", inc=("G", 16))
            P.step("G", lambda g: g.dma_start(out=dbg["v"], in_=vsb), waits=["d2f"], tag="d2g", inc=("G", 16))
            P.step("G", lambda g: g.dma_start(out=dbg["st"], in_=stat_t[:, :]), waits=["d2g"], tag="d2h", inc=("G", 16))
            P.step("G", lambda g: g.dma_start(out=dbg["bc"][:, 0:1024], in_=mubc2),
                   waits=["d2h"], tag="d2i0", inc=("G", 16))
            P.step("G", lambda g: g.dma_start(out=dbg["bc"][:, 1024:2048], in_=rhobc2),
                   waits=["d2i0"], tag="d2i", inc=("G", 16))
        P.step("G", None, waits=[f"sto{og}_{tc}" for og in range(8) for tc in range(8)]
               + (["d2i"] if debug2 else []))

        for (e_, fn_, waits_, inc_, pad_, dup_) in P.steps:
            for (sn, val) in waits_:
                assert val <= P.cnt[sn], (sn, val, P.cnt[sn])

        @block.sync
        def _(y):
            P.emit("Y", y, sems)

        @block.vector
        def _(v):
            P.emit("V", v, sems, padfn=lambda en: en.memset(junkv, 0.0))

        @block.scalar
        def _(s):
            def scpad(en):
                en.memzero(junks)
                return en.memzero(junks)
            P.emit("Sc", s, sems, padfn=scpad)

        @block.tensor
        def _(t):
            P.emit("Tm", t, sems,
                   padfn=lambda en: en.transpose(out=ptp[0][0:1, 0:128], in_=onb_t[:, :],
                                                 identity=idb_t[:, :]))

        @block.gpsimd
        def _(g):
            P.emit("G", g, sems, padfn=lambda en: en.memset(junkg, 0.0))

    return nc


def _host_prep(inputs):
    x = np.ascontiguousarray(np.asarray(inputs["L0_attn_x"], dtype=np.float32).reshape(NTOK, D))
    wqkvT = np.ascontiguousarray(np.asarray(inputs["L0_attn_w_qkv"], dtype=np.float32).T)
    woutT = np.ascontiguousarray(np.asarray(inputs["L0_attn_w_out"], dtype=np.float32).T)
    wffnT = np.ascontiguousarray(np.asarray(inputs["L0_ffn_weight"], dtype=np.float32).T)
    nw = np.asarray(inputs["L0_attn_norm_w"], dtype=np.float32)
    gf = np.asarray(inputs["L0_ffn_norm_weight"], dtype=np.float32)
    pm = np.asarray(inputs["L0_attn_pm"], dtype=np.float32)

    nw_f = np.ascontiguousarray(nw.reshape(16, 128).T)
    gf_f = np.ascontiguousarray(gf.reshape(16, 128).T)
    pmk_h = np.ascontiguousarray(pm[0].transpose(2, 0, 1).reshape(128, 256)).astype(ml_dtypes.bfloat16)
    pmv_h = np.ascontiguousarray(pm[1].transpose(1, 0, 2).reshape(16, 2048)).astype(ml_dtypes.bfloat16)

    inv = 1.0 / (10000.0 ** (np.arange(0, DH, 2, dtype=np.float64) / DH))
    ptmat = np.zeros((128, 128), dtype=np.float32)
    for i in range(64):
        ptmat[2 * i, 2 * i + 1] = -1.0
        ptmat[2 * i + 1, 2 * i] = 1.0
    ptm_h = np.ascontiguousarray(ptmat.T).astype(ml_dtypes.bfloat16)
    idb_h = np.eye(128, dtype=np.float32).astype(ml_dtypes.bfloat16)
    idf_h = np.eye(128, dtype=np.float32)
    onb_h = np.ones((128, 1), dtype=np.float32).astype(ml_dtypes.bfloat16)
    onf_h = np.ones((1, 128), dtype=np.float32)
    onc_h = np.ones((128, 1), dtype=np.float32)
    r = np.arange(128)
    c = np.arange(128)
    mdiag = np.where(c[None, :] <= r[:, None], 0.0, -1.0e30).astype(np.float32).astype(ml_dtypes.bfloat16)
    mfirst = np.where((c[None, :] <= r[:, None]) | (c[None, :] < 16), 0.0, -1.0e30).astype(np.float32).astype(ml_dtypes.bfloat16)

    in_maps = []
    for core in range(NCORES):
        sh = x[T * core:T * (core + 1)]
        pos = np.arange(T * core, T * (core + 1), dtype=np.float64)
        fr = pos[:, None] * inv[None, :]
        cosx = np.repeat(np.cos(fr), 2, axis=1).T.astype(np.float32)
        sinx = np.repeat(np.sin(fr), 2, axis=1).T.astype(np.float32)
        in_maps.append(dict(
            x_t=np.ascontiguousarray(sh.T), x_n=np.ascontiguousarray(sh),
            wqkvT=wqkvT, woutT=woutT, wffnT=wffnT, nw_f=nw_f, gf_f=gf_f,
            pmk_h=pmk_h, pmv_h=pmv_h,
            cos_h=np.ascontiguousarray(cosx).astype(ml_dtypes.bfloat16),
            sin_h=np.ascontiguousarray(sinx).astype(ml_dtypes.bfloat16),
            ptm_h=ptm_h, idb_h=idb_h, idf_h=idf_h, onb_h=onb_h, onf_h=onf_h,
            onc_h=onc_h, mfirst_h=mfirst, mdiag_h=mdiag,
        ))
    return in_maps


def kernel(**inputs):
    in_maps = _host_prep(inputs)
    nc = build_nc(debug=DEBUG)
    res = run_bass_kernel_spmd(nc, in_maps, core_ids=list(range(NCORES)))
    out = np.concatenate([r["out"] for r in res.results], axis=0)
    return out.reshape(1, NTOK, D)


if __name__ == "__main__":
    import reference
    inp = reference.setup_inputs(0)
    got = kernel(**{k: np.asarray(v) for k, v in inp.items()})
    ref = np.asarray(reference.reference(**inp))
    err = np.mean((got - ref) ** 2) / np.mean(ref ** 2)
    print("resvar:", err, "maxabs:", np.max(np.abs(got - ref)))


# version 26
# speedup vs baseline: 1.1124x; 1.0253x over previous
import numpy as np
import ml_dtypes
from collections import defaultdict
from contextlib import ExitStack

import concourse.bass as bass
import concourse.mybir as mybir
from concourse.bass_utils import run_bass_kernel_spmd

F32 = mybir.dt.float32
BF16 = mybir.dt.bfloat16
I32 = mybir.dt.int32
AF = mybir.ActivationFunctionType
ALU = mybir.AluOpType
AX = mybir.AxisListType.X

NCORES = 8
T = 1024
D = 2048
H = 16
DH = 128
WIN = 2
CSC = float(DH) ** -0.5
NEG = -3.0e38
NTOK = 8192

DEBUG = False


class Prog:
    def __init__(self):
        self.steps = []
        self.cnt = defaultdict(int)
        self.tags = {}

    def step(self, eng, fn, waits=(), tag=None, inc=None, pad=False, dup=False):
        for t in waits:
            assert t in self.tags, f"wait on undefined tag {t}"
        if inc is not None:
            self.cnt[inc[0]] += inc[1]
            if tag is not None:
                self.tags[tag] = (inc[0], self.cnt[inc[0]])
        else:
            assert tag is None
        self.steps.append((eng, fn, tuple(self.tags[t] for t in waits), inc, pad, dup))

    def emit(self, eng_name, eng, sems, padfn=None):
        for (e, fn, waits, inc, pad, dup) in self.steps:
            if e != eng_name:
                continue
            byname = {}
            for (sname, val) in waits:
                byname[sname] = max(byname.get(sname, 0), val)
            for sname, val in byname.items():
                eng.wait_ge(sems[sname], val)
            if fn is None:
                continue
            inst = fn(eng)
            if dup:
                inst = fn(eng)
            if pad:
                assert padfn is not None
                inst = padfn(eng)
            if inc is not None:
                assert inst is not None
                inst.then_inc(sems[inc[0]], inc[1])


def ecopy(en, out, in_):
    if hasattr(en, "tensor_copy"):
        return en.tensor_copy(out=out, in_=in_)
    return en.copy(out=out, in_=in_)


def build_nc(debug=False, debug2=False):
    nc = bass.Bass("TRN2", target_bir_lowering=False)

    x_t = nc.dram_tensor("x_t", [D, T], F32, kind="ExternalInput").ap()
    x_n = nc.dram_tensor("x_n", [T, D], F32, kind="ExternalInput").ap()
    wqkvT = nc.dram_tensor("wqkvT", [D, 3 * D], F32, kind="ExternalInput").ap()
    woutT = nc.dram_tensor("woutT", [D, D], F32, kind="ExternalInput").ap()
    wffnT = nc.dram_tensor("wffnT", [D, D], F32, kind="ExternalInput").ap()
    nw_f = nc.dram_tensor("nw_f", [128, 16], F32, kind="ExternalInput").ap()
    gf_f = nc.dram_tensor("gf_f", [128, 16], F32, kind="ExternalInput").ap()
    pmk_h = nc.dram_tensor("pmk_h", [128, 256], BF16, kind="ExternalInput").ap()
    pmv_h = nc.dram_tensor("pmv_h", [16, 2048], BF16, kind="ExternalInput").ap()
    cos_h = nc.dram_tensor("cos_h", [128, T], BF16, kind="ExternalInput").ap()
    sin_h = nc.dram_tensor("sin_h", [128, T], BF16, kind="ExternalInput").ap()
    ptm_h = nc.dram_tensor("ptm_h", [128, 128], BF16, kind="ExternalInput").ap()
    idb_h = nc.dram_tensor("idb_h", [128, 128], BF16, kind="ExternalInput").ap()
    idf_h = nc.dram_tensor("idf_h", [128, 128], F32, kind="ExternalInput").ap()
    onb_h = nc.dram_tensor("onb_h", [128, 1], BF16, kind="ExternalInput").ap()
    onf_h = nc.dram_tensor("onf_h", [1, 128], F32, kind="ExternalInput").ap()
    onc_h = nc.dram_tensor("onc_h", [128, 1], F32, kind="ExternalInput").ap()
    mfirst_h = nc.dram_tensor("mfirst_h", [128, 128], BF16, kind="ExternalInput").ap()
    mdiag_h = nc.dram_tensor("mdiag_h", [128, 128], BF16, kind="ExternalInput").ap()
    out_d = nc.dram_tensor("out", [T, D], F32, kind="ExternalOutput").ap()
    cc_in = nc.dram_tensor("cc_in", [128, 128], F32, kind="Internal").ap()
    cc_out = nc.dram_tensor("cc_out", [128, 128], F32, kind="Internal").ap()

    dbg = {}
    if debug2:
        dbg["ctx"] = nc.dram_tensor("dbg_ctx", [128, H * T], BF16, kind="ExternalOutput").ap()
        dbg["xqb"] = nc.dram_tensor("dbg_xqb", [128, 16 * T], BF16, kind="ExternalOutput").ap()
        dbg["sc"] = nc.dram_tensor("dbg_sc", [128, 16], F32, kind="ExternalOutput").ap()
        dbg["pn"] = nc.dram_tensor("dbg_pn", [2, 128, 544], BF16, kind="ExternalOutput").ap()
        dbg["z"] = nc.dram_tensor("dbg_z", [128, 128], F32, kind="ExternalOutput").ap()
        dbg["v"] = nc.dram_tensor("dbg_v", [128, 8 * D], BF16, kind="ExternalOutput").ap()
        dbg["st"] = nc.dram_tensor("dbg_st", [128, 2048], F32, kind="ExternalOutput").ap()
        dbg["bc"] = nc.dram_tensor("dbg_bc", [128, 2048], F32, kind="ExternalOutput").ap()
        dbg["st0"] = nc.dram_tensor("dbg_st0", [1, 2048], F32, kind="ExternalOutput").ap()
        dbg["gfa"] = nc.dram_tensor("dbg_gfa", [128, 16], F32, kind="ExternalOutput").ap()
        dbg["mb"] = nc.dram_tensor("dbg_mb", [128, 1024], F32, kind="ExternalOutput").ap()
        dbg["rb"] = nc.dram_tensor("dbg_rb", [128, 1024], F32, kind="ExternalOutput").ap()
    if debug:
        dbg["xab"] = nc.dram_tensor("dbg_xab", [128, 16 * T], BF16, kind="ExternalOutput").ap()
        dbg["qh"] = nc.dram_tensor("dbg_qh", [H, 128, T], BF16, kind="ExternalOutput").ap()
        dbg["kh"] = nc.dram_tensor("dbg_kh", [H, 128, T], BF16, kind="ExternalOutput").ap()
        dbg["v"] = nc.dram_tensor("dbg_v", [128, 8 * D], BF16, kind="ExternalOutput").ap()
        dbg["st"] = nc.dram_tensor("dbg_st", [128, 2048], F32, kind="ExternalOutput").ap()
        dbg["bc"] = nc.dram_tensor("dbg_bc", [128, 2048], F32, kind="ExternalOutput").ap()
        dbg["st0"] = nc.dram_tensor("dbg_st0", [1, 2048], F32, kind="ExternalOutput").ap()
        dbg["gfa"] = nc.dram_tensor("dbg_gfa", [128, 16], F32, kind="ExternalOutput").ap()
        dbg["mb"] = nc.dram_tensor("dbg_mb", [128, 1024], F32, kind="ExternalOutput").ap()
        dbg["rb"] = nc.dram_tensor("dbg_rb", [128, 1024], F32, kind="ExternalOutput").ap()
        dbg["p"] = nc.dram_tensor("dbg_p", [H, WIN, 4, 128, 544], BF16, kind="ExternalOutput").ap()
        dbg["ctx"] = nc.dram_tensor("dbg_ctx", [128, H * T], BF16, kind="ExternalOutput").ap()
        dbg["xqb"] = nc.dram_tensor("dbg_xqb", [128, 16 * T], BF16, kind="ExternalOutput").ap()
        dbg["sc"] = nc.dram_tensor("dbg_sc", [128, 16], F32, kind="ExternalOutput").ap()

    with ExitStack() as ctx:
        e = ctx.enter_context

        xab_t = e(nc.sbuf_tensor("xab", [128, 16 * T], BF16))
        vsb_t = e(nc.sbuf_tensor("vsb", [128, 8 * D], BF16))
        ctx_t = e(nc.sbuf_tensor("ctxb", [128, H * T], BF16))
        wst_t = e(nc.sbuf_tensor("wstage", [128, 8192], F32))
        wbf0_t = e(nc.sbuf_tensor("wbf0", [128, 4096], BF16))
        wbf1_t = e(nc.sbuf_tensor("wbf1", [128, 4096], BF16))
        mega_t = e(nc.sbuf_tensor("mega", [128, 10240], BF16))
        stat_t = e(nc.sbuf_tensor("statsl", [128, 2048], F32))
        smal_t = e(nc.sbuf_tensor("smal", [128, 128], F32))
        outs_t = e(nc.sbuf_tensor("outsb", [128, 1024], F32))
        xnst_t = e(nc.sbuf_tensor("xnst", [128, 512], F32))
        o1b_t = e(nc.sbuf_tensor("o1b", [128, 512], F32))
        mfir_t = e(nc.sbuf_tensor("mfirst", [128, 128], BF16))
        mdia_t = e(nc.sbuf_tensor("mdiag", [128, 128], BF16))
        nw_t = e(nc.sbuf_tensor("nwsb", [128, 16], F32))
        gf_t = e(nc.sbuf_tensor("gfsb", [128, 16], F32))
        gfa_t = e(nc.sbuf_tensor("gfab", [128, 16], F32))
        pmk_t = e(nc.sbuf_tensor("pmk", [128, 256], BF16))
        pmv_t = e(nc.sbuf_tensor("pmv", [16, 2048], BF16))
        ptm_t = e(nc.sbuf_tensor("ptm", [128, 128], BF16))
        idb_t = e(nc.sbuf_tensor("idb", [128, 128], BF16))
        idf_t = e(nc.sbuf_tensor("idf", [128, 128], F32))
        onb_t = e(nc.sbuf_tensor("onb", [128, 1], BF16))
        onf_t = e(nc.sbuf_tensor("onf", [1, 128], F32))
        onc_t = e(nc.sbuf_tensor("onc", [128, 1], F32))
        ccst_t = e(nc.sbuf_tensor("ccst", [128, 128], F32))
        pmpt_t = [e(nc.sbuf_tensor(f"pmpt{w}", [16, 512], BF16)) for w in range(2)]
        ptj_t = [[e(nc.sbuf_tensor(f"ptj{w}_{j}", [128, 512 - 128 * j], BF16)) for j in range(4)]
                 for w in range(2)]
        qtmp_t = e(nc.sbuf_tensor("qtmp", [128, 1024], BF16))
        qtmk_t = e(nc.sbuf_tensor("qtmk", [128, 1024], BF16))
        rots_t = e(nc.sbuf_tensor("rotsb", [128, 1024], BF16))
        rotk_t = e(nc.sbuf_tensor("rotkb", [128, 1024], BF16))
        t1b_t = e(nc.sbuf_tensor("t1b", [128, 1024], BF16))

        xab = xab_t[:, :]
        vsb = vsb_t[:, :]
        ctxb = ctx_t[:, :]
        wst = wst_t[:, :]
        mega = mega_t[:, :]
        outsb = outs_t[:, :]
        xnst = xnst_t[:, :]
        o1b = o1b_t[:, :]
        qtmp_b = [qtmp_t[:, :], qtmk_t[:, :]]
        rots_b = [rots_t[:, :], rotk_t[:, :]]
        t1b = t1b_t[:, :]
        wbf = [wbf0_t[:, :], wbf1_t[:, :]]
        ptj = [[p[:, :] for p in row] for row in ptj_t]
        pmpt = [p[:, :] for p in pmpt_t]

        qh = [mega[:, 0:1024], mega[:, 1024:2048]]
        kh = [mega[:, 2048:3072], mega[:, 3072:4096]]
        cosb = mega[:, 4096:5120]
        sinb = mega[:, 5120:6144]
        p_sb = [mega[:, 6144:6688], mega[:, 6688:7232]]
        p_n = [mega[:, 7232:7776], mega[:, 7776:8320]]
        wfb = [mega[:, 0:4096], mega[:, 4096:8192]]
        wob = [wbf[0], wbf[1]]

        xsc = [ctxb[:, 2048 * s:2048 * (s + 1)].bitcast(F32) for s in range(4)]
        tsc = ctxb[:, 8192:10240].bitcast(F32)
        xhsc = ctxb[:, 10240:12288].bitcast(F32)
        xbsc = ctxb[:, 12288:13312]
        sqsc = ctxb[:, 13312:14336]
        mubc = wbf[0][:, 0:2048].bitcast(F32)
        rhobc = wbf[1][:, 0:2048].bitcast(F32)
        xsc3 = [xab[:, 2048 * s:2048 * (s + 1)].bitcast(F32) for s in range(2)]
        tsc3 = xab[:, 4096:6144].bitcast(F32)
        xhsc3 = xab[:, 6144:8192].bitcast(F32)
        mubc2 = xab[:, 8192:10240].bitcast(F32)
        rhobc2 = xab[:, 10240:12288].bitcast(F32)
        xgsc = xab[:, 12288:14336].bitcast(F32)
        xisc = xab[:, 14336:16384].bitcast(I32)

        def sl(row, a, b):
            return stat_t[row:row + 1, a:b]
        mu_r = lambda tt: sl(0, 512 * tt, 512 * (tt + 1))
        rho_r = lambda tt: sl(0, 1024 + 512 * tt, 1024 + 512 * (tt + 1))
        aux1 = lambda tt: outs_t[0:1, 512 * tt:512 * (tt + 1)]
        aux2 = lambda tt: (xnst_t if tt == 0 else o1b_t)[0:1, 0:512]

        S_ = lambda i: smal_t[:, i:i + 1]
        gmax = S_(0)
        am_c, am2_c = S_(1), S_(2)
        m_c = [S_(3), S_(4)]
        m2_c = [S_(5), S_(6)]
        negm_c = [S_(7), S_(8)]
        za = [S_(9), S_(10)]
        zb = [S_(11), S_(12)]
        zs = [S_(13), S_(38)]
        rz = [S_(14), S_(39)]
        swacc = [S_(15 + i) for i in range(8)]
        swacc_b = S_(23)
        swsum, sw_s, rw_bc, sc_bc, s127, gamc, tmpa, tmpb, junk = (
            S_(24), S_(25), S_(26), S_(27), S_(28), S_(29), S_(30), S_(31), S_(32))
        eps5_c, eps6_c = S_(33), S_(34)
        junkv, junks, junkg = S_(35), S_(36), S_(37)
        amall = smal_t[:, 40:56]

        pb = [e(nc.psum_tensor(f"pb{i}", [128, 512], F32)) for i in range(6)]
        ptp = [e(nc.psum_tensor(f"ptp{i}", [128, 128], BF16)) for i in range(2)]
        sa = [pb[2], pb[3]]
        ctxps = [pb[4], pb[4]]
        rotps = pb[5]

        sems = {}
        for sname in ["Y", "G", "Tm", "V", "Sc"]:
            sems[sname] = e(nc.semaphore(f"sem{sname}"))
        block = e(nc.Block())

        P = Prog()

        def joiner(name, waits):
            P.step("V", lambda v: v.memset(junk, 0.0), waits=waits, tag=name, inc=("V", 1))

        def cdma(dst, src):
            return lambda g: g.dma_start(out=dst, in_=src)
        consts = [
            (nw_t[:, :], nw_f), (gf_t[:, :], gf_f), (pmk_t[:, :], pmk_h), (pmv_t[:, :], pmv_h),
            (cosb, cos_h), (sinb, sin_h), (ptm_t[:, :], ptm_h), (idb_t[:, :], idb_h),
            (idf_t[:, :], idf_h), (onb_t[:, :], onb_h), (onf_t[:, :], onf_h), (onc_t[:, :], onc_h),
            (mfir_t[:, :], mfirst_h), (mdia_t[:, :], mdiag_h),
        ]
        for i, (dst, src) in enumerate(consts):
            P.step("G", cdma(dst, src), tag=f"c{i}", inc=("G", 16))
        P.step("V", lambda v: v.memset(smal_t[:, :], 0.0), waits=[], tag="sm0", inc=("V", 1), pad=True)
        P.step("Sc", lambda s: s.activation(out=gfa_t[:, :], in_=gf_t[:, :], func=AF.Abs),
               waits=["c1"], tag="gfab", inc=("Sc", 1), pad=True)
        def epsf(v):
            v.memset(eps5_c, 1e-5)
            return v.memset(eps6_c, 1e-6)
        P.step("V", epsf, waits=[], tag="eps", inc=("V", 1))

        for cc in range(16):
            w = [f"amm{cc-4}"] if cc >= 4 else []
            P.step("Y", cdma(xsc[cc % 4][:, :], x_t[128 * cc:128 * (cc + 1), :]),
                   waits=w, tag=f"axl{cc}", inc=("Y", 16))
            P.step("V", (lambda cc: lambda v: v.tensor_copy(out=xbsc, in_=xsc[cc % 4][:, :]))(cc),
                   waits=[f"axl{cc}"] + ([f"amm{cc-1}"] if cc > 0 else []),
                   tag=f"axb{cc}", inc=("V", 1))
            P.step("Sc", (lambda cc: lambda s: s.activation(out=sqsc, in_=xsc[cc % 4][:, :], func=AF.Square))(cc),
                   waits=[f"axl{cc}"] + ([f"amm{cc-1}"] if cc > 0 else []),
                   tag=f"asq{cc}", inc=("Sc", 1))

            def statmm(cc):
                def f(t):
                    first, last = (cc == 0), (cc == 15)
                    t.matmul(out=pb[0][0:1, :], lhsT=onb_t[:, :], rhs=xbsc[:, 0:512], start=first, stop=last)
                    t.matmul(out=pb[1][0:1, :], lhsT=onb_t[:, :], rhs=xbsc[:, 512:1024], start=first, stop=last)
                    t.matmul(out=pb[2][0:1, :], lhsT=onb_t[:, :], rhs=sqsc[:, 0:512], start=first, stop=last)
                    return t.matmul(out=pb[3][0:1, :], lhsT=onb_t[:, :], rhs=sqsc[:, 512:1024], start=first, stop=last)
                return f
            P.step("Tm", statmm(cc), waits=[f"axb{cc}", f"asq{cc}", "c9"],
                   tag=f"amm{cc}", inc=("Tm", 1), pad=(cc == 15))

        for tt in range(2):
            P.step("V", (lambda tt: lambda v: v.tensor_scalar(
                out=mu_r(tt), in0=pb[tt][0:1, :], scalar1=1.0 / 2048.0, scalar2=None, op0=ALU.mult))(tt),
                waits=["amm15"], tag=f"amu{tt}", inc=("V", 1), pad=True)
            P.step("V", (lambda tt: lambda v: v.tensor_scalar(
                out=aux1(tt), in0=pb[2 + tt][0:1, :], scalar1=1.0 / 2048.0, scalar2=None, op0=ALU.mult))(tt),
                waits=["amm15"], tag=f"am2{tt}", inc=("V", 1), pad=True)
            P.step("Sc", (lambda tt: lambda s: s.activation(out=aux2(tt), in_=mu_r(tt), func=AF.Square))(tt),
                   waits=[f"amu{tt}"], tag=f"amsq{tt}", inc=("Sc", 1), pad=True)
            P.step("V", (lambda tt: lambda v: v.tensor_tensor(
                out=aux1(tt), in0=aux1(tt), in1=aux2(tt), op=ALU.subtract))(tt),
                waits=[f"am2{tt}", f"amsq{tt}"], tag=f"avar{tt}", inc=("V", 1), pad=True)
            P.step("Sc", (lambda tt: lambda s: s.activation(out=aux2(tt), in_=aux1(tt), func=AF.Ln, bias=eps5_c[0:1, :]))(tt),
                   waits=[f"avar{tt}", "eps"], tag=f"alnv{tt}", inc=("Sc", 1), pad=True)
            P.step("Sc", (lambda tt: lambda s: s.activation(out=rho_r(tt), in_=aux1(tt), func=AF.Ln))(tt),
                   waits=[f"avar{tt}"], tag=f"alnt{tt}", inc=("Sc", 1), pad=True)
            P.step("Sc", (lambda tt: lambda s: s.activation(out=aux1(tt), in_=aux2(tt), func=AF.Exp, scale=-0.5))(tt),
                   waits=[f"alnv{tt}", f"alnt{tt}"], tag=f"ar1{tt}", inc=("Sc", 1), pad=True)
            P.step("V", (lambda tt: lambda v: v.tensor_tensor(
                out=rho_r(tt), in0=rho_r(tt), in1=aux2(tt), op=ALU.subtract))(tt),
                waits=[f"ar1{tt}"], tag=f"alnd{tt}", inc=("V", 1), pad=True)
            P.step("Sc", (lambda tt: lambda s: s.activation(out=rho_r(tt), in_=rho_r(tt), func=AF.Exp))(tt),
                   waits=[f"alnd{tt}"], tag=f"at2{tt}", inc=("Sc", 1), pad=True)
            P.step("Sc", (lambda tt: lambda s: s.activation(out=rho_r(tt), in_=rho_r(tt), func=AF.Ln, bias=eps6_c[0:1, :]))(tt),
                   waits=[f"at2{tt}", "eps"], tag=f"alt2{tt}", inc=("Sc", 1), pad=True)
            P.step("Sc", (lambda tt: lambda s: s.activation(out=rho_r(tt), in_=rho_r(tt), func=AF.Exp, scale=-0.5))(tt),
                   waits=[f"alt2{tt}"], tag=f"ar2{tt}", inc=("Sc", 1), pad=True)
            P.step("V", (lambda tt: lambda v: v.tensor_tensor(
                out=rho_r(tt), in0=rho_r(tt), in1=aux1(tt), op=ALU.mult))(tt),
                waits=[f"ar2{tt}"], tag=f"arho{tt}", inc=("V", 1), pad=True)

        def bcmm(dst_bank, src, tt):
            def f(t):
                return t.matmul(out=pb[dst_bank][:, :], lhsT=onf_t[:, :], rhs=src(tt), start=True, stop=True)
            return f
        for tt in range(2):
            P.step("Tm", bcmm(tt, mu_r, tt), waits=[f"arho{tt}", f"amu{tt}", "c10"],
                   tag=f"bmu{tt}", inc=("Tm", 1), pad=True)
            P.step("Tm", bcmm(2 + tt, rho_r, tt), waits=[f"arho{tt}"], tag=f"brho{tt}", inc=("Tm", 1), pad=True)
        for tt in range(2):
            P.step("V", (lambda tt: lambda v: v.tensor_copy(out=mubc[:, 512 * tt:512 * (tt + 1)], in_=pb[tt][:, :]))(tt),
                   waits=[f"bmu{tt}"], tag=f"emu{tt}", inc=("V", 1))
            P.step("V", (lambda tt: lambda v: v.tensor_copy(out=rhobc[:, 512 * tt:512 * (tt + 1)], in_=pb[2 + tt][:, :]))(tt),
                   waits=[f"brho{tt}"], tag=f"erho{tt}", inc=("V", 1))

        if debug2:
            P.step("G", lambda g: g.dma_start(out=dbg["st0"], in_=stat_t[0:1, :]),
                   waits=["arho0", "arho1", "amu0", "amu1"], tag="dst0", inc=("G", 16))
        if debug2:
            P.step("G", lambda g: g.dma_start(out=dbg["gfa"], in_=gfa_t[:, :]),
                   waits=["gfab"], tag="dgfa", inc=("G", 16))
            P.step("G", lambda g: g.dma_start(out=dbg["mb"], in_=mubc),
                   waits=["emu0", "emu1"], tag="dmb", inc=("G", 16))
            P.step("G", lambda g: g.dma_start(out=dbg["rb"], in_=rhobc),
                   waits=["erho0", "erho1"], tag="drb", inc=("G", 16))
        for cc in range(16):
            w = [f"asub{cc-4}"] if cc >= 4 else ["amm15"]
            P.step("Y", cdma(xsc[cc % 4][:, :], x_t[128 * cc:128 * (cc + 1), :]),
                   waits=w, tag=f"axr{cc}", inc=("Y", 16))
            P.step("V", (lambda cc: lambda v: v.tensor_tensor(
                out=tsc, in0=xsc[cc % 4][:, :], in1=mubc, op=ALU.subtract))(cc),
                waits=[f"axr{cc}", "emu0", "emu1"] + ([f"axh{cc-1}"] if cc > 0 else []),
                tag=f"asub{cc}", inc=("V", 1))
            P.step("V", (lambda cc: lambda v: v.tensor_tensor(
                out=xhsc, in0=tsc, in1=rhobc, op=ALU.mult))(cc),
                waits=[f"asub{cc}", "erho0", "erho1"], tag=f"axh{cc}", inc=("V", 1))
            P.step("V", (lambda cc: lambda v: v.tensor_scalar(
                out=xab[:, T * cc:T * (cc + 1)], in0=xhsc,
                scalar1=nw_t[:, cc:cc + 1], scalar2=None, op0=ALU.mult))(cc),
                waits=[f"axh{cc}", "c0"], tag=f"axab{cc}", inc=("V", 1))
            P.step("V", (lambda cc: lambda v: v.tensor_reduce(
                out=amall[:, cc:cc + 1], in_=xhsc, axis=AX, op=ALU.max, apply_absolute_value=True))(cc),
                waits=[f"axh{cc}"], tag=f"aam{cc}", inc=("V", 1), pad=True)
        P.step("V", lambda v: v.tensor_tensor(out=amall, in0=amall, in1=gfa_t[:, :], op=ALU.mult),
               waits=[f"aam{cc}" for cc in range(16)] + ["gfab", "axab15"], tag="agm_m", inc=("V", 1), pad=True)
        P.step("V", lambda v: v.tensor_reduce(out=gmax, in_=amall, axis=AX, op=ALU.max),
               waits=["agm_m"], tag="agm", inc=("V", 1), pad=True)
        APPLY_DONE = "agm"
        if debug:
            P.step("G", lambda g: g.dma_start(out=dbg["xab"], in_=xab),
                   waits=["axab15", "axab14"], tag="dbgxab", inc=("G", 16))

        P.step("V", lambda v: v.memset(ccst_t[:, :], NEG), waits=[], tag="ccm", inc=("V", 1))
        P.step("V", lambda v: v.tensor_copy(out=ccst_t[:, 0:1], in_=gmax),
               waits=["ccm", APPLY_DONE], tag="ccc", inc=("V", 1), pad=True)

        wsh = [wst[:, 0:4096], wst[:, 4096:8192]]
        nvg = 8
        for g in range(nvg):
            w = [f"vcv{g-2}"] if g >= 2 else []
            src = wqkvT[:, 2 * D + 256 * g:2 * D + 256 * (g + 1)].rearrange("(c p) m -> p c m", p=128)
            dst3 = wsh[g % 2].rearrange("p (c m) -> p c m", m=256)
            P.step("Y", (lambda dst3, src: lambda y: y.dma_start(out=dst3, in_=src))(dst3, src),
                   waits=w, tag=f"vld{g}", inc=("Y", 16))
            wguard = [APPLY_DONE] if g < 2 else [f"vmm{g-2}_7"]
            cvt = []
            for i in range(16):
                engx = ["G", "Sc", "G"][i % 3]
                P.step(engx, (lambda g, i: lambda en: ecopy(en,
                    wbf[g % 2][:, 256 * i:256 * (i + 1)], wsh[g % 2][:, 256 * i:256 * (i + 1)]))(g, i),
                    waits=[f"vld{g}"] + wguard, tag=f"vcv{g}_{i}", inc=(engx, 1))
                cvt.append(f"vcv{g}_{i}")
            joiner(f"vcv{g}", cvt)
            for tc in range(8):
                bank = tc % 4

                def vmm(g, tc, bank):
                    def f(t):
                        r = None
                        for cc in range(16):
                            r = t.matmul(out=pb[bank][:, 0:256],
                                         lhsT=xab[:, T * cc + 128 * tc:T * cc + 128 * (tc + 1)],
                                         rhs=wbf[g % 2][:, 256 * cc:256 * (cc + 1)],
                                         start=(cc == 0), stop=(cc == 15))
                        return r
                    return f
                w = [f"vcv{g}", APPLY_DONE]
                if g > 0 or tc >= 4:
                    pg, ptc = (g, tc - 4) if tc >= 4 else (g - 1, tc + 4)
                    w.append(f"vev{pg}_{ptc}")
                P.step("Tm", vmm(g, tc, bank), waits=w, tag=f"vmm{g}_{tc}", inc=("Tm", 1))
                P.step("V", (lambda g, tc, bank: lambda v: v.tensor_copy(
                    out=vsb[:, D * tc + 256 * g:D * tc + 256 * (g + 1)], in_=pb[bank][:, 0:256]))(g, tc, bank),
                    waits=[f"vmm{g}_{tc}"], tag=f"vev{g}_{tc}", inc=("V", 1))
        if debug:
            P.step("G", lambda g: g.dma_start(out=dbg["v"], in_=vsb),
                   waits=[f"vev{nvg-1}_{tc}" for tc in range(8)], tag="dbgv", inc=("G", 16))

        P.step("V", lambda v: v.memset(sa[0][:, :], -1.0e30),
               waits=["erho0", "erho1", f"vev{nvg-1}_6"], tag="sainit0", inc=("V", 1))
        P.step("V", lambda v: v.memset(sa[1][:, :], -1.0e30),
               waits=["erho0", "erho1", f"vev{nvg-1}_7"], tag="sainit1", inc=("V", 1))
        sa_user = {0: "sainit0", 1: "sainit1"}
        ptp_user = {0: None, 1: None}
        ctx_user = {0: None, 1: None}
        negm_user = {0: None, 1: None}
        qkbank_ev = {0: f"vev{nvg-1}_6", 1: f"vev{nvg-1}_7"}
        rot_user = {0: None}
        av_prev = [None]
        rope_reader = {}

        def emit_qld(hh):
            hhp = hh % 2
            wq_src = wqkvT[:, 128 * hh:128 * (hh + 1)].rearrange("(c p) m -> p c m", p=128)
            wk_src = wqkvT[:, D + 128 * hh:D + 128 * (hh + 1)].rearrange("(c p) m -> p c m", p=128)
            dsth = wsh[hhp].rearrange("p (c m) -> p c m", m=256)
            wwait = [f"vcv{6+hhp}"] if hh < 2 else [f"qcv{hh-2}"]
            if 2 <= hh <= 9:
                wwait.append(f"swa{hh-2}b")
            P.step("Y", (lambda dsth, wq_src: lambda y: y.dma_start(out=dsth[:, :, 0:128], in_=wq_src))(dsth, wq_src),
                   waits=wwait, tag=f"qld{hh}a", inc=("Y", 16))
            P.step("Y", (lambda dsth, wk_src: lambda y: y.dma_start(out=dsth[:, :, 128:256], in_=wk_src))(dsth, wk_src),
                   waits=[], tag=f"qld{hh}", inc=("Y", 16))

        for h in range(16):
            hp = h % 2
            ah = h - 1

            def attn_A(ah, pos):
                w_ = pos % 2
                qc = pos // 2
                par = w_
                ntok = 128 * (qc + 1)
                ahp = ah % 2
                qslice = qh[ahp][:, 512 * w_ + 128 * qc:512 * w_ + 128 * (qc + 1)]
                kslice = kh[ahp][:, 512 * w_:512 * w_ + ntok]
                wts = [f"ropeq{ah}", f"ropek{ah}", "c2", "c12", "c13", "c7", sa_user[par]]
                msk = mfir_t if qc == 0 else mdia_t

                def qkmm(qslice, kslice, par, ntok, ah, qc, msk):
                    def f(t):
                        t.matmul(out=sa[par][:, 0:ntok], lhsT=qslice, rhs=kslice,
                                 start=True, stop=True)
                        mm = t.matmul(out=sa[par][:, 128 * qc:ntok], lhsT=idb_t[:, :],
                                      rhs=msk[:, :], start=False, stop=True)
                        if qc < 3:
                            mm = t.matmul(out=sa[par][:, ntok:ntok + 16], lhsT=qslice,
                                          rhs=pmk_t[:, 16 * ah:16 * (ah + 1)], start=True, stop=True)
                        return mm
                    return f
                P.step("Tm", qkmm(qslice, kslice, par, ntok, ah, qc, msk),
                       waits=wts, tag=f"qk{ah}_{w_}_{qc}", inc=("Tm", 1))
                area = ntok + 16 if qc < 3 else 512
                P.step("V", (lambda par, area: lambda v: v.tensor_reduce(
                    out=negm_c[par], in_=sa[par][:, 0:area], axis=AX, op=ALU.max, negate=True))(par, area),
                    waits=[f"qk{ah}_{w_}_{qc}"] + ([negm_user[par]] if negm_user[par] else []),
                    tag=f"ngm{ah}_{w_}_{qc}", inc=("V", 1), pad=True)

            def attn_PM3(ah, pos):
                w_ = pos % 2
                par = w_
                pm_bank, pm_c0 = 1 - par, 480
                ahp = ah % 2
                qc = pos // 2
                qslice = qh[ahp][:, 512 * w_ + 128 * qc:512 * w_ + 128 * (qc + 1)]
                P.step("Tm", (lambda qslice, pm_bank, pm_c0, ah: lambda t: t.matmul(
                    out=sa[pm_bank][:, pm_c0:pm_c0 + 16], lhsT=qslice,
                    rhs=pmk_t[:, 16 * ah:16 * (ah + 1)], start=True, stop=True))(qslice, pm_bank, pm_c0, ah),
                    waits=[f"qk{ah}_{w_}_3", sa_user[1 - par]], tag=f"pm3{ah}_{w_}", inc=("Tm", 1))

            def attn_B(ah, pos):
                w_ = pos % 2
                qc = pos // 2
                par = w_
                ntok = 128 * (qc + 1)
                pm_bank, pm_c0 = (par, ntok) if qc < 3 else (1 - par, 480)
                expw = [f"ngm{ah}_{w_}_{qc}"]
                if qc == 3:
                    expw.append(f"pm3{ah}_{w_}")
                prev_pn = (f"pn{ah}_{w_}_{qc-1}" if qc >= 1 else
                           (f"pn{ah-1}_{w_}_3" if ah > 0 else None))
                if prev_pn:
                    expw.append(prev_pn)

                def expf(par, ntok, pm_bank, pm_c0):
                    def f(s):
                        s.activation(out=p_sb[par][:, 16:16 + ntok], in_=sa[par][:, 0:ntok],
                                     func=AF.Exp, bias=negm_c[par], scale=1.0, accum_out=za[par])
                        return s.activation(out=p_sb[par][:, 0:16], in_=sa[pm_bank][:, pm_c0:pm_c0 + 16],
                                            func=AF.Exp, bias=negm_c[par], scale=1.0, accum_out=zb[par])
                    return f
                P.step("Sc", expf(par, ntok, pm_bank, pm_c0),
                       waits=expw, tag=f"exp{ah}_{w_}_{qc}", inc=("Sc", 1), pad=True)
                sa_user[par] = f"exp{ah}_{w_}_{qc}"
                if qc == 3:
                    sa_user[1 - par] = f"exp{ah}_{w_}_{qc}"
                negm_user[par] = f"exp{ah}_{w_}_{qc}"

            def attn_C(ah, pos):
                w_ = pos % 2
                qc = pos // 2
                par = w_
                ntok = 128 * (qc + 1)
                P.step("V", (lambda par: lambda v: v.tensor_tensor(
                    out=zs[par], in0=za[par], in1=zb[par], op=ALU.add))(par),
                    waits=[f"exp{ah}_{w_}_{qc}"], tag=f"zz{ah}_{w_}_{qc}", inc=("V", 1), pad=True, dup=True)
                P.step("V", (lambda par: lambda v: v.reciprocal(out=rz[par], in_=zs[par]))(par),
                       waits=[f"zz{ah}_{w_}_{qc}"], tag=f"rzt{ah}_{w_}_{qc}", inc=("V", 1), pad=True, dup=True)
                pnw = [f"rzt{ah}_{w_}_{qc}"]
                prev_tp = (f"tp{ah}_{w_}_{qc-1}" if qc >= 1 else
                           (f"tp{ah-1}_{w_}_3" if ah > 0 else None))
                if prev_tp:
                    pnw.append(prev_tp)
                P.step("V", (lambda par, ntok: lambda v: v.tensor_scalar(
                    out=p_n[par][:, 0:16 + ntok], in0=p_sb[par][:, 0:16 + ntok],
                    scalar1=rz[par], scalar2=None, op0=ALU.mult))(par, ntok),
                    waits=pnw, tag=f"pn{ah}_{w_}_{qc}", inc=("V", 1))
                if debug:
                    P.step("G", (lambda ah, w_, qc, par: lambda g: g.dma_start(
                        out=dbg["p"][ah, w_, qc], in_=p_n[par]))(ah, w_, qc, par),
                        waits=[f"pn{ah}_{w_}_{qc}"], tag=f"dbgp{ah}_{w_}_{qc}", inc=("G", 16))
                ntp = qc + 2
                tpe_tags = []
                for jj in range(ntp):
                    tpar = jj % 2
                    tw = [f"pn{ah}_{w_}_{qc}", "c7"]
                    if ptp_user[tpar]:
                        tw.append(ptp_user[tpar])
                    if jj == 0:
                        srcp = p_n[par][:, 0:16]
                        dstp = ptp[tpar][0:16, :]
                        dste = pmpt[w_][:, 128 * qc:128 * (qc + 1)]
                    else:
                        j = jj - 1
                        srcp = p_n[par][:, 16 + 128 * j:16 + 128 * (j + 1)]
                        dstp = ptp[tpar][:, :]
                        dste = ptj[w_][j][:, 128 * (qc - j):128 * (qc - j + 1)]
                    P.step("Tm", (lambda srcp, dstp: lambda t: t.transpose(
                        out=dstp, in_=srcp, identity=idb_t[:, :]))(srcp, dstp),
                        waits=tw, tag=f"tpp{ah}_{w_}_{qc}_{jj}", inc=("Tm", 1))
                    ew = [f"tpp{ah}_{w_}_{qc}_{jj}"]
                    if av_prev[0]:
                        ew.append(av_prev[0])
                    if jj % 2 == 0:
                        P.step("Sc", (lambda dstp, dste: lambda s: s.copy(out=dste, in_=dstp))(dstp, dste),
                               waits=ew, tag=f"tpe{ah}_{w_}_{qc}_{jj}", inc=("Sc", 1))
                    else:
                        P.step("V", (lambda dstp, dste: lambda v: v.tensor_copy(out=dste, in_=dstp))(dstp, dste),
                               waits=ew, tag=f"tpe{ah}_{w_}_{qc}_{jj}", inc=("V", 1))
                    ptp_user[tpar] = f"tpe{ah}_{w_}_{qc}_{jj}"
                    tpe_tags.append(f"tpe{ah}_{w_}_{qc}_{jj}")
                joiner(f"tp{ah}_{w_}_{qc}", tpe_tags)

            def attn_AV(ah):
                for w_ in range(2):
                    avw = [f"tp{ah}_{w_}_3", "c3"]
                    avw += [f"vev{ah // 2}_{4 * w_ + j}" for j in range(4)]
                    if ctx_user[0]:
                        avw.append(ctx_user[0])

                    def avf(ah, w_):
                        def f(t):
                            t.matmul(out=ctxps[0][:, :], lhsT=pmv_t[:, 128 * ah:128 * (ah + 1)],
                                     rhs=pmpt[w_][:, :], start=True, stop=False)
                            r = None
                            for j in range(4):
                                tcj = 4 * w_ + j
                                nq = 512 - 128 * j
                                r = t.matmul(out=ctxps[0][:, 128 * j:512],
                                             lhsT=vsb[:, D * tcj + 128 * ah:D * tcj + 128 * (ah + 1)],
                                             rhs=ptj[w_][j][:, 0:nq],
                                             start=False, stop=(j == 3))
                            return r
                        return f
                    P.step("Tm", avf(ah, w_), waits=avw, tag=f"av{ah}_{w_}", inc=("Tm", 1))
                    av_prev[0] = f"av{ah}_{w_}"
                    P.step("V", (lambda ah, w_: lambda v: v.tensor_copy(
                        out=ctxb[:, T * ah + 512 * w_:T * ah + 512 * (w_ + 1)], in_=ctxps[0][:, :]))(ah, w_),
                        waits=[f"av{ah}_{w_}"], tag=f"cev{ah}_{w_}", inc=("V", 1))
                    ctx_user[0] = f"cev{ah}_{w_}"
                rope_reader[ah % 2] = f"pm3{ah}_1"

            def rnd(r, ah):
                if ah < 0:
                    return
                if r == 7:
                    attn_PM3(ah, 6)
                if r == 8:
                    attn_PM3(ah, 7)
                if 1 <= r <= 8:
                    attn_B(ah, r - 1)
                if r < 8:
                    attn_A(ah, r)
                if 2 <= r:
                    attn_C(ah, r - 2)

            def qmm_step(part, name, tt):
                bank = tt
                qt_b = qtmp_b[part]

                def qmm(hp, part, tt, bank):
                    def f(t):
                        r = None
                        wb = wbf[hp]
                        for cc in range(16):
                            r = t.matmul(out=pb[bank][:, :],
                                         lhsT=wb[:, 256 * cc + 128 * part:256 * cc + 128 * (part + 1)],
                                         rhs=xab[:, T * cc + 512 * tt:T * cc + 512 * (tt + 1)],
                                         start=(cc == 0), stop=(cc == 15))
                        return r
                    return f
                P.step("Tm", qmm(hp, part, tt, bank),
                       waits=[f"qcv{h}", qkbank_ev[bank]],
                       tag=f"qmm{name}{h}_{tt}", inc=("Tm", 1))
                evw = [f"qmm{name}{h}_{tt}"]
                if h > 0:
                    evw.append(f"qrot{name}{h-1}_{tt}")
                if part == 0:
                    P.step("V", (lambda tt, bank, qt_b: lambda v: v.tensor_scalar(
                        out=qt_b[:, 512 * tt:512 * (tt + 1)], in0=pb[bank][:, :],
                        scalar1=CSC, scalar2=None, op0=ALU.mult))(tt, bank, qt_b),
                        waits=evw, tag=f"qev{name}{h}_{tt}", inc=("V", 1))
                else:
                    P.step("V", (lambda tt, bank, qt_b: lambda v: v.tensor_copy(
                        out=qt_b[:, 512 * tt:512 * (tt + 1)], in_=pb[bank][:, :]))(tt, bank, qt_b),
                        waits=evw, tag=f"qev{name}{h}_{tt}", inc=("V", 1))
                qkbank_ev[bank] = f"qev{name}{h}_{tt}"

            def rot_step(part, name, tt):
                qt_b = qtmp_b[part]
                ro_b = rots_b[part]
                rw_ = [f"qev{name}{h}_{tt}", "c6"]
                if rot_user[0]:
                    rw_.append(rot_user[0])
                P.step("Tm", (lambda tt, qt_b: lambda t: t.matmul(
                    out=rotps[:, :], lhsT=ptm_t[:, :], rhs=qt_b[:, 512 * tt:512 * (tt + 1)],
                    start=True, stop=True))(tt, qt_b),
                    waits=rw_, tag=f"qrot{name}{h}_{tt}", inc=("Tm", 1))
                sw_ = [f"qrot{name}{h}_{tt}"]
                if h > 0:
                    sw_.append(f"rope{name}{h-1}")
                P.step("Sc", (lambda tt, ro_b: lambda s: s.copy(
                    out=ro_b[:, 512 * tt:512 * (tt + 1)], in_=rotps[:, :]))(tt, ro_b),
                    waits=sw_, tag=f"qrev{name}{h}_{tt}", inc=("Sc", 1))
                rot_user[0] = f"qrev{name}{h}_{tt}"

            def rt1_step(part, name):
                qt_b = qtmp_b[part]
                P.step("V", (lambda qt_b: lambda v: v.tensor_tensor(
                    out=t1b, in0=qt_b, in1=cosb, op=ALU.mult))(qt_b),
                    waits=[f"qev{name}{h}_1", "c4"], tag=f"rt1{name}{h}", inc=("V", 1))

            def rt2_rope_step(part, name):
                dstbuf = qh[hp] if part == 0 else kh[hp]
                ro_b = rots_b[part]
                P.step("V", (lambda ro_b: lambda v: v.tensor_tensor(
                    out=ro_b, in0=ro_b, in1=sinb, op=ALU.mult))(ro_b),
                    waits=[f"qrev{name}{h}_0", f"qrev{name}{h}_1", "c5"], tag=f"rt2{name}{h}", inc=("V", 1))
                ropew = [f"rt1{name}{h}", f"rt2{name}{h}"]
                if h >= 2 and rope_reader.get(hp):
                    ropew.append(rope_reader[hp])
                P.step("V", (lambda dstbuf, ro_b: lambda v: v.tensor_tensor(
                    out=dstbuf, in0=t1b, in1=ro_b, op=ALU.add))(dstbuf, ro_b),
                    waits=ropew, tag=f"rope{name}{h}", inc=("V", 1))

            rnd(0, ah)
            if h == 0:
                emit_qld(0)
                emit_qld(1)
            elif h + 1 <= 15:
                emit_qld(h + 1)
            cvw = [f"vmm{6+hp}_7"] if h < 2 else [f"qmmk{h-2}_1"]
            cvt = []
            for i in range(16):
                engx = "Sc" if i % 4 == 3 else "G"
                P.step(engx, (lambda hp, i: lambda en: ecopy(en,
                    wbf[hp][:, 256 * i:256 * (i + 1)], wsh[hp][:, 256 * i:256 * (i + 1)]))(hp, i),
                    waits=[f"qld{h}"] + cvw, tag=f"qcv{h}_{i}", inc=(engx, 1))
                cvt.append(f"qcv{h}_{i}")
            joiner(f"qcv{h}", cvt)
            if h <= 7:
                k = h
                half = k % 2
                srcw_ = wffnT[:, 256 * k:256 * (k + 1)].rearrange("(c p) m -> p c m", p=128)
                dstw_ = wsh[half].rearrange("p (c m) -> p c m", m=256)
                P.step("Y", (lambda dstw_, srcw_: lambda y: y.dma_start(out=dstw_, in_=srcw_))(dstw_, srcw_),
                       waits=[f"qcv{k}"] + ([f"swa{k-2}b"] if k >= 2 else []),
                       tag=f"swl{k}", inc=("Y", 16))
            rnd(1, ah)
            qmm_step(0, "q", 0)
            rnd(2, ah)
            qmm_step(0, "q", 1)
            rot_step(0, "q", 0)
            rnd(3, ah)
            qmm_step(1, "k", 0)
            rot_step(0, "q", 1)
            rt1_step(0, "q")
            rnd(4, ah)
            qmm_step(1, "k", 1)
            rot_step(1, "k", 0)
            rt2_rope_step(0, "q")
            rnd(5, ah)
            rot_step(1, "k", 1)
            rt1_step(1, "k")
            rnd(6, ah)
            rt2_rope_step(1, "k")
            rnd(7, ah)
            rnd(8, ah)
            rnd(9, ah)
            if ah >= 0:
                attn_AV(ah)
            if h <= 7:
                k = h
                half = k % 2
                P.step("Sc", (lambda half, k: lambda s: s.activation(
                    out=outsb.bitcast(BF16)[:, 0:2048], in_=wsh[half][:, 0:2048], func=AF.Abs,
                    accum_out=swacc[k]))(half, k),
                    waits=[f"swl{k}"] + ([f"swv{k-1}"] if k >= 1 else []), tag=f"swa{k}a", inc=("Sc", 1))
                P.step("Sc", (lambda half, k: lambda s: s.activation(
                    out=outsb.bitcast(BF16)[:, 0:2048], in_=wsh[half][:, 2048:4096], func=AF.Abs,
                    accum_out=swacc_b))(half, k),
                    waits=[f"swa{k}a"], tag=f"swa{k}b", inc=("Sc", 1), pad=True)
                P.step("V", (lambda k: lambda v: v.tensor_tensor(
                    out=swacc[k], in0=swacc[k], in1=swacc_b, op=ALU.add))(k),
                    waits=[f"swa{k}b"], tag=f"swv{k}", inc=("V", 1), pad=True)
            if debug:
                P.step("G", (lambda h: lambda g: g.dma_start(out=dbg["qh"][h], in_=qh[h % 2]))(h),
                       waits=[f"ropeq{h}"], tag=f"dbgq{h}", inc=("G", 16))
                P.step("G", (lambda h: lambda g: g.dma_start(out=dbg["kh"][h], in_=kh[h % 2]))(h),
                       waits=[f"ropek{h}"], tag=f"dbgk{h}", inc=("G", 16))
        for r in range(10):
            rnd(r, 15)
        attn_AV(15)

        if debug:
            P.step("G", lambda g: g.dma_start(out=dbg["ctx"], in_=ctxb),
                   waits=["cev15_1", "cev15_0"], tag="dbgctx", inc=("G", 16))

        P.step("G", lambda g: g.dma_start(out=cc_in, in_=ccst_t[:, :]), waits=["ccc"], tag="ccw", inc=("G", 16))
        P.step("G", lambda g: g.collective_compute(
            "AllReduce", ALU.max, replica_groups=[list(range(NCORES))],
            ins=[cc_in.opt()], outs=[cc_out.opt()]),
            waits=["ccw"], tag="ccr", inc=("G", 1))
        P.step("G", lambda g: g.dma_start(out=ccst_t[:, :], in_=cc_out),
               waits=["ccr"], tag="ccb", inc=("G", 16))

        SA0_FIN = sa_user[0]
        SA1_FIN = sa_user[1]
        QK0_FIN = qkbank_ev[0]
        QK1_FIN = qkbank_ev[1]

        P.step("V", lambda v: v.tensor_reduce(out=swsum, in_=smal_t[:, 15:23], axis=AX, op=ALU.add),
               waits=[f"swv{k}" for k in range(8)], tag="sws", inc=("V", 1), pad=True)
        P.step("Tm", lambda t: t.matmul(out=pb[0][0:1, 0:1], lhsT=swsum, rhs=onc_t[:, :],
                                        start=True, stop=True),
               waits=["sws", "c11", QK0_FIN], tag="swm", inc=("Tm", 1), pad=True)
        P.step("V", lambda v: v.tensor_scalar(out=sw_s[0:1, :], in0=pb[0][0:1, 0:1],
                                              scalar1=1.0 / (2048.0 * 2048.0), scalar2=None, op0=ALU.mult),
               waits=["swm"], tag="swv", inc=("V", 1), pad=True)
        P.step("Tm", lambda t: t.transpose(out=pb[1][0:1, 0:128], in_=ccst_t[:, 0:1], identity=idf_t[:, :]),
               waits=["ccb", "c8", QK1_FIN], tag="qst", inc=("Tm", 1), pad=True)
        P.step("V", lambda v: v.tensor_reduce(out=tmpa[0:1, :], in_=pb[1][0:1, 0:128], axis=AX, op=ALU.max),
               waits=["qst"], tag="qsr", inc=("V", 1), pad=True)
        P.step("V", lambda v: v.tensor_scalar(out=tmpa[0:1, :], in0=tmpa[0:1, :], scalar1=1e-5,
                                              scalar2=None, op0=ALU.max),
               waits=["qsr"], tag="qse", inc=("V", 1), pad=True)
        P.step("Tm", lambda t: t.matmul(out=pb[0][:, 0:1], lhsT=onf_t[:, :], rhs=tmpa[0:1, :],
                                        start=True, stop=True),
               waits=["qse", "swv", "c10"], tag="qsb", inc=("Tm", 1), pad=True)
        P.step("V", lambda v: v.tensor_copy(out=sc_bc, in_=pb[0][:, 0:1]),
               waits=["qsb"], tag="qsc", inc=("V", 1), pad=True)
        P.step("Tm", lambda t: t.matmul(out=pb[1][:, 0:1], lhsT=onf_t[:, :], rhs=sw_s[0:1, :], start=True, stop=True),
               waits=["qsc", "swv"], tag="qwb", inc=("Tm", 1), pad=True)
        P.step("V", lambda v: v.tensor_copy(out=tmpb, in_=pb[1][:, 0:1]),
               waits=["qwb"], tag="qwc", inc=("V", 1), pad=True)
        P.step("V", lambda v: v.reciprocal(out=rz[0], in_=sc_bc), waits=["qsc", "pn15_1_3"], tag="qrc", inc=("V", 1), pad=True)
        P.step("V", lambda v: v.tensor_scalar(out=s127, in0=rz[0], scalar1=127.0, scalar2=None, op0=ALU.mult),
               waits=["qrc"], tag="qs127", inc=("V", 1), pad=True)
        P.step("V", lambda v: v.reciprocal(out=rw_bc, in_=tmpb), waits=["qwc"], tag="qrw", inc=("V", 1), pad=True)
        P.step("V", lambda v: v.tensor_tensor(out=gamc, in0=sc_bc, in1=tmpb, op=ALU.mult),
               waits=["qwc"], tag="qg1", inc=("V", 1), pad=True)
        P.step("V", lambda v: v.tensor_scalar(out=gamc, in0=gamc, scalar1=1.0 / 127.0, scalar2=None, op0=ALU.mult),
               waits=["qg1"], tag="qgam", inc=("V", 1), pad=True)
        if debug:
            def dbgsc_f(v):
                v.tensor_copy(out=outsb[:, 0:1], in_=sc_bc)
                v.tensor_copy(out=outsb[:, 1:2], in_=tmpb)
                v.tensor_copy(out=outsb[:, 2:3], in_=gamc)
                v.tensor_copy(out=outsb[:, 3:4], in_=s127)
                for kk in range(8):
                    v.tensor_copy(out=outsb[:, 5 + kk:6 + kk], in_=swacc[kk])
                v.tensor_copy(out=outsb[:, 13:14], in_=swsum)
                return v.tensor_copy(out=outsb[:, 4:5], in_=gmax)
            P.step("V", dbgsc_f, waits=["qgam", "qs127", "swa7b"], tag="dbgscv", inc=("V", 1))
            P.step("G", lambda g: g.dma_start(out=dbg["sc"], in_=outsb[:, 0:16]),
                   waits=["dbgscv"], tag="dbgsc", inc=("G", 16))

        XAB_FREE = "qmmk15_1"
        for tt in range(2):
            P.step("Tm", bcmm(tt, mu_r, tt), waits=["qsb", "qsc", XAB_FREE], tag=f"b2mu{tt}", inc=("Tm", 1), pad=True)
            P.step("Tm", bcmm(2 + tt, rho_r, tt), waits=[SA0_FIN, SA1_FIN, "qwc"], tag=f"b2rho{tt}", inc=("Tm", 1), pad=True)
        for tt in range(2):
            P.step("V", (lambda tt: lambda v: v.tensor_copy(
                out=mubc2[:, 512 * tt:512 * (tt + 1)], in_=pb[tt][:, :]))(tt),
                waits=[f"b2mu{tt}"], tag=f"e2mu{tt}", inc=("V", 1))
            P.step("V", (lambda tt: lambda v: v.tensor_copy(
                out=rhobc2[:, 512 * tt:512 * (tt + 1)], in_=pb[2 + tt][:, :]))(tt),
                waits=[f"b2rho{tt}"], tag=f"e2rho{tt}", inc=("V", 1))

        xqb = vsb
        xisc2 = [tsc3.bitcast(I32), xhsc3.bitcast(I32)]
        for cc in range(16):
            s_ = cc % 2
            w = [f"exi{cc-2}"] if cc >= 2 else ["e2mu0", "e2mu1", "e2rho0", "e2rho1"]
            P.step("Y", (lambda cc, s_: lambda y: y.dma_start(
                out=xsc3[s_][:, :], in_=x_t[128 * cc:128 * (cc + 1), :]))(cc, s_),
                waits=w, tag=f"exl{cc}", inc=("Y", 16))
            P.step("G", (lambda s_: lambda g: g.tensor_tensor(
                out=xsc3[s_][:, :], in0=xsc3[s_][:, :], in1=mubc2, op=ALU.subtract))(s_),
                waits=[f"exl{cc}"], tag=f"esb{cc}", inc=("G", 1))
            P.step("V", (lambda s_: lambda v: v.tensor_tensor(
                out=xsc3[s_][:, :], in0=xsc3[s_][:, :], in1=rhobc2, op=ALU.mult))(s_),
                waits=[f"esb{cc}"], tag=f"exh{cc}", inc=("V", 1))
            P.step("V", (lambda cc, s_: lambda v: v.tensor_scalar(
                out=xsc3[s_][:, :], in0=xsc3[s_][:, :], scalar1=gf_t[:, cc:cc + 1],
                scalar2=s127, op0=ALU.mult, op1=ALU.mult))(cc, s_),
                waits=[f"exh{cc}", "qs127", "c1"], tag=f"exg{cc}", inc=("V", 1))
            iw = [f"exg{cc}"]
            if cc >= 2:
                iw.append(f"exq{cc-2}")
            P.step("Sc", (lambda cc, s_: lambda s: s.copy(out=xisc2[s_], in_=xsc3[s_][:, :]))(cc, s_),
                   waits=iw, tag=f"exi{cc}", inc=("Sc", 1))
            qw = [f"exi{cc}"]
            if cc == 0:
                qw += ["av15_1", "cev15_1"]
            P.step("V", (lambda cc, s_: lambda v: v.tensor_copy(
                out=xqb[:, T * cc:T * (cc + 1)], in_=xisc2[s_]))(cc, s_),
                waits=qw, tag=f"exq{cc}", inc=("V", 1))
        if debug:
            P.step("G", lambda g: g.dma_start(out=dbg["xqb"], in_=xqb),
                   waits=["exq15"], tag="dbgxqb", inc=("G", 16))

        wo_user = {0: "swa6b", 1: "swa7b"}
        wob_user = {0: "qmmk14_1", 1: "qmmk15_1"}
        wfb_user = {0: "tp15_1_3", 1: "tp15_1_3"}
        pbD_user = {0: "e2mu0", 1: "e2mu1"}
        pbF_user = {0: "e2rho0", 1: "e2rho1"}
        prep_state = {}

        def d_prep(og, ph):
            if og > 7:
                return
            par = og % 2
            st = prep_state.setdefault(og, {"cvt": []})

            def emit_f(i):
                c0 = 512 * i
                srcw = wsh[1 - par][:, c0:c0 + 512]
                prevd = f"f2_{og}_{i-1}" if i > 0 else (f"f2_{og-1}_7" if og > 0 else None)
                prevc = f"f3_{og}_{i-1}" if i > 0 else (f"f3_{og-1}_7" if og > 0 else None)
                w1 = [f"dfl{og}", "qrw", wfb_user[par], "exq15"]
                if prevd:
                    w1.append(prevd)
                P.step("V", (lambda srcw: lambda v: v.tensor_scalar(
                    out=xgsc[:, 0:512], in0=srcw, scalar1=rw_bc, scalar2=1.0,
                    op0=ALU.mult, op1=ALU.min))(srcw),
                    waits=w1, tag=f"f1_{og}_{i}", inc=("V", 1))
                w2 = [f"f1_{og}_{i}"]
                if prevc:
                    w2.append(prevc)
                P.step("V", lambda v: v.tensor_scalar(
                    out=tsc3[:, 0:512], in0=xgsc[:, 0:512], scalar1=-1.0, scalar2=None, op0=ALU.max),
                    waits=w2, tag=f"f2_{og}_{i}", inc=("V", 1))
                P.step("V", lambda v: v.tensor_copy(out=xisc[:, 0:512], in_=tsc3[:, 0:512]),
                       waits=[f"f2_{og}_{i}"], tag=f"f3_{og}_{i}", inc=("V", 1))
                P.step("V", (lambda par, i: lambda v: v.tensor_copy(
                    out=wfb[par].rearrange("p (a b) -> p a b", a=8)[:, i, :],
                    in_=xisc[:, 0:512]))(par, i),
                    waits=[f"f3_{og}_{i}"], tag=f"fcv{og}_{i}", inc=("V", 1))

            def emit_conv(ii):
                engx = ["G", "Sc"][ii % 2]
                P.step(engx, (lambda par, ii: lambda en: ecopy(en,
                    wob[par][:, 256 * ii:256 * (ii + 1)], wsh[par][:, 256 * ii:256 * (ii + 1)]))(par, ii),
                    waits=[f"dol{og}", wob_user[par]], tag=f"dcv{og}_{ii}", inc=(engx, 1))
                st["cvt"].append(f"dcv{og}_{ii}")

            if ph == 0:
                srco = woutT[:, 256 * og:256 * (og + 1)].rearrange("(c p) m -> p c m", p=128)
                srcf = wffnT[:, 256 * og:256 * (og + 1)].rearrange("(c p) m -> p c m", p=128)
                dsto = wsh[par].rearrange("p (c m) -> p c m", m=256)
                dstf = wsh[1 - par].rearrange("p (c m) -> p c m", m=256)
                P.step("Y", (lambda dsto, srco: lambda y: y.dma_start(out=dsto, in_=srco))(dsto, srco),
                       waits=[wo_user[par]], tag=f"dol{og}", inc=("Y", 16))
                P.step("Y", (lambda dstf, srcf: lambda y: y.dma_start(out=dstf, in_=srcf))(dstf, srcf),
                       waits=[wo_user[1 - par]], tag=f"dfl{og}", inc=("Y", 16))
            elif ph == 1:
                for ii in range(0, 6):
                    emit_conv(ii)
            elif ph == 2:
                for ii in range(6, 11):
                    emit_conv(ii)
                emit_f(0)
            elif ph == 3:
                for ii in range(11, 16):
                    emit_conv(ii)
                joiner(f"dcv{og}", st["cvt"])
                wo_user[par] = f"dcv{og}"
                emit_f(1)
            elif ph in (4, 5):
                emit_f(2 * ph - 6)
                emit_f(2 * ph - 5)
            elif ph == 6:
                emit_f(6)
                emit_f(7)
                joiner(f"fcv{og}", [f"fcv{og}_{i}" for i in range(8)])
                wo_user[1 - par] = f"fcv{og}"

        for ph in range(8):
            d_prep(0, ph)

        for og in range(8):
            par = og % 2
            for tc in range(8):
                xw = [f"res{og}_{tc-2}"] if tc >= 2 else ([f"res{og-1}_{tc+6}"] if og > 0 else [])
                P.step("G", (lambda og, tc: lambda g: g.dma_start(
                    out=xnst[:, 256 * (tc % 2):256 * (tc % 2) + 256],
                    in_=x_n[128 * tc:128 * (tc + 1), 256 * og:256 * (og + 1)]))(og, tc),
                    waits=xw, tag=f"xnl{og}_{tc}", inc=("G", 16))
                bD = tc % 2
                bF = 2 + tc % 2

                def domm(og, tc, bD, par):
                    def f(t):
                        r = None
                        for hh in range(16):
                            r = t.matmul(out=pb[bD][:, 0:256],
                                         lhsT=ctxb[:, T * hh + 128 * tc:T * hh + 128 * (tc + 1)],
                                         rhs=wob[par][:, 256 * hh:256 * (hh + 1)],
                                         start=(hh == 0), stop=(hh == 15))
                        return r
                    return f

                def femm(og, tc, bF, par):
                    def f(t):
                        r = None
                        for cc in range(16):
                            r = t.matmul(out=pb[bF][:, 0:256],
                                         lhsT=xqb[:, T * cc + 128 * tc:T * cc + 128 * (tc + 1)],
                                         rhs=wfb[par][:, 256 * cc:256 * (cc + 1)],
                                         start=(cc == 0), stop=(cc == 15))
                        return r
                    return f
                P.step("Tm", domm(og, tc, bD, par),
                       waits=[f"dcv{og}", pbD_user[tc % 2], "cev15_1"], tag=f"dmm{og}_{tc}", inc=("Tm", 1))
                P.step("Tm", femm(og, tc, bF, par),
                       waits=[f"fcv{og}", pbF_user[tc % 2], "exq15"], tag=f"fmm{og}_{tc}", inc=("Tm", 1))
                e1w = [f"dmm{og}_{tc}", f"xnl{og}_{tc}"]
                if tc >= 2:
                    e1w.append(f"res{og}_{tc-2}")
                elif og > 0:
                    e1w.append(f"res{og-1}_{tc+6}")
                P.step("V", (lambda og, tc, bD: lambda v: v.tensor_tensor(
                    out=o1b[:, 256 * (tc % 2):256 * (tc % 2) + 256], in0=pb[bD][:, 0:256],
                    in1=xnst[:, 256 * (tc % 2):256 * (tc % 2) + 256], op=ALU.add))(og, tc, bD),
                    waits=e1w, tag=f"ev1{og}_{tc}", inc=("V", 1))
                pbD_user[tc % 2] = f"ev1{og}_{tc}"
                rw2 = [f"ev1{og}_{tc}", f"fmm{og}_{tc}", "qgam"]
                if tc >= 2:
                    rw2.append(f"sto{og}_{tc-2}")
                elif og > 0:
                    rw2.append(f"sto{og-1}_{tc+6}")
                P.step("V", (lambda og, tc, bF: lambda v: v.scalar_tensor_tensor(
                    out=outsb[:, 512 * (tc % 2):512 * (tc % 2) + 256], in0=pb[bF][:, 0:256],
                    scalar=gamc, in1=o1b[:, 256 * (tc % 2):256 * (tc % 2) + 256],
                    op0=ALU.mult, op1=ALU.add))(og, tc, bF),
                    waits=rw2, tag=f"res{og}_{tc}", inc=("V", 1))
                pbF_user[tc % 2] = f"res{og}_{tc}"
                P.step("G", (lambda og, tc: lambda g: g.dma_start(
                    out=out_d[128 * tc:128 * (tc + 1), 256 * og:256 * (og + 1)],
                    in_=outsb[:, 512 * (tc % 2):512 * (tc % 2) + 256]))(og, tc),
                    waits=[f"res{og}_{tc}"], tag=f"sto{og}_{tc}", inc=("G", 16))
                d_prep(og + 1, tc)
            wob_user[par] = f"dmm{og}_7"
            wfb_user[par] = f"fmm{og}_7"

        if debug2:
            ALLST = [f"sto{og}_{tc}" for og in range(8) for tc in range(8)]
            P.step("G", lambda g: g.dma_start(out=dbg["ctx"], in_=ctxb), waits=ALLST, tag="d2a", inc=("G", 16))
            P.step("G", lambda g: g.dma_start(out=dbg["xqb"], in_=xqb), waits=["d2a"], tag="d2b", inc=("G", 16))
            def d2scf(v):
                v.tensor_copy(out=outsb[:, 0:1], in_=sc_bc)
                v.tensor_copy(out=outsb[:, 1:2], in_=tmpb)
                v.tensor_copy(out=outsb[:, 2:3], in_=gamc)
                v.tensor_copy(out=outsb[:, 3:4], in_=s127)
                return v.tensor_copy(out=outsb[:, 4:5], in_=gmax)
            P.step("V", d2scf, waits=ALLST, tag="d2sc", inc=("V", 1))
            P.step("G", lambda g: g.dma_start(out=dbg["sc"], in_=outsb[:, 0:16]), waits=["d2sc"], tag="d2c", inc=("G", 16))
            P.step("G", lambda g: g.dma_start(out=dbg["pn"][0], in_=p_n[0]), waits=["d2c"], tag="d2d", inc=("G", 16))
            P.step("G", lambda g: g.dma_start(out=dbg["pn"][1], in_=p_n[1]), waits=["d2d"], tag="d2e", inc=("G", 16))
            P.step("G", lambda g: g.dma_start(out=dbg["z"], in_=smal_t[:, :]), waits=["d2e"], tag="d2f", inc=("G", 16))
            P.step("G", lambda g: g.dma_start(out=dbg["v"], in_=vsb), waits=["d2f"], tag="d2g", inc=("G", 16))
            P.step("G", lambda g: g.dma_start(out=dbg["st"], in_=stat_t[:, :]), waits=["d2g"], tag="d2h", inc=("G", 16))
            P.step("G", lambda g: g.dma_start(out=dbg["bc"][:, 0:1024], in_=mubc2),
                   waits=["d2h"], tag="d2i0", inc=("G", 16))
            P.step("G", lambda g: g.dma_start(out=dbg["bc"][:, 1024:2048], in_=rhobc2),
                   waits=["d2i0"], tag="d2i", inc=("G", 16))
        P.step("G", None, waits=[f"sto{og}_{tc}" for og in range(8) for tc in range(8)]
               + (["d2i"] if debug2 else []))

        for (e_, fn_, waits_, inc_, pad_, dup_) in P.steps:
            for (sn, val) in waits_:
                assert val <= P.cnt[sn], (sn, val, P.cnt[sn])

        @block.sync
        def _(y):
            P.emit("Y", y, sems)

        @block.vector
        def _(v):
            P.emit("V", v, sems, padfn=lambda en: en.memset(junkv, 0.0))

        @block.scalar
        def _(s):
            def scpad(en):
                en.memzero(junks)
                return en.memzero(junks)
            P.emit("Sc", s, sems, padfn=scpad)

        @block.tensor
        def _(t):
            P.emit("Tm", t, sems,
                   padfn=lambda en: en.transpose(out=ptp[0][0:1, 0:128], in_=onb_t[:, :],
                                                 identity=idb_t[:, :]))

        @block.gpsimd
        def _(g):
            P.emit("G", g, sems, padfn=lambda en: en.memset(junkg, 0.0))

    return nc


def _host_prep(inputs):
    x = np.ascontiguousarray(np.asarray(inputs["L0_attn_x"], dtype=np.float32).reshape(NTOK, D))
    wqkvT = np.ascontiguousarray(np.asarray(inputs["L0_attn_w_qkv"], dtype=np.float32).T)
    woutT = np.ascontiguousarray(np.asarray(inputs["L0_attn_w_out"], dtype=np.float32).T)
    wffnT = np.ascontiguousarray(np.asarray(inputs["L0_ffn_weight"], dtype=np.float32).T)
    nw = np.asarray(inputs["L0_attn_norm_w"], dtype=np.float32)
    gf = np.asarray(inputs["L0_ffn_norm_weight"], dtype=np.float32)
    pm = np.asarray(inputs["L0_attn_pm"], dtype=np.float32)

    nw_f = np.ascontiguousarray(nw.reshape(16, 128).T)
    gf_f = np.ascontiguousarray(gf.reshape(16, 128).T)
    pmk_h = np.ascontiguousarray(pm[0].transpose(2, 0, 1).reshape(128, 256)).astype(ml_dtypes.bfloat16)
    pmv_h = np.ascontiguousarray(pm[1].transpose(1, 0, 2).reshape(16, 2048)).astype(ml_dtypes.bfloat16)

    inv = 1.0 / (10000.0 ** (np.arange(0, DH, 2, dtype=np.float64) / DH))
    ptmat = np.zeros((128, 128), dtype=np.float32)
    for i in range(64):
        ptmat[2 * i, 2 * i + 1] = -1.0
        ptmat[2 * i + 1, 2 * i] = 1.0
    ptm_h = np.ascontiguousarray(ptmat.T).astype(ml_dtypes.bfloat16)
    idb_h = np.eye(128, dtype=np.float32).astype(ml_dtypes.bfloat16)
    idf_h = np.eye(128, dtype=np.float32)
    onb_h = np.ones((128, 1), dtype=np.float32).astype(ml_dtypes.bfloat16)
    onf_h = np.ones((1, 128), dtype=np.float32)
    onc_h = np.ones((128, 1), dtype=np.float32)
    r = np.arange(128)
    c = np.arange(128)
    mdiag = np.where(c[None, :] <= r[:, None], 0.0, -1.0e30).astype(np.float32).astype(ml_dtypes.bfloat16)
    mfirst = np.where((c[None, :] <= r[:, None]) | (c[None, :] < 16), 0.0, -1.0e30).astype(np.float32).astype(ml_dtypes.bfloat16)

    in_maps = []
    for core in range(NCORES):
        sh = x[T * core:T * (core + 1)]
        pos = np.arange(T * core, T * (core + 1), dtype=np.float64)
        fr = pos[:, None] * inv[None, :]
        cosx = np.repeat(np.cos(fr), 2, axis=1).T.astype(np.float32)
        sinx = np.repeat(np.sin(fr), 2, axis=1).T.astype(np.float32)
        in_maps.append(dict(
            x_t=np.ascontiguousarray(sh.T), x_n=np.ascontiguousarray(sh),
            wqkvT=wqkvT, woutT=woutT, wffnT=wffnT, nw_f=nw_f, gf_f=gf_f,
            pmk_h=pmk_h, pmv_h=pmv_h,
            cos_h=np.ascontiguousarray(cosx).astype(ml_dtypes.bfloat16),
            sin_h=np.ascontiguousarray(sinx).astype(ml_dtypes.bfloat16),
            ptm_h=ptm_h, idb_h=idb_h, idf_h=idf_h, onb_h=onb_h, onf_h=onf_h,
            onc_h=onc_h, mfirst_h=mfirst, mdiag_h=mdiag,
        ))
    return in_maps


def kernel(**inputs):
    in_maps = _host_prep(inputs)
    nc = build_nc(debug=DEBUG)
    res = run_bass_kernel_spmd(nc, in_maps, core_ids=list(range(NCORES)))
    out = np.concatenate([r["out"] for r in res.results], axis=0)
    return out.reshape(1, NTOK, D)


if __name__ == "__main__":
    import reference
    inp = reference.setup_inputs(0)
    got = kernel(**{k: np.asarray(v) for k, v in inp.items()})
    ref = np.asarray(reference.reference(**inp))
    err = np.mean((got - ref) ** 2) / np.mean(ref ** 2)
    print("resvar:", err, "maxabs:", np.max(np.abs(got - ref)))
